# version 24
# baseline (speedup 1.0000x reference)
import contextlib
import numpy as np
import ml_dtypes
import concourse.bass as bass
import concourse.mybir as mybir
from concourse.bass_utils import run_bass_kernel_spmd

F32 = mybir.dt.float32
BF16 = mybir.dt.bfloat16
AL = mybir.AluOpType
AF = mybir.ActivationFunctionType
AX = mybir.AxisListType

S = 4096
D = 1024
I = 2816
TOK = 2048
NHS = 16
NF = 1.0 / 32.0
EXPB = -40.0


class Plan:

    ENGS = ("sync", "scalar", "vector", "tensor", "gpsimd")

    def __init__(self, nc):
        self.nc = nc
        self.ops = []
        self.cnt = {}
        self.sem = {}
        self.lastw = {e: {} for e in self.ENGS}

    def mksem(self, *names):
        for n in names:
            self.sem[n] = self._es.enter_context(self.nc.semaphore(n))
            self.cnt[n] = 0

    def emit(self, eng, fn):
        self.ops.append((eng, fn))

    def wait(self, eng, sname, val):
        if val <= 0:
            return
        if self.lastw[eng].get(sname, 0) >= val:
            return
        self.lastw[eng][sname] = val
        h = self.sem[sname]
        self.emit(eng, lambda e, h=h, v=val: e.wait_ge(h, v))

    AUTOCHAIN = {"vector": "vch", "scalar": "sch", "gpsimd": "gch"}

    def op(self, eng, build, deps=(), inc=None, inc_by=1):
        deps = list(deps)
        ac = self.AUTOCHAIN.get(eng)
        if ac is not None and self.cnt[ac] > 0:
            deps.append((ac, self.cnt[ac]))
        for (sn, v) in deps:
            self.wait(eng, sn, v)
        sname = inc if inc is not None else {"sync": None, "scalar": "sch",
                                             "vector": "vch", "tensor": "tch",
                                             "gpsimd": "gch"}[eng]
        assert sname is not None
        self.cnt[sname] += inc_by
        h = self.sem[sname]
        k = inc_by

        def fn(e, build=build, h=h, k=k):
            build(e).then_inc(h, k)
        self.emit(eng, fn)
        return self.cnt[sname]

    def dma(self, eng, sname, out, in_, deps=()):
        return self.op(eng, lambda e, o=out, i=in_: e.dma_start(out=o, in_=i),
                       deps=deps, inc=sname, inc_by=16)


def fp4_quant(p, xn_ap, ax_ap, out_ap, tmp, nblk, dep_src, gkey=0, gps=False,
              sa=None):
    F = 16 * nblk
    qt, qu1 = tmp["qt"], tmp["qu1"]
    qbm, qsc, qr = tmp["qbm"], tmp["qsc"], tmp["qr"]
    axg = tmp.setdefault("_axg", {})
    xv = xn_ap.rearrange("p (b w) -> p b w", w=16)
    if sa is None:
        sa = p.op("scalar", lambda e: e.activation(
            out=ax_ap, in_=xn_ap, func=AF.Abs),
            deps=list(dep_src) + [("vch", axg.get(gkey, 0))])
    c = p.op("vector", lambda e: e.tensor_reduce(
        out=qbm[:, 0:nblk], in_=xv, axis=AX.X, op=AL.max,
        apply_absolute_value=True), deps=dep_src)
    c = p.op("vector", lambda e: e.tensor_scalar(
        out=qsc[:, 0:nblk], in0=qbm[:, 0:nblk], scalar1=1.0 / 12.0,
        scalar2=0.5e-12, op0=AL.mult, op1=AL.max), deps=[("vch", c)])
    c = p.op("vector", lambda e: e.reciprocal(out=qr[:, 0:nblk], in_=qsc[:, 0:nblk]),
             deps=[("vch", c)])
    rb = qr[:, 0:nblk].unsqueeze(-1).broadcast_to([128, nblk, 16])
    scb = qsc[:, 0:nblk].unsqueeze(-1).broadcast_to([128, nblk, 16])
    ct = p.op("vector", lambda e: e.tensor_tensor(
        out=qt[:, 0:F].rearrange("p (b w) -> p b w", w=16),
        in0=ax_ap.rearrange("p (b w) -> p b w", w=16),
        in1=rb, op=AL.mult), deps=[("vch", c), ("sch", sa)])
    if gps:
        g1 = p.op("gpsimd", lambda e: e.tensor_scalar(
            out=ax_ap, in0=xn_ap, scalar1=0.0, scalar2=2.0,
            op0=AL.is_gt, op1=AL.mult), deps=list(dep_src) + [("vch", ct)])
        g1 = p.op("gpsimd", lambda e: e.tensor_scalar(
            out=ax_ap, in0=ax_ap, scalar1=-1.0, scalar2=None,
            op0=AL.add, op1=AL.bypass))
        gss = p.op("gpsimd", lambda e: e.tensor_tensor(
            out=ax_ap.rearrange("p (b w) -> p b w", w=16),
            in0=ax_ap.rearrange("p (b w) -> p b w", w=16),
            in1=scb, op=AL.mult))
    c = p.op("vector", lambda e: e.tensor_scalar(
        out=qu1[:, 0:F], in0=qt[:, 0:F], scalar1=10.0, scalar2=2.0,
        op0=AL.is_gt, op1=AL.mult), deps=[("vch", ct)])
    for thr in (5.0, 7.0):
        c = p.op("vector", lambda e, t=thr: e.scalar_tensor_tensor(
            out=qu1[:, 0:F], in0=qt[:, 0:F], scalar=t, in1=qu1[:, 0:F],
            op0=AL.is_gt, op1=AL.add), deps=[("vch", c)])
    c = p.op("vector", lambda e: e.tensor_scalar(
        out=qu1[:, 0:F], in0=qu1[:, 0:F], scalar1=2.0, scalar2=None,
        op0=AL.mult, op1=AL.bypass), deps=[("vch", c)])
    for thr in (0.5, 1.5, 2.5, 3.5):
        c = p.op("vector", lambda e, t=thr: e.scalar_tensor_tensor(
            out=qu1[:, 0:F], in0=qt[:, 0:F], scalar=t, in1=qu1[:, 0:F],
            op0=AL.is_gt, op1=AL.add), deps=[("vch", c)])
    if gps:
        c = p.op("vector", lambda e: e.tensor_tensor(
            out=out_ap, in0=qu1[:, 0:F], in1=ax_ap, op=AL.mult),
            deps=[("vch", c), ("gch", gss)])
        axg[gkey] = c
    else:
        c2 = p.op("vector", lambda e: e.tensor_scalar(
            out=qt[:, 0:F], in0=xn_ap, scalar1=0.0, scalar2=2.0,
            op0=AL.is_gt, op1=AL.mult), deps=[("vch", c)])
        c2 = p.op("vector", lambda e: e.scalar_tensor_tensor(
            out=qt[:, 0:F].rearrange("p (b w) -> p b w", w=16),
            in0=qt[:, 0:F].rearrange("p (b w) -> p b w", w=16),
            scalar=-1.0, in1=scb, op0=AL.add, op1=AL.mult), deps=[("vch", c2)])
        c = p.op("vector", lambda e: e.tensor_tensor(
            out=out_ap, in0=qu1[:, 0:F], in1=qt[:, 0:F], op=AL.mult),
            deps=[("vch", c2)])
        axg[gkey] = ct
    return c


def build_nc():
    nc = bass.Bass("TRN2", target_bir_lowering=False)
    xb_h = nc.dram_tensor("xb", [S, D], F32, kind="ExternalInput").ap()
    wqT_h = nc.dram_tensor("wqT", [D, D], F32, kind="ExternalInput").ap()
    wkT_h = nc.dram_tensor("wkT", [D, D], F32, kind="ExternalInput").ap()
    wvT_h = nc.dram_tensor("wvT", [D, D], F32, kind="ExternalInput").ap()
    pwc_h = nc.dram_tensor("pwc", [17, 128, 512], F32, kind="ExternalInput").ap()
    idf_h = nc.dram_tensor("idf", [128, 128], F32, kind="ExternalInput").ap()
    idb_h = nc.dram_tensor("idb", [128, 128], BF16, kind="ExternalInput").ap()
    y_h = nc.dram_tensor("y", [TOK, D], F32, kind="ExternalOutput").ap()
    sxnT_h = nc.dram_tensor("s_xnT", [128, 8 * S], BF16).ap()
    sxqT_h = nc.dram_tensor("s_xqT", [128, 16 * 1024], BF16).ap()
    satt_h = nc.dram_tensor("s_att", [TOK, D], F32).ap()
    ccin_h = nc.dram_tensor("cc_in", [128, 17 * 512], BF16).ap()
    ccout_h = nc.dram_tensor("cc_out", [8, 128, 17 * 512], BF16).ap()

    es_top = contextlib.ExitStack()
    p = Plan(nc)
    p._es = es_top
    sb = lambda name, shape, dt: es_top.enter_context(nc.sbuf_tensor(name, shape, dt))

    idf = sb("idf_s", [128, 128], F32)
    idb = sb("idb_s", [128, 128], BF16)
    nb40 = sb("nb40", [128, 1], F32)

    p.mksem("tch", "vch", "sch", "gch", "dID",
            "dWa", "dWb", "dWSa", "dWSb",
            "dW3g0", "dW3g1", "dW3g2", "dW3g3", "dW3g4", "dW3g5", "dW3d",
            "dPWa", "dPWb", "dPWc", "dPWSa", "dPWSb",
            "dXa", "dXb", "dXNa", "dXNb", "dXQSa", "dXQSb",
            "dSIa", "dSIb", "dAVa", "dAVb", "dATa", "dATb", "dYa", "dYb")

    def phase_fence(engines=("tensor", "vector", "scalar", "sync")):
        snap = {"tch": p.cnt["tch"], "vch": p.cnt["vch"], "sch": p.cnt["sch"]}
        dsems = ("dXNa", "dXNb", "dXQSa", "dXQSb", "dAVa", "dAVb",
                 "dPWSa", "dPWSb")
        dsnap = {sn: p.cnt[sn] for sn in dsems}
        for e in engines:
            for o, sn in (("tensor", "tch"), ("vector", "vch"), ("scalar", "sch")):
                if o != e:
                    p.wait(e, sn, snap[sn])
            for sn in dsems:
                p.wait(e, sn, dsnap[sn])

    with es_top:
        p.dma("sync", "dID", idf[:], idf_h[:, :])
        p.dma("sync", "dID", idb[:], idb_h[:, :])
        p.op("gpsimd", lambda e: e.memset(nb40[:], EXPB))

        es_pw = contextlib.ExitStack()

        def pwsb(n, sh, dt):
            return es_pw.enter_context(nc.sbuf_tensor(n, sh, dt, side="right"))
        pwb = {
            "win": pwsb("pw_in", [128, 2 * 512], F32),
            "ax": pwsb("pw_ax", [128, 512], F32),
            "qt": pwsb("pw_t", [128, 512], F32),
            "qu1": pwsb("pw_u1", [128, 512], F32),
            
            "qbm": pwsb("pw_bm", [128, 32], F32),
            "qsc": pwsb("pw_sc", [128, 32], F32),
            "qr": pwsb("pw_r", [128, 32], F32),
            "wq": pwsb("pw_q", [128, 2 * 512], BF16),
            "wqT": pwsb("pw_qT", [128, 2 * 512], BF16),
        }
        pwb["qss"] = pwb["ax"]
        ps_pw = es_pw.enter_context(nc.psum_tensor("ps_pw", [128, 2048], BF16))

        pw_global = []
        for kind in ("g", "u"):
            for r in range(22):
                for cch in range(2):
                    pw_global.append((kind, r, cch * 512, 512))
        for r in range(8):
            for cch in range(6):
                c0 = cch * 512
                pw_global.append(("d", r, c0, min(512, I - c0)))
        assert len(pw_global) == 136
        pw_sched = list(range(17))
        pw_state = {"a": 0, "b": 0, "c": 0,
                    "slotdep": [0, 0],
                    "chain": {}, "trans": {}, "loadsem": {},
                    "st_dep": [0, 0], "bank_evac": [0, 0]}

        def pw_stage_a():
            i = pw_state["a"]
            if i >= len(pw_sched):
                return False
            pw_state["a"] = i + 1
            width = 512
            slot = i % 2
            win = pwb["win"][:, slot * 512: slot * 512 + width]
            dep = [("vch", pw_state["slotdep"][slot])]
            dsem = ("dPWa", "dPWb")[slot]
            dcount = p.dma("sync", dsem, win, pwc_h[i], deps=dep)
            nblk = width // 16
            dsrc = [(dsem, dcount)]
            if i >= 2:
                assert i - 2 in pw_state["trans"], (i, pw_state)
                dsrc.append(("tch", pw_state["trans"][i - 2]))
            vc = fp4_quant(p, win, pwb["ax"][:, 0:width],
                           pwb["wq"][:, (i % 2) * 512:(i % 2) * 512 + width],
                           pwb, nblk, dep_src=dsrc)
            pw_state["slotdep"][slot] = vc
            pw_state["chain"][i] = vc
            return True

        def pw_stage_b():
            i = pw_state["b"]
            if i >= pw_state["a"] or i >= len(pw_sched):
                return False
            pw_state["b"] = i + 1
            width = 512
            nsub = 4
            half = i % 2
            bank = (i // 2) % 2
            base = bank * 1024 + (i % 2) * 512
            tc = None
            tdeps = [("vch", pw_state["chain"][i]),
                     ("sch", pw_state["bank_evac"][bank]), ("dID", 32)]
            for k in range(nsub):
                tc = p.op("tensor", lambda e, k=k, base=base, half=half: e.transpose(
                    out=ps_pw[:, base + k * 128: base + (k + 1) * 128],
                    in_=pwb["wq"][:, half * 512 + k * 128: half * 512 + k * 128 + 128],
                    identity=idb[:]), deps=tdeps)
                tdeps = []
            pw_state["trans"][i] = tc
            return True

        def pw_stage_c():
            i = pw_state["c"]
            if i >= pw_state["b"] or i >= len(pw_sched):
                return False
            pw_state["c"] = i + 1
            width = 512
            half = i % 2
            bank = (i // 2) % 2
            base = bank * 1024 + (i % 2) * 512
            pwssem = "dPWSa" if half == 0 else "dPWSb"
            ec = p.op("scalar", lambda e, w=width, base=base, half=half: e.copy(
                out=pwb["wqT"][:, half * 512: half * 512 + w],
                in_=ps_pw[:, base: base + w]),
                deps=[("tch", pw_state["trans"][i]),
                      (pwssem, pw_state["st_dep"][half])])
            pw_state["bank_evac"][bank] = ec
            pw_state["st_dep"][half] = p.cnt[pwssem] + 16
            p.op("scalar", lambda e, i=i, half=half: e.dma_start(
                out=ccin_h[:, i * 512:(i + 1) * 512],
                in_=pwb["wqT"][:, half * 512: half * 512 + 512]),
                deps=[("sch", ec)], inc=pwssem, inc_by=16)
            return True

        PW_LAG = 2

        def pw_step(n=1):
            for _ in range(n):
                if pw_state["b"] - pw_state["c"] >= PW_LAG or \
                        pw_state["b"] >= len(pw_sched):
                    pw_stage_c()
                if pw_state["a"] - pw_state["b"] >= PW_LAG or \
                        pw_state["a"] >= len(pw_sched):
                    pw_stage_b()
                pw_stage_a()

        def pw_drain():
            while pw_state["c"] < len(pw_sched):
                pw_stage_c()
                pw_stage_b()
                pw_stage_a()

        es_attn1 = contextlib.ExitStack()
        K_fm = es_attn1.enter_context(nc.sbuf_tensor("K_fm", [128, 8 * S], BF16))
        Q_fm = es_attn1.enter_context(nc.sbuf_tensor("Q_fm", [128, 8 * TOK], BF16))

        es_p0 = contextlib.ExitStack()
        p0 = lambda n, sh, dt: es_p0.enter_context(nc.sbuf_tensor(n, sh, dt))
        wkbf = p0("wk_bf", [128, 8 * 1024], BF16)
        wqbf = p0("wq_bf", [128, 8 * 1024], BF16)
        xin = p0("xin", [128, 2 * 2048], F32)
        xn1 = p0("xn1", [128, 2 * 2048], F32)
        xnT2 = p0("xnT2", [128, 2 * 2048], BF16)
        stt = p0("stt", [128, 2 * 24], F32)
        mvt = p0("mvt", [128, 2 * 4], F32)
        lns = p0("lns", [128, 2 * 12], F32)
        q_ax = p0("q_ax", [128, 2 * 1024], F32)
        q_t = p0("q_t", [128, 1024], F32)
        q_u1 = p0("q_u1", [128, 1024], F32)
        q_bm = p0("q_bm", [128, 64], F32)
        q_sc = p0("q_sc", [128, 64], F32)
        q_r = p0("q_r", [128, 64], F32)
        xq1 = p0("xq1", [128, 2048], BF16)
        qtmp = {"qt": q_t, "qu1": q_u1,
                "qbm": q_bm, "qsc": q_sc, "qr": q_r}
        ps_tp = [es_p0.enter_context(nc.psum_tensor(f"ps_tp{i}", [128, 512], F32))
                 for i in range(2)]
        ps_k = [es_p0.enter_context(nc.psum_tensor(f"ps_k{i}", [128, 512], F32))
                for i in range(2)]
        ps_xq = es_p0.enter_context(nc.psum_tensor("ps_xq", [128, 1024], BF16))

        wstg = xn1[:, 2048:3072]
        castdep = [0] * 16
        for i6 in range(16):
            w_src, w_dst = (wkT_h, wkbf) if i6 < 8 else (wqT_h, wqbf)
            ec = i6 % 8
            dep = [("vch", castdep[i6 - 1] if i6 >= 1 else 0)]
            dc = p.dma("sync", "dWa", wstg,
                       w_src[ec * 128:(ec + 1) * 128, :], deps=dep)
            castdep[i6] = p.op("vector", lambda e, ec=ec, w_dst=w_dst:
                               e.tensor_copy(
                out=w_dst[:, ec * 1024:(ec + 1) * 1024],
                in_=wstg), deps=[("dWa", dc)])
        wk_ready = castdep[7]
        wqproj_ready = castdep[15]

        xb_r = xb_h.rearrange("(H j p) e -> H p j e", j=2, p=128)
        xin_cons = [0] * NHS
        xin_cons_s = [0] * NHS
        xnT_cons_t = [0] * NHS
        xnT_cons_d = [0] * NHS
        kevac = {}
        xnT_spill_val = [None] * NHS
        norm_by_hs = [0] * NHS
        xn1_cons_t = [0] * NHS
        xn1_cons_v = [0] * NHS
        xn1_cons_s2 = [0] * NHS
        tp_evac = [("sch", 0), ("sch", 0)]
        k_evac_bank = [("sch", 0), ("sch", 0)]
        xq_evac = 0
        xqst_dep = [0, 0]

        def ln_stage(hs):
            slot = hs % 2
            xsem = "dXa" if slot == 0 else "dXb"
            xsem_n = "dXNa" if slot == 0 else "dXNb"
            st0[hs] = {"slot": slot, "xsem": xsem, "xsem_n": xsem_n}
            deps = []
            if hs >= 2:
                deps = [("vch", xin_cons[hs - 2]), ("sch", xin_cons_s[hs - 2])]
            dxc = p.dma("sync", xsem, xin[:, slot * 2048:(slot + 1) * 2048]
                        .rearrange("p (j e) -> p j e", j=2), xb_r[hs], deps=deps)
            L = slot * 12
            ST = slot * 24
            MT = slot * 4
            lndeps = [(xsem, dxc)]
            if hs >= 2:
                lndeps.append(("sch", norm_by_hs[hs - 2]))
            vv = 0
            for j in range(2):
                for half in range(2):
                    vv = p.op("vector", lambda e, j=j, h2=half, slot=slot, ST=ST:
                              e.bn_stats(
                        out=stt[:, ST + j * 12 + h2 * 6: ST + j * 12 + h2 * 6 + 6],
                        in_=xin[:, slot * 2048 + j * 1024 + h2 * 512:
                                slot * 2048 + j * 1024 + h2 * 512 + 512]),
                        deps=lndeps)
                    lndeps = []
            for j in range(2):
                vv = p.op("vector", lambda e, j=j, ST=ST, MT=MT: e.bn_aggr(
                    out=mvt[:, MT + j * 2: MT + j * 2 + 2],
                    in_=stt[:, ST + j * 12: ST + j * 12 + 12]),
                    deps=[("vch", vv)])
            mvv = mvt[:, MT: MT + 4].rearrange("p (t two) -> p t two", two=2)
            var2 = mvv[:, :, 1:2].rearrange("p t one -> p (t one)")
            mean2 = mvv[:, :, 0:1].rearrange("p t one -> p (t one)")
            vv = p.op("vector", lambda e, L=L, var2=var2: e.tensor_scalar(
                out=lns[:, L + 0: L + 2], in0=var2, scalar1=1e-5, scalar2=None,
                op0=AL.add, op1=AL.bypass), deps=[("vch", vv)])
            sv = p.op("scalar", lambda e, L=L: e.activation(
                out=lns[:, L + 2: L + 4], in_=lns[:, L + 0: L + 2], func=AF.Sqrt),
                deps=[("vch", vv)])
            vv = p.op("vector", lambda e, L=L: e.reciprocal(
                out=lns[:, L + 4: L + 6], in_=lns[:, L + 2: L + 4]),
                deps=[("sch", sv)])
            vv = p.op("vector", lambda e, L=L: e.scalar_tensor_tensor(
                out=lns[:, L + 6: L + 8], in0=lns[:, L + 0: L + 2], scalar=1.0,
                in1=lns[:, L + 4: L + 6],
                op0=AL.mult, op1=AL.mult), deps=[("vch", vv)])
            vv = p.op("vector", lambda e, L=L: e.tensor_tensor(
                out=lns[:, L + 6: L + 8], in0=lns[:, L + 6: L + 8],
                in1=lns[:, L + 2: L + 4], op=AL.add),
                deps=[("vch", vv)])
            vv = p.op("vector", lambda e, L=L: e.tensor_scalar(
                out=lns[:, L + 6: L + 8], in0=lns[:, L + 6: L + 8], scalar1=0.5,
                scalar2=None, op0=AL.mult, op1=AL.bypass), deps=[("vch", vv)])
            vv = p.op("vector", lambda e, L=L: e.reciprocal(
                out=lns[:, L + 8: L + 10], in_=lns[:, L + 6: L + 8]),
                deps=[("vch", vv)])
            vv = p.op("vector", lambda e, L=L, mean2=mean2: e.scalar_tensor_tensor(
                out=lns[:, L + 10: L + 12], in0=mean2, scalar=-1.0,
                in1=lns[:, L + 8: L + 10],
                op0=AL.mult, op1=AL.mult), deps=[("vch", vv)])
            xndeps = [("vch", vv), (xsem, dxc)]
            if hs == 1:
                xndeps.append(("vch", wqproj_ready))
            if hs >= 2:
                xndeps += [("tch", xn1_cons_t[hs - 2]), ("vch", xn1_cons_v[hs - 2]),
                           ("sch", xn1_cons_s2[hs - 2])]
            sn = 0
            for j in range(2):
                sn = p.op("scalar", lambda e, j=j, slot=slot, L=L: e.activation(
                    out=xn1[:, slot * 2048 + j * 1024: slot * 2048 + (j + 1) * 1024],
                    in_=xin[:, slot * 2048 + j * 1024: slot * 2048 + (j + 1) * 1024],
                    func=AF.Identity, scale=lns[:, L + 8 + j: L + 9 + j],
                    bias=lns[:, L + 10 + j: L + 11 + j]), deps=xndeps)
            xin_cons_s[hs] = sn
            norm_done = sn
            norm_by_hs[hs] = sn
            st0[hs]["dxc"] = dxc
            st0[hs]["norm"] = sn

        def work_stage(hs):
            nonlocal xq_evac
            slot = st0[hs]["slot"]
            xsem = st0[hs]["xsem"]
            xsem_n = st0[hs]["xsem_n"]
            norm_pre = st0[hs]["norm"]
            sa_pre = [None, None]
            if hs < 8:
                axg0 = qtmp.setdefault("_axg", {})
                for j in range(2):
                    sa_pre[j] = p.op(
                        "scalar", lambda e, j=j, slot=slot: e.activation(
                            out=q_ax[:, j * 1024:(j + 1) * 1024],
                            in_=xn1[:, slot * 2048 + j * 1024:
                                    slot * 2048 + (j + 1) * 1024],
                            func=AF.Abs),
                        deps=[("sch", norm_pre), ("vch", axg0.get(j, 0))])
            dxc = st0[hs]["dxc"]
            norm_done = st0[hs]["norm"]

            tt = 0
            for ecpair in range(4):
                bank = ecpair % 2
                bdeps = [("sch", norm_done), tp_evac[bank],
                         ("dID", 32)]
                if hs >= 2:
                    bdeps.append((xsem_n, xnT_cons_d[hs - 2]))
                    bdeps.append(("tch", xnT_cons_t[hs - 2]))
                for q4 in range(4):
                    ec = ecpair * 2 + q4 // 2
                    j = q4 % 2
                    tt = p.op("tensor", lambda e, b=bank, q4=q4, ec=ec, j=j,
                              slot=slot: e.transpose(
                        out=ps_tp[b][:, q4 * 128:(q4 + 1) * 128],
                        in_=xn1[:, slot * 2048 + j * 1024 + ec * 128:
                                 slot * 2048 + j * 1024 + ec * 128 + 128],
                        identity=idf[:]), deps=bdeps)
                    bdeps = []
                ec0 = ecpair * 2
                tp_evac[bank] = ("sch", p.op(
                    "scalar", lambda e, b=bank, ec0=ec0, slot=slot: e.copy(
                        out=xnT2[:, slot * 2048 + ec0 * 256:
                                 slot * 2048 + ec0 * 256 + 512],
                        in_=ps_tp[b][:]), deps=[("tch", tt)]))
            xnT_ready = tp_evac[0], tp_evac[1]
            xn1_cons_t[hs] = tt
            sxnT_r = sxnT_h.rearrange("p (g f) -> p g f", f=S)
            xnT_cons_d[hs] = p.op(
                "scalar", lambda e, slot=slot, hs=hs: e.dma_start(
                    out=sxnT_r[:, :, hs * 256:(hs + 1) * 256],
                    in_=xnT2[:, slot * 2048:(slot + 1) * 2048]
                    .rearrange("p (g f) -> p g f", f=256)),
                deps=list(xnT_ready), inc=xsem_n, inc_by=16)
            xnT_spill_val[hs] = (xsem_n, xnT_cons_d[hs])
            for dc in range(8):
                bank = dc % 2
                mdeps = [("vch", wk_ready), xnT_ready[0], xnT_ready[1],
                         k_evac_bank[bank]]
                for ec in range(8):
                    tt = p.op("tensor", lambda e, b=bank, dc=dc, ec=ec, slot=slot: e.matmul(
                        out=ps_k[b][:, 0:256],
                        lhsT=wkbf[:, ec * 1024 + dc * 128: ec * 1024 + dc * 128 + 128],
                        rhs=xnT2[:, slot * 2048 + ec * 256: slot * 2048 + ec * 256 + 256],
                        start=(ec == 0), stop=(ec == 7)), deps=mdeps)
                    mdeps = []
                k_evac_bank[bank] = ("sch", p.op(
                    "scalar", lambda e, b=bank, dc=dc, hs=hs: e.copy(
                        out=K_fm[:, dc * S + hs * 256: dc * S + hs * 256 + 256],
                        in_=ps_k[b][:, 0:256]), deps=[("tch", tt)]))
            if hs < 8:
                for dc in range(8):
                    bank = dc % 2
                    mdeps = [("vch", wqproj_ready), k_evac_bank[bank]]
                    for ec in range(8):
                        tt = p.op("tensor", lambda e, b=bank, dc=dc, ec=ec,
                                  slot=slot: e.matmul(
                            out=ps_k[b][:, 0:256],
                            lhsT=wqbf[:, ec * 1024 + dc * 128: ec * 1024 + dc * 128 + 128],
                            rhs=xnT2[:, slot * 2048 + ec * 256: slot * 2048 + ec * 256 + 256],
                            start=(ec == 0), stop=(ec == 7)), deps=mdeps)
                        mdeps = []
                    k_evac_bank[bank] = ("sch", p.op(
                        "scalar", lambda e, b=bank, dc=dc, hs=hs: e.copy(
                            out=Q_fm[:, dc * TOK + hs * 256: dc * TOK + hs * 256 + 256],
                            in_=ps_k[b][:, 0:256]), deps=[("tch", tt)]))
            xnT_cons_t[hs] = tt
            if hs < 8:
                vlast = 0
                for j in range(2):
                    xqssem0 = "dXQSa" if j == 0 else "dXQSb"
                    o0 = j * 1024
                    xn_ap = xn1[:, slot * 2048 + o0: slot * 2048 + o0 + 1024]
                    out_ap = xq1[:, o0: o0 + 1024]
                    vlast = fp4_quant(p, xn_ap,
                                      q_ax[:, j * 1024:(j + 1) * 1024],
                                      out_ap, qtmp, 64,
                                      dep_src=[("sch", norm_done),
                                               (xqssem0, xqst_dep[j])],
                                      gkey=j, sa=sa_pre[j])
                    tq = 0
                    tdeps = [("vch", vlast), ("vch", xq_evac), ("dID", 32)]
                    for ec in range(8):
                        tq = p.op("tensor", lambda e, ec=ec, j=j: e.transpose(
                            out=ps_xq[:, ec * 128:(ec + 1) * 128],
                            in_=xq1[:, j * 1024 + ec * 128: j * 1024 + ec * 128 + 128],
                            identity=idb[:]), deps=tdeps)
                        tdeps = []
                    xqssem = "dXQSa" if j == 0 else "dXQSb"
                    xq_evac = p.op("vector", lambda e, j=j: e.tensor_copy(
                        out=xq1[:, j * 1024:(j + 1) * 1024], in_=ps_xq[:]),
                        deps=[("tch", tq), (xqssem, xqst_dep[j])])
                    tt2 = 2 * hs + j
                    xqst_dep[j] = p.cnt[xqssem] + 16
                    p.op("gpsimd", lambda e, tt2=tt2, j=j: e.dma_start(
                        out=sxqT_h[:, tt2 * 1024:(tt2 + 1) * 1024],
                        in_=xq1[:, j * 1024:(j + 1) * 1024]),
                        deps=[("vch", xq_evac)], inc=xqssem, inc_by=16)
                xn1_cons_v[hs] = vlast
                xn1_cons_s2[hs] = p.cnt["sch"]
                xin_cons[hs] = vlast
            else:
                xn1_cons_v[hs] = p.cnt["vch"]
                xn1_cons_s2[hs] = p.cnt["sch"]
                xin_cons[hs] = p.cnt["vch"]

        st0 = {}
        ln_stage(0)
        for hs in range(NHS):
            if hs + 1 < NHS:
                ln_stage(hs + 1)
            work_stage(hs)

        es_p0.close()

        phase_fence()
        es_attn2 = contextlib.ExitStack()
        V_tm = es_attn2.enter_context(nc.sbuf_tensor("V_tm", [128, 32 * 1032], BF16))
        es_pvq = contextlib.ExitStack()
        wproj = es_pvq.enter_context(nc.sbuf_tensor("wproj", [128, 8 * 1024], BF16))
        wstg2 = es_pvq.enter_context(nc.sbuf_tensor("wstg2", [128, 2 * 1024], F32))
        es_pv = contextlib.ExitStack()
        xstr = es_pv.enter_context(nc.sbuf_tensor("xstr", [128, 2 * 2048], BF16))
        ps_v = [es_pv.enter_context(nc.psum_tensor(f"ps_v{i}", [128, 512], F32))
                for i in range(2)]
        ones_c = p.op("gpsimd", lambda e: e.memset(
            V_tm[:].rearrange("p (a b) -> p a b", b=1032)[:, :, 1024:1032], 1.0),
            deps=[("tch", p.cnt["tch"]), ("vch", p.cnt["vch"]),
                  ("sch", p.cnt["sch"]), ("dXNa", p.cnt["dXNa"]),
                  ("dXNb", p.cnt["dXNb"]), ("dXQSa", p.cnt["dXQSa"]),
                  ("dXQSb", p.cnt["dXQSb"])])
        castdep = [0] * 8
        for ec in range(8):
            sl = ec % 2
            wsem = "dWSa" if sl == 0 else "dWSb"
            dep = [("sch", castdep[ec - 2] if ec >= 2 else 0)]
            dc = p.dma("sync", wsem, wstg2[:, sl * 1024:(sl + 1) * 1024],
                       wvT_h[ec * 128:(ec + 1) * 128, :], deps=dep)
            castdep[ec] = p.op("scalar", lambda e, ec=ec, sl=sl: e.activation(
                out=wproj[:, ec * 1024:(ec + 1) * 1024],
                in_=wstg2[:, sl * 1024:(sl + 1) * 1024], func=AF.Identity),
                deps=[(wsem, dc)])
        sxnT_r = sxnT_h.rearrange("p (g f) -> p g f", f=S)
        v_bank_evac = [0, 0]
        xstr_cons = [0] * NHS
        for hs in range(NHS):
            slot = hs % 2
            ssem = "dSIa" if slot == 0 else "dSIb"
            deps = [xnT_spill_val[hs]]
            if hs >= 2:
                deps.append(("tch", xstr_cons[hs - 2]))
            dcs = p.dma("sync", ssem, xstr[:, slot * 2048:(slot + 1) * 2048]
                        .rearrange("p (g f) -> p g f", f=256),
                        sxnT_r[:, :, hs * 256:(hs + 1) * 256], deps=deps)
            pw_step(1)
            for tc_ in range(2):
                for db in range(2):
                    bank = (tc_ * 2 + db) % 2
                    mdeps = [(ssem, dcs),
                             ("sch", v_bank_evac[bank])]
                    for ec in range(8):
                        mdeps.append(("sch", castdep[ec]))
                        tt = p.op("tensor", lambda e, b=bank, tc_=tc_, db=db, ec=ec,
                                  slot=slot: e.matmul(
                            out=ps_v[b][:],
                            lhsT=xstr[:, slot * 2048 + ec * 256 + tc_ * 128:
                                      slot * 2048 + ec * 256 + tc_ * 128 + 128],
                            rhs=wproj[:, ec * 1024 + db * 512: ec * 1024 + db * 512 + 512],
                            start=(ec == 0), stop=(ec == 7)), deps=mdeps)
                        mdeps = []
                    tglob = 2 * hs + tc_
                    v_bank_evac[bank] = p.op(
                        "scalar", lambda e, b=bank, tglob=tglob, db=db: e.copy(
                            out=V_tm[:, tglob * 1032 + db * 512:
                                     tglob * 1032 + db * 512 + 512],
                            in_=ps_v[b][:]), deps=[("tch", tt)])
            xstr_cons[hs] = tt
        pw_drain()
        cc_done = p.op("gpsimd", lambda e: e.collective_compute(
            "AllGather", AL.bypass,
            replica_groups=[[0, 1, 2, 3, 4, 5, 6, 7]],
            ins=[ccin_h[:, :]], outs=[ccout_h[:, :, :]]),
            deps=[("dPWSa", p.cnt["dPWSa"]), ("dPWSb", p.cnt["dPWSb"])])
        es_pv.close()

        es_pvq.close()

        phase_fence()
        es_p2 = contextlib.ExitStack()
        pT = es_p2.enter_context(nc.sbuf_tensor("pT", [128, 32 * 256], BF16))
        attev = es_p2.enter_context(nc.sbuf_tensor("attev", [128, 2 * 1024], F32))
        arec = es_p2.enter_context(nc.sbuf_tensor("arec", [128, 2], F32))
        ps_l = [es_p2.enter_context(nc.psum_tensor(f"ps_l{i}", [128, 512], F32))
                for i in range(2)]
        ps_a = es_p2.enter_context(nc.psum_tensor("ps_a", [128, 1024], F32))
        ps_d = es_p2.enter_context(nc.psum_tensor("ps_d", [128, 512], F32))
        exp_bank = [0, 0]
        att_store_val = [None] * 16
        pv_done_prev = 0
        att_evac_v = 0
        att_evac_s = 0
        attev_dma = [0, 0]
        kq_ready_t = p.cnt["tch"]
        kq_ready_s = p.cnt["sch"]
        for qb in range(8):
            for kc in range(32):
                bank = kc % 2
                mdeps = [("sch", exp_bank[bank]), ("sch", kq_ready_s),
                         ("gch", 2)]
                if kc == 0:
                    mdeps.append(("tch", pv_done_prev))
                for dc in range(8):
                    tt = p.op("tensor", lambda e, b=bank, kc=kc, dc=dc, qb=qb: e.matmul(
                        out=ps_l[b][:, 0:256],
                        lhsT=K_fm[:, dc * S + kc * 128: dc * S + kc * 128 + 128],
                        rhs=Q_fm[:, dc * TOK + qb * 256: dc * TOK + qb * 256 + 256],
                        start=(dc == 0), stop=(dc == 7)), deps=mdeps)
                    mdeps = []
                exp_bank[bank] = p.op("scalar", lambda e, b=bank, kc=kc: e.activation(
                    out=pT[:, kc * 256:(kc + 1) * 256], in_=ps_l[b][:, 0:256],
                    func=AF.Exp, bias=nb40[:], scale=1.0), deps=[("tch", tt)])
            exps_done = max(exp_bank)
            for qc in range(2):
                mdeps = [("sch", exps_done), ("vch", att_evac_v),
                         ("sch", att_evac_s)]
                for kc in range(32):
                    tt = p.op("tensor", lambda e, kc=kc, qc=qc: e.matmul(
                        out=ps_a[:, 0:512],
                        lhsT=pT[:, kc * 256 + qc * 128: kc * 256 + qc * 128 + 128],
                        rhs=V_tm[:, kc * 1032: kc * 1032 + 512],
                        start=(kc == 0), stop=(kc == 31)), deps=mdeps)
                    mdeps = []
                    tt = p.op("tensor", lambda e, kc=kc, qc=qc: e.matmul(
                        out=ps_a[:, 512:1024],
                        lhsT=pT[:, kc * 256 + qc * 128: kc * 256 + qc * 128 + 128],
                        rhs=V_tm[:, kc * 1032 + 512: kc * 1032 + 1024],
                        start=(kc == 0), stop=(kc == 31)))
                    tt = p.op("tensor", lambda e, kc=kc, qc=qc: e.matmul(
                        out=ps_d[:, 0:8],
                        lhsT=pT[:, kc * 256 + qc * 128: kc * 256 + qc * 128 + 128],
                        rhs=V_tm[:, kc * 1032 + 1024: kc * 1032 + 1032],
                        start=(kc == 0), stop=(kc == 31)))
                slot = qc
                vv = p.op("vector", lambda e, slot=slot: e.tensor_scalar(
                    out=arec[:, slot: slot + 1], in0=ps_d[:, 0:1], scalar1=1e-30,
                    scalar2=None, op0=AL.max, op1=AL.bypass),
                    deps=[("tch", tt), ("sch", att_evac_s)])
                vv = p.op("vector", lambda e, slot=slot: e.reciprocal(
                    out=arec[:, slot: slot + 1], in_=arec[:, slot: slot + 1]),
                    deps=[("vch", vv)])
                vv = p.op("vector", lambda e, slot=slot: e.tensor_scalar(
                    out=arec[:, slot: slot + 1], in0=arec[:, slot: slot + 1],
                    scalar1=NF, scalar2=None, op0=AL.mult, op1=AL.bypass),
                    deps=[("vch", vv)])
                att_evac_v = vv
                avsem = "dAVa" if slot == 0 else "dAVb"
                sdeps = [("vch", vv), (avsem, attev_dma[slot])]
                ss = p.op("scalar", lambda e, slot=slot: e.activation(
                    out=attev[:, slot * 1024:(slot + 1) * 1024], in_=ps_a[:, 0:1024],
                    func=AF.Identity, scale=arec[:, slot: slot + 1]), deps=sdeps)
                att_evac_s = ss
                row = (qb * 2 + qc) * 128
                attev_dma[slot] = p.cnt[avsem] + 16
                p.op("scalar", lambda e, row=row, slot=slot: e.dma_start(
                    out=satt_h[row: row + 128, :],
                    in_=attev[:, slot * 1024:(slot + 1) * 1024]),
                    deps=[("sch", ss)], inc=avsem, inc_by=16)
                att_store_val[qb * 2 + qc] = (avsem, p.cnt[avsem])
            pv_done_prev = p.cnt["tch"]
        es_p2.close()
        es_attn2.close()
        es_attn1.close()
        es_pw.close()

        phase_fence(engines=("tensor", "vector", "scalar", "sync", "gpsimd"))
        es_p3 = contextlib.ExitStack()
        p3 = lambda n, sh, dt: es_p3.enter_context(nc.sbuf_tensor(n, sh, dt))
        wgq = p3("wgq", [128, 8 * I], BF16)
        wuq = p3("wuq", [128, 8 * I], BF16)
        wdq = p3("wdq", [128, 22 * D], BF16)
        xqT_t = p3("xqT_t", [128, 2 * 1024], BF16)
        h_f = p3("h_f", [128, I], F32)
        hq = p3("hq", [128, 2 * I], BF16)
        hqT = p3("hqT", [128, 22 * 128], BF16)
        sgb = p3("sgb", [128, 2 * 512], F32)
        xt = p3("xt", [128, 2 * 1024], F32)
        at = p3("at", [128, 2 * 1024], F32)
        f_ax = p3("f_ax", [128, 2 * 1408], F32)
        f_t = p3("f_t", [128, 1408], F32)
        f_u1 = p3("f_u1", [128, 1408], F32)
        f_bm = p3("f_bm", [128, 88], F32)
        f_sc = p3("f_sc", [128, 88], F32)
        f_r = p3("f_r", [128, 88], F32)
        ftmp = {"qt": f_t, "qu1": f_u1,
                "qbm": f_bm, "qsc": f_sc, "qr": f_r}
        ps_g = [es_p3.enter_context(nc.psum_tensor(f"ps_g{i}", [128, 512], F32))
                for i in range(2)]
        ps_u = [es_p3.enter_context(nc.psum_tensor(f"ps_u{i}", [128, 512], F32))
                for i in range(2)]
        ps_dn = es_p3.enter_context(nc.psum_tensor("ps_dn", [128, 1024], F32))
        ps_ht = [es_p3.enter_context(nc.psum_tensor(f"ps_ht{i}", [128, 1024], BF16))
                 for i in range(2)]

        def grp_of(g):
            kind, r, c0, width = pw_global[g]
            return "dW3g%d" % (r // 4) if kind in ("g", "u") else "dW3d"
        order = sorted(range(136), key=lambda g: (grp_of(g) == "dW3d", grp_of(g)))
        ldeps = [("gch", cc_done)]
        for g in order:
            kind, r, c0, width = pw_global[g]
            rank, k = divmod(g, 17)
            src = ccout_h[rank, :, k * 512: k * 512 + width] \
                .rearrange("p (j f) -> p j f", f=128)
            nsub = (width + 127) // 128
            if kind == "g" or kind == "u":
                wdst = wgq if kind == "g" else wuq
                dst = wdst[:].rearrange("p (q f) -> p q f", f=I)[
                    :, c0 // 128:(c0 + width) // 128, r * 128:(r + 1) * 128]
            else:
                dst = wdq[:].rearrange("p (q f) -> p q f", f=D)[
                    :, c0 // 128: c0 // 128 + nsub, r * 128:(r + 1) * 128]
            p.dma("gpsimd", grp_of(g), dst, src, deps=ldeps)
            ldeps = []
        w3g_total = {sn: p.cnt[sn] for sn in
                     ("dW3g0", "dW3g1", "dW3g2", "dW3g3", "dW3g4", "dW3g5",
                      "dW3d")}

        xq_cons = [0] * 16
        x_cons = [0] * 16
        at_cons = [0, 0]
        g_evac_bank = [0, 0]
        hmul_bank = [0, 0]
        ht_evac = [0, 0]
        dn_evac = 0
        ot_dma = [0, 0]
        hq_guard = [0, 0]
        stA = {}
        iblocks = [(0, 512), (512, 512), (1024, 512), (1536, 512), (2048, 512),
                   (2560, 256)]

        def p3_stage_a_load(tt2):
            slot = tt2 % 2
            ssem = "dSIa" if slot == 0 else "dSIb"
            xsem = "dXa" if slot == 0 else "dXb"
            deps = [("dXQSa", p.cnt["dXQSa"]), ("dXQSb", p.cnt["dXQSb"])] \
                if tt2 < 2 else [("tch", xq_cons[tt2 - 2])]
            dxq = p.dma("sync", ssem, xqT_t[:, slot * 1024:(slot + 1) * 1024],
                        sxqT_h[:, tt2 * 1024:(tt2 + 1) * 1024], deps=deps)
            ysem = "dYa" if slot == 0 else "dYb"
            deps = [] if tt2 < 2 else [(ysem, x_cons[tt2 - 2])]
            dxt = p.dma("sync", xsem, xt[:, slot * 1024:(slot + 1) * 1024],
                        xb_h[tt2 * 128:(tt2 + 1) * 128, :], deps=deps)
            atsem = "dATa" if slot == 0 else "dATb"
            dat = p.dma("sync", atsem, at[:, slot * 1024:(slot + 1) * 1024],
                        satt_h[tt2 * 128:(tt2 + 1) * 128, :],
                        deps=[("vch", at_cons[slot]), att_store_val[tt2]])
            stA[tt2] = dict(dxt=dxt, dat=dat, atsem=atsem, slot=slot,
                            dxq=dxq, ssem=ssem, hmul_ib=[])

        def p3_stage_a_mm(tt2, bsel):
            st = stA[tt2]
            slot = st["slot"]
            ssem, dxq = st["ssem"], st["dxq"]
            for bi, (i0, iw) in enumerate(iblocks):
                if bi not in bsel:
                    continue
                bank = bi % 2
                gsn = "dW3g%d" % (i0 // 512)
                mdeps = [(gsn, w3g_total[gsn]), (ssem, dxq),
                         ("sch", g_evac_bank[bank])]
                for ec in range(8):
                    tg = p.op("tensor", lambda e, b=bank, ec=ec, i0=i0, iw=iw,
                              slot=slot: e.matmul(
                        out=ps_g[b][:, 0:iw],
                        lhsT=xqT_t[:, slot * 1024 + ec * 128: slot * 1024 + ec * 128 + 128],
                        rhs=wgq[:, ec * I + i0: ec * I + i0 + iw],
                        start=(ec == 0), stop=(ec == 7)), deps=mdeps)
                    mdeps = []
                mdeps = [("vch", hmul_bank[bank])]
                for ec in range(8):
                    tu = p.op("tensor", lambda e, b=bank, ec=ec, i0=i0, iw=iw,
                              slot=slot: e.matmul(
                        out=ps_u[b][:, 0:iw],
                        lhsT=xqT_t[:, slot * 1024 + ec * 128: slot * 1024 + ec * 128 + 128],
                        rhs=wuq[:, ec * I + i0: ec * I + i0 + iw],
                        start=(ec == 0), stop=(ec == 7)), deps=mdeps)
                    mdeps = []
                g_evac_bank[bank] = p.op(
                    "scalar", lambda e, b=bank, iw=iw, sg0=bank * 512: e.activation(
                        out=sgb[:, sg0: sg0 + iw], in_=ps_g[b][:, 0:iw],
                        func=AF.Silu),
                    deps=[("tch", tg), ("vch", hmul_bank[bank])])
                hmul_bank[bank] = p.op(
                    "vector", lambda e, b=bank, i0=i0, iw=iw, sg0=bank * 512:
                    e.tensor_tensor(
                        out=h_f[:, i0: i0 + iw], in0=sgb[:, sg0: sg0 + iw],
                        in1=ps_u[b][:, 0:iw],
                        op=AL.mult), deps=[("sch", g_evac_bank[bank]), ("tch", tu)])
                st["hmul_ib"].append(hmul_bank[bank])

        def p3_stage_a_q(tt2, ch):
            st = stA[tt2]
            slot = st["slot"]
            hmul_dep = st["hmul_ib"][2] if ch == 0 else st["hmul_ib"][5]
            hap = h_f[:, ch * 1408: (ch + 1) * 1408]
            dsrc = [("vch", hmul_dep), ("tch", hq_guard[slot])]
            vq = fp4_quant(p, hap, f_ax[:, ch * 1408:(ch + 1) * 1408],
                           hq[:, slot * I + ch * 1408: slot * I + (ch + 1) * 1408],
                           ftmp, 88, dep_src=dsrc, gkey=ch)
            st["vq"] = vq

        def p3_stage_b_mm(tt2):
            st = stA[tt2]
            slot = st["slot"]
            tq = 0
            groups = [(0, 8), (8, 8), (16, 6)]
            for gi, (ic0, n) in enumerate(groups):
                bank = gi % 2
                tdeps = [("vch", st["vq"]), ("sch", ht_evac[bank])]
                for k in range(n):
                    ic = ic0 + k
                    tq = p.op("tensor", lambda e, b=bank, k=k, ic=ic, slot=slot:
                              e.transpose(
                        out=ps_ht[b][:, k * 128:(k + 1) * 128],
                        in_=hq[:, slot * I + ic * 128: slot * I + (ic + 1) * 128],
                        identity=idb[:]), deps=tdeps)
                    tdeps = []
                ht_evac[bank] = p.op("scalar", lambda e, b=bank, ic0=ic0, n=n: e.copy(
                    out=hqT[:, ic0 * 128: (ic0 + n) * 128],
                    in_=ps_ht[b][:, 0: n * 128]), deps=[("tch", tq)])
            hq_guard[slot] = tq
            hT_ready = max(ht_evac)
            mdeps = [("sch", hT_ready), ("dW3d", w3g_total["dW3d"]),
                     ("vch", dn_evac)]
            for ic in range(22):
                td = p.op("tensor", lambda e, ic=ic: e.matmul(
                    out=ps_dn[:, 0:512],
                    lhsT=hqT[:, ic * 128:(ic + 1) * 128],
                    rhs=wdq[:, ic * D: ic * D + 512],
                    start=(ic == 0), stop=(ic == 21)), deps=mdeps)
                mdeps = []
                td = p.op("tensor", lambda e, ic=ic: e.matmul(
                    out=ps_dn[:, 512:1024],
                    lhsT=hqT[:, ic * 128:(ic + 1) * 128],
                    rhs=wdq[:, ic * D + 512: ic * D + 1024],
                    start=(ic == 0), stop=(ic == 21)))
            st["td"] = td

        def p3_stage_b_epi(tt2):
            nonlocal dn_evac
            st = stA.pop(tt2)
            slot = st["slot"]
            td = st["td"]
            xsem = "dXa" if slot == 0 else "dXb"
            ysem = "dYa" if slot == 0 else "dYb"
            vv = 0
            gtt = 0
            for half in range(2):
                c0 = half * 512
                vv = p.op("vector", lambda e, c0=c0, slot=slot: e.scalar_tensor_tensor(
                    out=xt[:, slot * 1024 + c0: slot * 1024 + c0 + 512],
                    in0=ps_dn[:, c0: c0 + 512], scalar=1.0,
                    in1=xt[:, slot * 1024 + c0: slot * 1024 + c0 + 512],
                    op0=AL.mult, op1=AL.add),
                    deps=[("tch", td), (xsem, st["dxt"]), (ysem, ot_dma[slot])])
                vv = p.op("vector", lambda e, c0=c0, slot=slot: e.tensor_tensor(
                    out=xt[:, slot * 1024 + c0: slot * 1024 + c0 + 512],
                    in0=xt[:, slot * 1024 + c0: slot * 1024 + c0 + 512],
                    in1=at[:, slot * 1024 + c0: slot * 1024 + c0 + 512], op=AL.add),
                    deps=[("vch", vv), (st["atsem"], st["dat"])])
            dn_evac = vv
            at_cons[slot] = vv
            xq_cons[tt2] = td
            ot_dma[slot] = p.cnt[ysem] + 16
            x_cons[tt2] = p.cnt[ysem] + 16
            p.op("scalar", lambda e, tt2=tt2, slot=slot: e.dma_start(
                out=y_h[tt2 * 128:(tt2 + 1) * 128, :],
                in_=xt[:, slot * 1024:(slot + 1) * 1024]),
                deps=[("vch", vv)], inc=ysem, inc_by=16)

        for tt2 in range(16):
            p3_stage_a_load(tt2)
            p3_stage_a_mm(tt2, (0, 1, 2))
            p3_stage_a_q(tt2, 0)
            p3_stage_a_mm(tt2, (3, 4, 5))
            if tt2 >= 1:
                p3_stage_b_mm(tt2 - 1)
            p3_stage_a_q(tt2, 1)
            if tt2 >= 1:
                p3_stage_b_epi(tt2 - 1)
        p3_stage_b_mm(15)
        p3_stage_b_epi(15)
        es_p3.close()

        p.wait("sync", "dYa", p.cnt["dYa"])
        p.wait("sync", "dYb", p.cnt["dYb"])

        with nc.Block() as block:
            streams = {e: [fn for (eng, fn) in p.ops if eng == e]
                       for e in Plan.ENGS}

            @block.sync
            def _(e):
                for fn in streams["sync"]:
                    fn(e)

            @block.scalar
            def _(e):
                for fn in streams["scalar"]:
                    fn(e)

            @block.vector
            def _(e):
                for fn in streams["vector"]:
                    fn(e)

            @block.tensor
            def _(e):
                for fn in streams["tensor"]:
                    fn(e)

            @block.gpsimd
            def _(e):
                for fn in streams["gpsimd"]:
                    fn(e)

    return nc


_NC_CACHE = None


def kernel(L0_attn_x, L0_attn_Wq, L0_attn_Wk, L0_attn_Wv,
           L0_ffn_w_gate, L0_ffn_w_up, L0_ffn_w_down):
    global _NC_CACHE
    x = np.ascontiguousarray(np.asarray(L0_attn_x, dtype=np.float32))
    wqT = np.ascontiguousarray(np.asarray(L0_attn_Wq, np.float32).T)
    wkT = np.ascontiguousarray(np.asarray(L0_attn_Wk, np.float32).T)
    wvT = np.ascontiguousarray(np.asarray(L0_attn_Wv, np.float32).T)
    wg = np.ascontiguousarray(np.asarray(L0_ffn_w_gate, np.float32))
    wu = np.ascontiguousarray(np.asarray(L0_ffn_w_up, np.float32))
    wd = np.ascontiguousarray(np.asarray(L0_ffn_w_down, np.float32))
    pw_global = []
    for kind in ("g", "u"):
        for r in range(22):
            for cch in range(2):
                pw_global.append((kind, r, cch * 512, 512))
    for r in range(8):
        for cch in range(6):
            c0 = cch * 512
            pw_global.append(("d", r, c0, min(512, I - c0)))
    pwc_all = np.zeros((136, 128, 512), dtype=np.float32)
    for g, (kind, r, c0, width) in enumerate(pw_global):
        src = {"g": wg, "u": wu, "d": wd}[kind]
        pwc_all[g, :, :width] = src[r * 128:(r + 1) * 128, c0:c0 + width]
    idf = np.eye(128, dtype=np.float32)
    idb = np.eye(128, dtype=np.float32).astype(ml_dtypes.bfloat16)
    if _NC_CACHE is None:
        _NC_CACHE = build_nc()
    nc = _NC_CACHE
    in_maps = []
    for c in range(8):
        b, h = c // 2, c % 2
        own = x[b, h * TOK:(h + 1) * TOK]
        oth = x[b, (1 - h) * TOK:(2 - h) * TOK]
        xbc = np.ascontiguousarray(np.concatenate([own, oth], axis=0))
        in_maps.append({"xb": xbc, "wqT": wqT, "wkT": wkT, "wvT": wvT,
                        "pwc": np.ascontiguousarray(pwc_all[17 * c: 17 * (c + 1)]),
                        "idf": idf, "idb": idb})
    res = run_bass_kernel_spmd(nc, in_maps, core_ids=list(range(8)))
    y = np.empty((4, S, D), dtype=np.float32)
    for c in range(8):
        b, h = c // 2, c % 2
        y[b, h * TOK:(h + 1) * TOK] = res.results[c]["y"]
    return y


# version 25
# speedup vs baseline: 1.0244x; 1.0244x over previous
import contextlib
import numpy as np
import ml_dtypes
import concourse.bass as bass
import concourse.mybir as mybir
from concourse.bass_utils import run_bass_kernel_spmd

F32 = mybir.dt.float32
BF16 = mybir.dt.bfloat16
AL = mybir.AluOpType
AF = mybir.ActivationFunctionType
AX = mybir.AxisListType

S = 4096
D = 1024
I = 2816
TOK = 2048
NHS = 16
NF = 1.0 / 32.0
EXPB = -40.0


class Plan:

    ENGS = ("sync", "scalar", "vector", "tensor", "gpsimd")

    def __init__(self, nc):
        self.nc = nc
        self.ops = []
        self.cnt = {}
        self.sem = {}
        self.lastw = {e: {} for e in self.ENGS}

    def mksem(self, *names):
        for n in names:
            self.sem[n] = self._es.enter_context(self.nc.semaphore(n))
            self.cnt[n] = 0

    def emit(self, eng, fn):
        self.ops.append((eng, fn))

    def wait(self, eng, sname, val):
        if val <= 0:
            return
        if self.lastw[eng].get(sname, 0) >= val:
            return
        self.lastw[eng][sname] = val
        h = self.sem[sname]
        self.emit(eng, lambda e, h=h, v=val: e.wait_ge(h, v))

    AUTOCHAIN = {"vector": "vch", "scalar": "sch", "gpsimd": "gch"}

    def op(self, eng, build, deps=(), inc=None, inc_by=1):
        deps = list(deps)
        ac = self.AUTOCHAIN.get(eng)
        if ac is not None and self.cnt[ac] > 0:
            deps.append((ac, self.cnt[ac]))
        for (sn, v) in deps:
            self.wait(eng, sn, v)
        sname = inc if inc is not None else {"sync": None, "scalar": "sch",
                                             "vector": "vch", "tensor": "tch",
                                             "gpsimd": "gch"}[eng]
        assert sname is not None
        self.cnt[sname] += inc_by
        h = self.sem[sname]
        k = inc_by

        def fn(e, build=build, h=h, k=k):
            build(e).then_inc(h, k)
        self.emit(eng, fn)
        return self.cnt[sname]

    def dma(self, eng, sname, out, in_, deps=()):
        return self.op(eng, lambda e, o=out, i=in_: e.dma_start(out=o, in_=i),
                       deps=deps, inc=sname, inc_by=16)


def fp4_quant(p, xn_ap, ax_ap, out_ap, tmp, nblk, dep_src, gkey=0, gps=False,
              sa=None):
    F = 16 * nblk
    qt, qu1 = tmp["qt"], tmp["qu1"]
    qbm, qsc, qr = tmp["qbm"], tmp["qsc"], tmp["qr"]
    axg = tmp.setdefault("_axg", {})
    xv = xn_ap.rearrange("p (b w) -> p b w", w=16)
    if sa is None:
        sa = p.op("scalar", lambda e: e.activation(
            out=ax_ap, in_=xn_ap, func=AF.Abs),
            deps=list(dep_src) + [("vch", axg.get(gkey, 0))])
    c = p.op("vector", lambda e: e.tensor_reduce(
        out=qbm[:, 0:nblk], in_=xv, axis=AX.X, op=AL.max,
        apply_absolute_value=True), deps=dep_src)
    c = p.op("vector", lambda e: e.tensor_scalar(
        out=qsc[:, 0:nblk], in0=qbm[:, 0:nblk], scalar1=1.0 / 12.0,
        scalar2=0.5e-12, op0=AL.mult, op1=AL.max), deps=[("vch", c)])
    c = p.op("vector", lambda e: e.reciprocal(out=qr[:, 0:nblk], in_=qsc[:, 0:nblk]),
             deps=[("vch", c)])
    rb = qr[:, 0:nblk].unsqueeze(-1).broadcast_to([128, nblk, 16])
    scb = qsc[:, 0:nblk].unsqueeze(-1).broadcast_to([128, nblk, 16])
    ct = p.op("vector", lambda e: e.tensor_tensor(
        out=qt[:, 0:F].rearrange("p (b w) -> p b w", w=16),
        in0=ax_ap.rearrange("p (b w) -> p b w", w=16),
        in1=rb, op=AL.mult), deps=[("vch", c), ("sch", sa)])
    if gps:
        g1 = p.op("gpsimd", lambda e: e.tensor_scalar(
            out=ax_ap, in0=xn_ap, scalar1=0.0, scalar2=2.0,
            op0=AL.is_gt, op1=AL.mult), deps=list(dep_src) + [("vch", ct)])
        g1 = p.op("gpsimd", lambda e: e.tensor_scalar(
            out=ax_ap, in0=ax_ap, scalar1=-1.0, scalar2=None,
            op0=AL.add, op1=AL.bypass))
        gss = p.op("gpsimd", lambda e: e.tensor_tensor(
            out=ax_ap.rearrange("p (b w) -> p b w", w=16),
            in0=ax_ap.rearrange("p (b w) -> p b w", w=16),
            in1=scb, op=AL.mult))
    c = p.op("vector", lambda e: e.tensor_scalar(
        out=qu1[:, 0:F], in0=qt[:, 0:F], scalar1=10.0, scalar2=2.0,
        op0=AL.is_gt, op1=AL.mult), deps=[("vch", ct)])
    for thr in (5.0, 7.0):
        c = p.op("vector", lambda e, t=thr: e.scalar_tensor_tensor(
            out=qu1[:, 0:F], in0=qt[:, 0:F], scalar=t, in1=qu1[:, 0:F],
            op0=AL.is_gt, op1=AL.add), deps=[("vch", c)])
    c = p.op("vector", lambda e: e.tensor_scalar(
        out=qu1[:, 0:F], in0=qu1[:, 0:F], scalar1=2.0, scalar2=None,
        op0=AL.mult, op1=AL.bypass), deps=[("vch", c)])
    for thr in (0.5, 1.5, 2.5, 3.5):
        c = p.op("vector", lambda e, t=thr: e.scalar_tensor_tensor(
            out=qu1[:, 0:F], in0=qt[:, 0:F], scalar=t, in1=qu1[:, 0:F],
            op0=AL.is_gt, op1=AL.add), deps=[("vch", c)])
    if gps:
        c = p.op("vector", lambda e: e.tensor_tensor(
            out=out_ap, in0=qu1[:, 0:F], in1=ax_ap, op=AL.mult),
            deps=[("vch", c), ("gch", gss)])
        axg[gkey] = c
    else:
        c2 = p.op("vector", lambda e: e.tensor_scalar(
            out=qt[:, 0:F], in0=xn_ap, scalar1=0.0, scalar2=2.0,
            op0=AL.is_gt, op1=AL.mult), deps=[("vch", c)])
        c2 = p.op("vector", lambda e: e.scalar_tensor_tensor(
            out=qt[:, 0:F].rearrange("p (b w) -> p b w", w=16),
            in0=qt[:, 0:F].rearrange("p (b w) -> p b w", w=16),
            scalar=-1.0, in1=scb, op0=AL.add, op1=AL.mult), deps=[("vch", c2)])
        c = p.op("vector", lambda e: e.tensor_tensor(
            out=out_ap, in0=qu1[:, 0:F], in1=qt[:, 0:F], op=AL.mult),
            deps=[("vch", c2)])
        axg[gkey] = ct
    return c


def build_nc():
    nc = bass.Bass("TRN2", target_bir_lowering=False)
    xb_h = nc.dram_tensor("xb", [S, D], F32, kind="ExternalInput").ap()
    wqT_h = nc.dram_tensor("wqT", [D, D], F32, kind="ExternalInput").ap()
    wkT_h = nc.dram_tensor("wkT", [D, D], F32, kind="ExternalInput").ap()
    wvT_h = nc.dram_tensor("wvT", [D, D], F32, kind="ExternalInput").ap()
    pwc_h = nc.dram_tensor("pwc", [17, 128, 512], F32, kind="ExternalInput").ap()
    idf_h = nc.dram_tensor("idf", [128, 128], F32, kind="ExternalInput").ap()
    idb_h = nc.dram_tensor("idb", [128, 128], BF16, kind="ExternalInput").ap()
    y_h = nc.dram_tensor("y", [TOK, D], F32, kind="ExternalOutput").ap()
    sxnT_h = nc.dram_tensor("s_xnT", [128, 8 * S], BF16).ap()
    sxqT_h = nc.dram_tensor("s_xqT", [128, 16 * 1024], BF16).ap()
    satt_h = nc.dram_tensor("s_att", [TOK, D], F32).ap()
    ccin_h = nc.dram_tensor("cc_in", [128, 17 * 512], BF16).ap()
    ccout_h = nc.dram_tensor("cc_out", [8, 128, 17 * 512], BF16).ap()

    es_top = contextlib.ExitStack()
    p = Plan(nc)
    p._es = es_top
    sb = lambda name, shape, dt: es_top.enter_context(nc.sbuf_tensor(name, shape, dt))

    idf = sb("idf_s", [128, 128], F32)
    idb = sb("idb_s", [128, 128], BF16)
    nb40 = sb("nb40", [128, 1], F32)

    p.mksem("tch", "vch", "sch", "gch", "dID",
            "dWa", "dWb", "dWSa", "dWSb",
            "dW3g0", "dW3g1", "dW3g2", "dW3g3", "dW3g4", "dW3g5", "dW3d",
            "dPWa", "dPWb", "dPWc", "dPWSa", "dPWSb",
            "dXa", "dXb", "dXNa", "dXNb", "dXQSa", "dXQSb",
            "dSIa", "dSIb", "dAVa", "dAVb", "dATa", "dATb", "dYa", "dYb")

    def phase_fence(engines=("tensor", "vector", "scalar", "sync")):
        snap = {"tch": p.cnt["tch"], "vch": p.cnt["vch"], "sch": p.cnt["sch"]}
        dsems = ("dXNa", "dXNb", "dXQSa", "dXQSb", "dAVa", "dAVb",
                 "dPWSa", "dPWSb")
        dsnap = {sn: p.cnt[sn] for sn in dsems}
        for e in engines:
            for o, sn in (("tensor", "tch"), ("vector", "vch"), ("scalar", "sch")):
                if o != e:
                    p.wait(e, sn, snap[sn])
            for sn in dsems:
                p.wait(e, sn, dsnap[sn])

    with es_top:
        p.dma("sync", "dID", idf[:], idf_h[:, :])
        p.dma("sync", "dID", idb[:], idb_h[:, :])
        p.op("gpsimd", lambda e: e.memset(nb40[:], EXPB))

        es_pw = contextlib.ExitStack()

        def pwsb(n, sh, dt):
            return es_pw.enter_context(nc.sbuf_tensor(n, sh, dt, side="right"))
        pwb = {
            "win": pwsb("pw_in", [128, 2 * 512], F32),
            "ax": pwsb("pw_ax", [128, 512], F32),
            "qt": pwsb("pw_t", [128, 512], F32),
            "qu1": pwsb("pw_u1", [128, 512], F32),
            
            "qbm": pwsb("pw_bm", [128, 32], F32),
            "qsc": pwsb("pw_sc", [128, 32], F32),
            "qr": pwsb("pw_r", [128, 32], F32),
            "wq": pwsb("pw_q", [128, 2 * 512], BF16),
            "wqT": pwsb("pw_qT", [128, 2 * 512], BF16),
        }
        pwb["qss"] = pwb["ax"]
        ps_pw = es_pw.enter_context(nc.psum_tensor("ps_pw", [128, 2048], BF16))

        pw_global = []
        for kind in ("g", "u"):
            for r in range(22):
                for cch in range(2):
                    pw_global.append((kind, r, cch * 512, 512))
        for r in range(8):
            for cch in range(6):
                c0 = cch * 512
                pw_global.append(("d", r, c0, min(512, I - c0)))
        assert len(pw_global) == 136
        pw_sched = list(range(17))
        pw_state = {"a": 0, "b": 0, "c": 0,
                    "slotdep": [0, 0],
                    "chain": {}, "trans": {}, "loadsem": {},
                    "st_dep": [0, 0], "bank_evac": [0, 0]}

        def pw_stage_a():
            i = pw_state["a"]
            if i >= len(pw_sched):
                return False
            pw_state["a"] = i + 1
            width = 512
            slot = i % 2
            win = pwb["win"][:, slot * 512: slot * 512 + width]
            dep = [("vch", pw_state["slotdep"][slot])]
            dsem = ("dPWa", "dPWb")[slot]
            dcount = p.dma("sync", dsem, win, pwc_h[i], deps=dep)
            nblk = width // 16
            dsrc = [(dsem, dcount)]
            if i >= 2:
                assert i - 2 in pw_state["trans"], (i, pw_state)
                dsrc.append(("tch", pw_state["trans"][i - 2]))
            vc = fp4_quant(p, win, pwb["ax"][:, 0:width],
                           pwb["wq"][:, (i % 2) * 512:(i % 2) * 512 + width],
                           pwb, nblk, dep_src=dsrc)
            pw_state["slotdep"][slot] = vc
            pw_state["chain"][i] = vc
            return True

        def pw_stage_b():
            i = pw_state["b"]
            if i >= pw_state["a"] or i >= len(pw_sched):
                return False
            pw_state["b"] = i + 1
            width = 512
            nsub = 4
            half = i % 2
            bank = (i // 2) % 2
            base = bank * 1024 + (i % 2) * 512
            tc = None
            tdeps = [("vch", pw_state["chain"][i]),
                     ("sch", pw_state["bank_evac"][bank]), ("dID", 32)]
            for k in range(nsub):
                tc = p.op("tensor", lambda e, k=k, base=base, half=half: e.transpose(
                    out=ps_pw[:, base + k * 128: base + (k + 1) * 128],
                    in_=pwb["wq"][:, half * 512 + k * 128: half * 512 + k * 128 + 128],
                    identity=idb[:]), deps=tdeps)
                tdeps = []
            pw_state["trans"][i] = tc
            return True

        def pw_stage_c():
            i = pw_state["c"]
            if i >= pw_state["b"] or i >= len(pw_sched):
                return False
            pw_state["c"] = i + 1
            width = 512
            half = i % 2
            bank = (i // 2) % 2
            base = bank * 1024 + (i % 2) * 512
            pwssem = "dPWSa" if half == 0 else "dPWSb"
            ec = p.op("scalar", lambda e, w=width, base=base, half=half: e.copy(
                out=pwb["wqT"][:, half * 512: half * 512 + w],
                in_=ps_pw[:, base: base + w]),
                deps=[("tch", pw_state["trans"][i]),
                      (pwssem, pw_state["st_dep"][half])])
            pw_state["bank_evac"][bank] = ec
            pw_state["st_dep"][half] = p.cnt[pwssem] + 16
            p.op("scalar", lambda e, i=i, half=half: e.dma_start(
                out=ccin_h[:, i * 512:(i + 1) * 512],
                in_=pwb["wqT"][:, half * 512: half * 512 + 512]),
                deps=[("sch", ec)], inc=pwssem, inc_by=16)
            return True

        PW_LAG = 2

        def pw_step(n=1):
            for _ in range(n):
                if pw_state["b"] - pw_state["c"] >= PW_LAG or \
                        pw_state["b"] >= len(pw_sched):
                    pw_stage_c()
                if pw_state["a"] - pw_state["b"] >= PW_LAG or \
                        pw_state["a"] >= len(pw_sched):
                    pw_stage_b()
                pw_stage_a()

        def pw_drain():
            while pw_state["c"] < len(pw_sched):
                pw_stage_c()
                pw_stage_b()
                pw_stage_a()

        es_attn1 = contextlib.ExitStack()
        K_fm = es_attn1.enter_context(nc.sbuf_tensor("K_fm", [128, 8 * S], BF16))
        Q_fm = es_attn1.enter_context(nc.sbuf_tensor("Q_fm", [128, 8 * TOK], BF16))

        es_p0 = contextlib.ExitStack()
        p0 = lambda n, sh, dt: es_p0.enter_context(nc.sbuf_tensor(n, sh, dt))
        wkbf = p0("wk_bf", [128, 8 * 1024], BF16)
        wqbf = p0("wq_bf", [128, 8 * 1024], BF16)
        xin = p0("xin", [128, 2 * 2048], F32)
        xn1 = p0("xn1", [128, 2 * 2048], F32)
        xnT2 = p0("xnT2", [128, 2 * 2048], BF16)
        stt = p0("stt", [128, 2 * 24], F32)
        mvt = p0("mvt", [128, 2 * 4], F32)
        lns = p0("lns", [128, 2 * 12], F32)
        q_ax = p0("q_ax", [128, 2 * 1024], F32)
        q_t = p0("q_t", [128, 1024], F32)
        q_u1 = p0("q_u1", [128, 1024], F32)
        q_bm = p0("q_bm", [128, 64], F32)
        q_sc = p0("q_sc", [128, 64], F32)
        q_r = p0("q_r", [128, 64], F32)
        xq1 = p0("xq1", [128, 2048], BF16)
        qtmp = {"qt": q_t, "qu1": q_u1,
                "qbm": q_bm, "qsc": q_sc, "qr": q_r}
        ps_tp = [es_p0.enter_context(nc.psum_tensor(f"ps_tp{i}", [128, 512], F32))
                 for i in range(2)]
        ps_k = [es_p0.enter_context(nc.psum_tensor(f"ps_k{i}", [128, 512], F32))
                for i in range(2)]
        ps_xq = es_p0.enter_context(nc.psum_tensor("ps_xq", [128, 1024], BF16))

        wstg = xn1[:, 2048:3072]
        castdep = [0] * 16
        for i6 in range(16):
            w_src, w_dst = (wkT_h, wkbf) if i6 < 8 else (wqT_h, wqbf)
            ec = i6 % 8
            dep = [("vch", castdep[i6 - 1] if i6 >= 1 else 0)]
            dc = p.dma("sync", "dWa", wstg,
                       w_src[ec * 128:(ec + 1) * 128, :], deps=dep)
            castdep[i6] = p.op("vector", lambda e, ec=ec, w_dst=w_dst:
                               e.tensor_copy(
                out=w_dst[:, ec * 1024:(ec + 1) * 1024],
                in_=wstg), deps=[("dWa", dc)])
        wk_ready = castdep[7]
        wqproj_ready = castdep[15]

        xb_r = xb_h.rearrange("(H j p) e -> H p j e", j=2, p=128)
        xin_cons = [0] * NHS
        xin_cons_s = [0] * NHS
        xnT_cons_t = [0] * NHS
        xnT_cons_d = [0] * NHS
        kevac = {}
        xnT_spill_val = [None] * NHS
        norm_by_hs = [0] * NHS
        xn1_cons_t = [0] * NHS
        xn1_cons_v = [0] * NHS
        xn1_cons_s2 = [0] * NHS
        tp_evac = [("sch", 0), ("sch", 0)]
        k_evac_bank = [("sch", 0), ("sch", 0)]
        xq_evac = 0
        xqst_dep = [0, 0]

        def ln_stage(hs):
            slot = hs % 2
            xsem = "dXa" if slot == 0 else "dXb"
            xsem_n = "dXNa" if slot == 0 else "dXNb"
            st0[hs] = {"slot": slot, "xsem": xsem, "xsem_n": xsem_n}
            deps = []
            if hs >= 2:
                deps = [("vch", xin_cons[hs - 2]), ("sch", xin_cons_s[hs - 2])]
            dxc = p.dma("sync", xsem, xin[:, slot * 2048:(slot + 1) * 2048]
                        .rearrange("p (j e) -> p j e", j=2), xb_r[hs], deps=deps)
            L = slot * 12
            ST = slot * 24
            MT = slot * 4
            lndeps = [(xsem, dxc)]
            if hs >= 2:
                lndeps.append(("sch", norm_by_hs[hs - 2]))
            vv = 0
            for j in range(2):
                for half in range(2):
                    vv = p.op("vector", lambda e, j=j, h2=half, slot=slot, ST=ST:
                              e.bn_stats(
                        out=stt[:, ST + j * 12 + h2 * 6: ST + j * 12 + h2 * 6 + 6],
                        in_=xin[:, slot * 2048 + j * 1024 + h2 * 512:
                                slot * 2048 + j * 1024 + h2 * 512 + 512]),
                        deps=lndeps)
                    lndeps = []
            for j in range(2):
                vv = p.op("vector", lambda e, j=j, ST=ST, MT=MT: e.bn_aggr(
                    out=mvt[:, MT + j * 2: MT + j * 2 + 2],
                    in_=stt[:, ST + j * 12: ST + j * 12 + 12]),
                    deps=[("vch", vv)])
            mvv = mvt[:, MT: MT + 4].rearrange("p (t two) -> p t two", two=2)
            var2 = mvv[:, :, 1:2].rearrange("p t one -> p (t one)")
            mean2 = mvv[:, :, 0:1].rearrange("p t one -> p (t one)")
            vv = p.op("vector", lambda e, L=L, var2=var2: e.tensor_scalar(
                out=lns[:, L + 0: L + 2], in0=var2, scalar1=1e-5, scalar2=None,
                op0=AL.add, op1=AL.bypass), deps=[("vch", vv)])
            sv = p.op("scalar", lambda e, L=L: e.activation(
                out=lns[:, L + 2: L + 4], in_=lns[:, L + 0: L + 2], func=AF.Sqrt),
                deps=[("vch", vv)])
            vv = p.op("vector", lambda e, L=L: e.reciprocal(
                out=lns[:, L + 4: L + 6], in_=lns[:, L + 2: L + 4]),
                deps=[("sch", sv)])
            vv = p.op("vector", lambda e, L=L: e.scalar_tensor_tensor(
                out=lns[:, L + 6: L + 8], in0=lns[:, L + 0: L + 2], scalar=1.0,
                in1=lns[:, L + 4: L + 6],
                op0=AL.mult, op1=AL.mult), deps=[("vch", vv)])
            vv = p.op("vector", lambda e, L=L: e.tensor_tensor(
                out=lns[:, L + 6: L + 8], in0=lns[:, L + 6: L + 8],
                in1=lns[:, L + 2: L + 4], op=AL.add),
                deps=[("vch", vv)])
            vv = p.op("vector", lambda e, L=L: e.tensor_scalar(
                out=lns[:, L + 6: L + 8], in0=lns[:, L + 6: L + 8], scalar1=0.5,
                scalar2=None, op0=AL.mult, op1=AL.bypass), deps=[("vch", vv)])
            vv = p.op("vector", lambda e, L=L: e.reciprocal(
                out=lns[:, L + 8: L + 10], in_=lns[:, L + 6: L + 8]),
                deps=[("vch", vv)])
            vv = p.op("vector", lambda e, L=L, mean2=mean2: e.scalar_tensor_tensor(
                out=lns[:, L + 10: L + 12], in0=mean2, scalar=-1.0,
                in1=lns[:, L + 8: L + 10],
                op0=AL.mult, op1=AL.mult), deps=[("vch", vv)])
            xndeps = [("vch", vv), (xsem, dxc)]
            if hs == 1:
                xndeps.append(("vch", wqproj_ready))
            if hs >= 2:
                xndeps += [("tch", xn1_cons_t[hs - 2]), ("vch", xn1_cons_v[hs - 2]),
                           ("sch", xn1_cons_s2[hs - 2])]
            sn = 0
            for j in range(2):
                sn = p.op("scalar", lambda e, j=j, slot=slot, L=L: e.activation(
                    out=xn1[:, slot * 2048 + j * 1024: slot * 2048 + (j + 1) * 1024],
                    in_=xin[:, slot * 2048 + j * 1024: slot * 2048 + (j + 1) * 1024],
                    func=AF.Identity, scale=lns[:, L + 8 + j: L + 9 + j],
                    bias=lns[:, L + 10 + j: L + 11 + j]), deps=xndeps)
            xin_cons_s[hs] = sn
            norm_done = sn
            norm_by_hs[hs] = sn
            st0[hs]["dxc"] = dxc
            st0[hs]["norm"] = sn

        def work_stage(hs):
            nonlocal xq_evac
            slot = st0[hs]["slot"]
            xsem = st0[hs]["xsem"]
            xsem_n = st0[hs]["xsem_n"]
            norm_pre = st0[hs]["norm"]
            sa_pre = [None, None]
            if hs < 8:
                axg0 = qtmp.setdefault("_axg", {})
                for j in range(2):
                    sa_pre[j] = p.op(
                        "scalar", lambda e, j=j, slot=slot: e.activation(
                            out=q_ax[:, j * 1024:(j + 1) * 1024],
                            in_=xn1[:, slot * 2048 + j * 1024:
                                    slot * 2048 + (j + 1) * 1024],
                            func=AF.Abs),
                        deps=[("sch", norm_pre), ("vch", axg0.get(j, 0))])
            dxc = st0[hs]["dxc"]
            norm_done = st0[hs]["norm"]

            tt = 0
            for ecpair in range(4):
                bank = ecpair % 2
                bdeps = [("sch", norm_done), tp_evac[bank],
                         ("dID", 32)]
                if hs >= 2:
                    bdeps.append((xsem_n, xnT_cons_d[hs - 2]))
                    bdeps.append(("tch", xnT_cons_t[hs - 2]))
                for q4 in range(4):
                    ec = ecpair * 2 + q4 // 2
                    j = q4 % 2
                    tt = p.op("tensor", lambda e, b=bank, q4=q4, ec=ec, j=j,
                              slot=slot: e.transpose(
                        out=ps_tp[b][:, q4 * 128:(q4 + 1) * 128],
                        in_=xn1[:, slot * 2048 + j * 1024 + ec * 128:
                                 slot * 2048 + j * 1024 + ec * 128 + 128],
                        identity=idf[:]), deps=bdeps)
                    bdeps = []
                ec0 = ecpair * 2
                tp_evac[bank] = ("sch", p.op(
                    "scalar", lambda e, b=bank, ec0=ec0, slot=slot: e.copy(
                        out=xnT2[:, slot * 2048 + ec0 * 256:
                                 slot * 2048 + ec0 * 256 + 512],
                        in_=ps_tp[b][:]), deps=[("tch", tt)]))
            xnT_ready = tp_evac[0], tp_evac[1]
            xn1_cons_t[hs] = tt
            sxnT_r = sxnT_h.rearrange("p (g f) -> p g f", f=S)
            xnT_cons_d[hs] = p.op(
                "scalar", lambda e, slot=slot, hs=hs: e.dma_start(
                    out=sxnT_r[:, :, hs * 256:(hs + 1) * 256],
                    in_=xnT2[:, slot * 2048:(slot + 1) * 2048]
                    .rearrange("p (g f) -> p g f", f=256)),
                deps=list(xnT_ready), inc=xsem_n, inc_by=16)
            xnT_spill_val[hs] = (xsem_n, xnT_cons_d[hs])
            for dc in range(8):
                bank = dc % 2
                mdeps = [("vch", wk_ready), xnT_ready[0], xnT_ready[1],
                         k_evac_bank[bank]]
                for ec in range(8):
                    tt = p.op("tensor", lambda e, b=bank, dc=dc, ec=ec, slot=slot: e.matmul(
                        out=ps_k[b][:, 0:256],
                        lhsT=wkbf[:, ec * 1024 + dc * 128: ec * 1024 + dc * 128 + 128],
                        rhs=xnT2[:, slot * 2048 + ec * 256: slot * 2048 + ec * 256 + 256],
                        start=(ec == 0), stop=(ec == 7)), deps=mdeps)
                    mdeps = []
                k_evac_bank[bank] = ("sch", p.op(
                    "scalar", lambda e, b=bank, dc=dc, hs=hs: e.copy(
                        out=K_fm[:, dc * S + hs * 256: dc * S + hs * 256 + 256],
                        in_=ps_k[b][:, 0:256]), deps=[("tch", tt)]))
            if hs < 8:
                for dc in range(8):
                    bank = dc % 2
                    mdeps = [("vch", wqproj_ready), k_evac_bank[bank]]
                    for ec in range(8):
                        tt = p.op("tensor", lambda e, b=bank, dc=dc, ec=ec,
                                  slot=slot: e.matmul(
                            out=ps_k[b][:, 0:256],
                            lhsT=wqbf[:, ec * 1024 + dc * 128: ec * 1024 + dc * 128 + 128],
                            rhs=xnT2[:, slot * 2048 + ec * 256: slot * 2048 + ec * 256 + 256],
                            start=(ec == 0), stop=(ec == 7)), deps=mdeps)
                        mdeps = []
                    k_evac_bank[bank] = ("sch", p.op(
                        "scalar", lambda e, b=bank, dc=dc, hs=hs: e.copy(
                            out=Q_fm[:, dc * TOK + hs * 256: dc * TOK + hs * 256 + 256],
                            in_=ps_k[b][:, 0:256]), deps=[("tch", tt)]))
            xnT_cons_t[hs] = tt
            if hs < 8:
                vls = [0, 0]
                for j in range(2):
                    xqssem0 = "dXQSa" if j == 0 else "dXQSb"
                    o0 = j * 1024
                    xn_ap = xn1[:, slot * 2048 + o0: slot * 2048 + o0 + 1024]
                    out_ap = xq1[:, o0: o0 + 1024]
                    vls[j] = fp4_quant(p, xn_ap,
                                       q_ax[:, j * 1024:(j + 1) * 1024],
                                       out_ap, qtmp, 64,
                                       dep_src=[("sch", norm_done),
                                                (xqssem0, xqst_dep[j])],
                                       gkey=j, sa=sa_pre[j])
                vlast = vls[1]
                for j in range(2):
                    tq = 0
                    tdeps = [("vch", vls[j]), ("vch", xq_evac), ("dID", 32)]
                    for ec in range(8):
                        tq = p.op("tensor", lambda e, ec=ec, j=j: e.transpose(
                            out=ps_xq[:, ec * 128:(ec + 1) * 128],
                            in_=xq1[:, j * 1024 + ec * 128: j * 1024 + ec * 128 + 128],
                            identity=idb[:]), deps=tdeps)
                        tdeps = []
                    xqssem = "dXQSa" if j == 0 else "dXQSb"
                    xq_evac = p.op("vector", lambda e, j=j: e.tensor_copy(
                        out=xq1[:, j * 1024:(j + 1) * 1024], in_=ps_xq[:]),
                        deps=[("tch", tq), (xqssem, xqst_dep[j])])
                    tt2 = 2 * hs + j
                    xqst_dep[j] = p.cnt[xqssem] + 16
                    p.op("gpsimd", lambda e, tt2=tt2, j=j: e.dma_start(
                        out=sxqT_h[:, tt2 * 1024:(tt2 + 1) * 1024],
                        in_=xq1[:, j * 1024:(j + 1) * 1024]),
                        deps=[("vch", xq_evac)], inc=xqssem, inc_by=16)
                xn1_cons_v[hs] = vlast
                xn1_cons_s2[hs] = p.cnt["sch"]
                xin_cons[hs] = vlast
            else:
                xn1_cons_v[hs] = p.cnt["vch"]
                xn1_cons_s2[hs] = p.cnt["sch"]
                xin_cons[hs] = p.cnt["vch"]

        st0 = {}
        ln_stage(0)
        for hs in range(NHS):
            if hs + 1 < NHS:
                ln_stage(hs + 1)
            work_stage(hs)

        es_p0.close()

        phase_fence()
        es_attn2 = contextlib.ExitStack()
        V_tm = es_attn2.enter_context(nc.sbuf_tensor("V_tm", [128, 32 * 1032], BF16))
        es_pvq = contextlib.ExitStack()
        wproj = es_pvq.enter_context(nc.sbuf_tensor("wproj", [128, 8 * 1024], BF16))
        wstg2 = es_pvq.enter_context(nc.sbuf_tensor("wstg2", [128, 2 * 1024], F32))
        es_pv = contextlib.ExitStack()
        xstr = es_pv.enter_context(nc.sbuf_tensor("xstr", [128, 2 * 2048], BF16))
        ps_v = [es_pv.enter_context(nc.psum_tensor(f"ps_v{i}", [128, 512], F32))
                for i in range(2)]
        ones_c = p.op("gpsimd", lambda e: e.memset(
            V_tm[:].rearrange("p (a b) -> p a b", b=1032)[:, :, 1024:1032], 1.0),
            deps=[("tch", p.cnt["tch"]), ("vch", p.cnt["vch"]),
                  ("sch", p.cnt["sch"]), ("dXNa", p.cnt["dXNa"]),
                  ("dXNb", p.cnt["dXNb"]), ("dXQSa", p.cnt["dXQSa"]),
                  ("dXQSb", p.cnt["dXQSb"])])
        castdep = [0] * 8
        for ec in range(8):
            sl = ec % 2
            wsem = "dWSa" if sl == 0 else "dWSb"
            dep = [("sch", castdep[ec - 2] if ec >= 2 else 0)]
            dc = p.dma("sync", wsem, wstg2[:, sl * 1024:(sl + 1) * 1024],
                       wvT_h[ec * 128:(ec + 1) * 128, :], deps=dep)
            castdep[ec] = p.op("scalar", lambda e, ec=ec, sl=sl: e.activation(
                out=wproj[:, ec * 1024:(ec + 1) * 1024],
                in_=wstg2[:, sl * 1024:(sl + 1) * 1024], func=AF.Identity),
                deps=[(wsem, dc)])
        sxnT_r = sxnT_h.rearrange("p (g f) -> p g f", f=S)
        v_bank_evac = [0, 0]
        xstr_cons = [0] * NHS
        for hs in range(NHS):
            slot = hs % 2
            ssem = "dSIa" if slot == 0 else "dSIb"
            deps = [xnT_spill_val[hs]]
            if hs >= 2:
                deps.append(("tch", xstr_cons[hs - 2]))
            dcs = p.dma("sync", ssem, xstr[:, slot * 2048:(slot + 1) * 2048]
                        .rearrange("p (g f) -> p g f", f=256),
                        sxnT_r[:, :, hs * 256:(hs + 1) * 256], deps=deps)
            pw_step(1)
            for tc_ in range(2):
                for db in range(2):
                    bank = (tc_ * 2 + db) % 2
                    mdeps = [(ssem, dcs),
                             ("sch", v_bank_evac[bank])]
                    for ec in range(8):
                        mdeps.append(("sch", castdep[ec]))
                        tt = p.op("tensor", lambda e, b=bank, tc_=tc_, db=db, ec=ec,
                                  slot=slot: e.matmul(
                            out=ps_v[b][:],
                            lhsT=xstr[:, slot * 2048 + ec * 256 + tc_ * 128:
                                      slot * 2048 + ec * 256 + tc_ * 128 + 128],
                            rhs=wproj[:, ec * 1024 + db * 512: ec * 1024 + db * 512 + 512],
                            start=(ec == 0), stop=(ec == 7)), deps=mdeps)
                        mdeps = []
                    tglob = 2 * hs + tc_
                    v_bank_evac[bank] = p.op(
                        "scalar", lambda e, b=bank, tglob=tglob, db=db: e.copy(
                            out=V_tm[:, tglob * 1032 + db * 512:
                                     tglob * 1032 + db * 512 + 512],
                            in_=ps_v[b][:]), deps=[("tch", tt)])
            xstr_cons[hs] = tt
        pw_drain()
        cc_done = p.op("gpsimd", lambda e: e.collective_compute(
            "AllGather", AL.bypass,
            replica_groups=[[0, 1, 2, 3, 4, 5, 6, 7]],
            ins=[ccin_h[:, :]], outs=[ccout_h[:, :, :]]),
            deps=[("dPWSa", p.cnt["dPWSa"]), ("dPWSb", p.cnt["dPWSb"])])
        es_pv.close()

        es_pvq.close()

        phase_fence()
        es_p2 = contextlib.ExitStack()
        pT = es_p2.enter_context(nc.sbuf_tensor("pT", [128, 32 * 256], BF16))
        attev = es_p2.enter_context(nc.sbuf_tensor("attev", [128, 2 * 1024], F32))
        arec = es_p2.enter_context(nc.sbuf_tensor("arec", [128, 2], F32))
        ps_l = [es_p2.enter_context(nc.psum_tensor(f"ps_l{i}", [128, 512], F32))
                for i in range(2)]
        ps_a = es_p2.enter_context(nc.psum_tensor("ps_a", [128, 1024], F32))
        ps_d = es_p2.enter_context(nc.psum_tensor("ps_d", [128, 512], F32))
        exp_bank = [0, 0]
        att_store_val = [None] * 16
        pv_done_prev = 0
        att_evac_v = 0
        att_evac_s = 0
        attev_dma = [0, 0]
        kq_ready_t = p.cnt["tch"]
        kq_ready_s = p.cnt["sch"]
        for qb in range(8):
            for kc in range(32):
                bank = kc % 2
                mdeps = [("sch", exp_bank[bank]), ("sch", kq_ready_s),
                         ("gch", 2)]
                if kc == 0:
                    mdeps.append(("tch", pv_done_prev))
                for dc in range(8):
                    tt = p.op("tensor", lambda e, b=bank, kc=kc, dc=dc, qb=qb: e.matmul(
                        out=ps_l[b][:, 0:256],
                        lhsT=K_fm[:, dc * S + kc * 128: dc * S + kc * 128 + 128],
                        rhs=Q_fm[:, dc * TOK + qb * 256: dc * TOK + qb * 256 + 256],
                        start=(dc == 0), stop=(dc == 7)), deps=mdeps)
                    mdeps = []
                exp_bank[bank] = p.op("scalar", lambda e, b=bank, kc=kc: e.activation(
                    out=pT[:, kc * 256:(kc + 1) * 256], in_=ps_l[b][:, 0:256],
                    func=AF.Exp, bias=nb40[:], scale=1.0), deps=[("tch", tt)])
            exps_done = max(exp_bank)
            for qc in range(2):
                mdeps = [("sch", exps_done), ("vch", att_evac_v),
                         ("sch", att_evac_s)]
                for kc in range(32):
                    tt = p.op("tensor", lambda e, kc=kc, qc=qc: e.matmul(
                        out=ps_a[:, 0:512],
                        lhsT=pT[:, kc * 256 + qc * 128: kc * 256 + qc * 128 + 128],
                        rhs=V_tm[:, kc * 1032: kc * 1032 + 512],
                        start=(kc == 0), stop=(kc == 31)), deps=mdeps)
                    mdeps = []
                    tt = p.op("tensor", lambda e, kc=kc, qc=qc: e.matmul(
                        out=ps_a[:, 512:1024],
                        lhsT=pT[:, kc * 256 + qc * 128: kc * 256 + qc * 128 + 128],
                        rhs=V_tm[:, kc * 1032 + 512: kc * 1032 + 1024],
                        start=(kc == 0), stop=(kc == 31)))
                    tt = p.op("tensor", lambda e, kc=kc, qc=qc: e.matmul(
                        out=ps_d[:, 0:8],
                        lhsT=pT[:, kc * 256 + qc * 128: kc * 256 + qc * 128 + 128],
                        rhs=V_tm[:, kc * 1032 + 1024: kc * 1032 + 1032],
                        start=(kc == 0), stop=(kc == 31)))
                slot = qc
                vv = p.op("vector", lambda e, slot=slot: e.tensor_scalar(
                    out=arec[:, slot: slot + 1], in0=ps_d[:, 0:1], scalar1=1e-30,
                    scalar2=None, op0=AL.max, op1=AL.bypass),
                    deps=[("tch", tt), ("sch", att_evac_s)])
                vv = p.op("vector", lambda e, slot=slot: e.reciprocal(
                    out=arec[:, slot: slot + 1], in_=arec[:, slot: slot + 1]),
                    deps=[("vch", vv)])
                vv = p.op("vector", lambda e, slot=slot: e.tensor_scalar(
                    out=arec[:, slot: slot + 1], in0=arec[:, slot: slot + 1],
                    scalar1=NF, scalar2=None, op0=AL.mult, op1=AL.bypass),
                    deps=[("vch", vv)])
                att_evac_v = vv
                avsem = "dAVa" if slot == 0 else "dAVb"
                sdeps = [("vch", vv), (avsem, attev_dma[slot])]
                ss = p.op("scalar", lambda e, slot=slot: e.activation(
                    out=attev[:, slot * 1024:(slot + 1) * 1024], in_=ps_a[:, 0:1024],
                    func=AF.Identity, scale=arec[:, slot: slot + 1]), deps=sdeps)
                att_evac_s = ss
                row = (qb * 2 + qc) * 128
                attev_dma[slot] = p.cnt[avsem] + 16
                p.op("scalar", lambda e, row=row, slot=slot: e.dma_start(
                    out=satt_h[row: row + 128, :],
                    in_=attev[:, slot * 1024:(slot + 1) * 1024]),
                    deps=[("sch", ss)], inc=avsem, inc_by=16)
                att_store_val[qb * 2 + qc] = (avsem, p.cnt[avsem])
            pv_done_prev = p.cnt["tch"]
        es_p2.close()
        es_attn2.close()
        es_attn1.close()
        es_pw.close()

        phase_fence(engines=("tensor", "vector", "scalar", "sync", "gpsimd"))
        es_p3 = contextlib.ExitStack()
        p3 = lambda n, sh, dt: es_p3.enter_context(nc.sbuf_tensor(n, sh, dt))
        wgq = p3("wgq", [128, 8 * I], BF16)
        wuq = p3("wuq", [128, 8 * I], BF16)
        wdq = p3("wdq", [128, 22 * D], BF16)
        xqT_t = p3("xqT_t", [128, 2 * 1024], BF16)
        h_f = p3("h_f", [128, I], F32)
        hq = p3("hq", [128, 2 * I], BF16)
        hqT = p3("hqT", [128, 22 * 128], BF16)
        sgb = p3("sgb", [128, 2 * 512], F32)
        xt = p3("xt", [128, 2 * 1024], F32)
        at = p3("at", [128, 2 * 1024], F32)
        f_ax = p3("f_ax", [128, 2 * 1408], F32)
        f_t = p3("f_t", [128, 1408], F32)
        f_u1 = p3("f_u1", [128, 1408], F32)
        f_bm = p3("f_bm", [128, 88], F32)
        f_sc = p3("f_sc", [128, 88], F32)
        f_r = p3("f_r", [128, 88], F32)
        ftmp = {"qt": f_t, "qu1": f_u1,
                "qbm": f_bm, "qsc": f_sc, "qr": f_r}
        ps_g = [es_p3.enter_context(nc.psum_tensor(f"ps_g{i}", [128, 512], F32))
                for i in range(2)]
        ps_u = [es_p3.enter_context(nc.psum_tensor(f"ps_u{i}", [128, 512], F32))
                for i in range(2)]
        ps_dn = es_p3.enter_context(nc.psum_tensor("ps_dn", [128, 1024], F32))
        ps_ht = [es_p3.enter_context(nc.psum_tensor(f"ps_ht{i}", [128, 1024], BF16))
                 for i in range(2)]

        def grp_of(g):
            kind, r, c0, width = pw_global[g]
            return "dW3g%d" % (r // 4) if kind in ("g", "u") else "dW3d"
        order = sorted(range(136), key=lambda g: (grp_of(g) == "dW3d", grp_of(g)))
        ldeps = [("gch", cc_done)]
        for g in order:
            kind, r, c0, width = pw_global[g]
            rank, k = divmod(g, 17)
            src = ccout_h[rank, :, k * 512: k * 512 + width] \
                .rearrange("p (j f) -> p j f", f=128)
            nsub = (width + 127) // 128
            if kind == "g" or kind == "u":
                wdst = wgq if kind == "g" else wuq
                dst = wdst[:].rearrange("p (q f) -> p q f", f=I)[
                    :, c0 // 128:(c0 + width) // 128, r * 128:(r + 1) * 128]
            else:
                dst = wdq[:].rearrange("p (q f) -> p q f", f=D)[
                    :, c0 // 128: c0 // 128 + nsub, r * 128:(r + 1) * 128]
            p.dma("gpsimd", grp_of(g), dst, src, deps=ldeps)
            ldeps = []
        w3g_total = {sn: p.cnt[sn] for sn in
                     ("dW3g0", "dW3g1", "dW3g2", "dW3g3", "dW3g4", "dW3g5",
                      "dW3d")}

        xq_cons = [0] * 16
        x_cons = [0] * 16
        at_cons = [0, 0]
        g_evac_bank = [0, 0]
        hmul_bank = [0, 0]
        ht_evac = [0, 0]
        dn_evac = 0
        ot_dma = [0, 0]
        hq_guard = [0, 0]
        stA = {}
        iblocks = [(0, 512), (512, 512), (1024, 512), (1536, 512), (2048, 512),
                   (2560, 256)]

        def p3_stage_a_load(tt2):
            slot = tt2 % 2
            ssem = "dSIa" if slot == 0 else "dSIb"
            xsem = "dXa" if slot == 0 else "dXb"
            deps = [("dXQSa", p.cnt["dXQSa"]), ("dXQSb", p.cnt["dXQSb"])] \
                if tt2 < 2 else [("tch", xq_cons[tt2 - 2])]
            dxq = p.dma("sync", ssem, xqT_t[:, slot * 1024:(slot + 1) * 1024],
                        sxqT_h[:, tt2 * 1024:(tt2 + 1) * 1024], deps=deps)
            ysem = "dYa" if slot == 0 else "dYb"
            deps = [] if tt2 < 2 else [(ysem, x_cons[tt2 - 2])]
            dxt = p.dma("sync", xsem, xt[:, slot * 1024:(slot + 1) * 1024],
                        xb_h[tt2 * 128:(tt2 + 1) * 128, :], deps=deps)
            atsem = "dATa" if slot == 0 else "dATb"
            dat = p.dma("sync", atsem, at[:, slot * 1024:(slot + 1) * 1024],
                        satt_h[tt2 * 128:(tt2 + 1) * 128, :],
                        deps=[("vch", at_cons[slot]), att_store_val[tt2]])
            stA[tt2] = dict(dxt=dxt, dat=dat, atsem=atsem, slot=slot,
                            dxq=dxq, ssem=ssem, hmul_ib=[])

        def p3_stage_a_mm(tt2, bsel):
            st = stA[tt2]
            slot = st["slot"]
            ssem, dxq = st["ssem"], st["dxq"]
            for bi, (i0, iw) in enumerate(iblocks):
                if bi not in bsel:
                    continue
                bank = bi % 2
                gsn = "dW3g%d" % (i0 // 512)
                mdeps = [(gsn, w3g_total[gsn]), (ssem, dxq),
                         ("sch", g_evac_bank[bank])]
                for ec in range(8):
                    tg = p.op("tensor", lambda e, b=bank, ec=ec, i0=i0, iw=iw,
                              slot=slot: e.matmul(
                        out=ps_g[b][:, 0:iw],
                        lhsT=xqT_t[:, slot * 1024 + ec * 128: slot * 1024 + ec * 128 + 128],
                        rhs=wgq[:, ec * I + i0: ec * I + i0 + iw],
                        start=(ec == 0), stop=(ec == 7)), deps=mdeps)
                    mdeps = []
                mdeps = [("vch", hmul_bank[bank])]
                for ec in range(8):
                    tu = p.op("tensor", lambda e, b=bank, ec=ec, i0=i0, iw=iw,
                              slot=slot: e.matmul(
                        out=ps_u[b][:, 0:iw],
                        lhsT=xqT_t[:, slot * 1024 + ec * 128: slot * 1024 + ec * 128 + 128],
                        rhs=wuq[:, ec * I + i0: ec * I + i0 + iw],
                        start=(ec == 0), stop=(ec == 7)), deps=mdeps)
                    mdeps = []
                g_evac_bank[bank] = p.op(
                    "scalar", lambda e, b=bank, iw=iw, sg0=bank * 512: e.activation(
                        out=sgb[:, sg0: sg0 + iw], in_=ps_g[b][:, 0:iw],
                        func=AF.Silu),
                    deps=[("tch", tg), ("vch", hmul_bank[bank])])
                hmul_bank[bank] = p.op(
                    "vector", lambda e, b=bank, i0=i0, iw=iw, sg0=bank * 512:
                    e.tensor_tensor(
                        out=h_f[:, i0: i0 + iw], in0=sgb[:, sg0: sg0 + iw],
                        in1=ps_u[b][:, 0:iw],
                        op=AL.mult), deps=[("sch", g_evac_bank[bank]), ("tch", tu)])
                st["hmul_ib"].append(hmul_bank[bank])

        def p3_stage_a_q(tt2, ch):
            st = stA[tt2]
            slot = st["slot"]
            hmul_dep = st["hmul_ib"][2] if ch == 0 else st["hmul_ib"][5]
            hap = h_f[:, ch * 1408: (ch + 1) * 1408]
            dsrc = [("vch", hmul_dep), ("tch", hq_guard[slot])]
            vq = fp4_quant(p, hap, f_ax[:, ch * 1408:(ch + 1) * 1408],
                           hq[:, slot * I + ch * 1408: slot * I + (ch + 1) * 1408],
                           ftmp, 88, dep_src=dsrc, gkey=ch)
            st["vq"] = vq

        def p3_stage_b_mm(tt2):
            st = stA[tt2]
            slot = st["slot"]
            tq = 0
            groups = [(0, 8), (8, 8), (16, 6)]
            for gi, (ic0, n) in enumerate(groups):
                bank = gi % 2
                tdeps = [("vch", st["vq"]), ("sch", ht_evac[bank])]
                for k in range(n):
                    ic = ic0 + k
                    tq = p.op("tensor", lambda e, b=bank, k=k, ic=ic, slot=slot:
                              e.transpose(
                        out=ps_ht[b][:, k * 128:(k + 1) * 128],
                        in_=hq[:, slot * I + ic * 128: slot * I + (ic + 1) * 128],
                        identity=idb[:]), deps=tdeps)
                    tdeps = []
                ht_evac[bank] = p.op("scalar", lambda e, b=bank, ic0=ic0, n=n: e.copy(
                    out=hqT[:, ic0 * 128: (ic0 + n) * 128],
                    in_=ps_ht[b][:, 0: n * 128]), deps=[("tch", tq)])
            hq_guard[slot] = tq
            hT_ready = max(ht_evac)
            mdeps = [("sch", hT_ready), ("dW3d", w3g_total["dW3d"]),
                     ("vch", dn_evac)]
            for ic in range(22):
                td = p.op("tensor", lambda e, ic=ic: e.matmul(
                    out=ps_dn[:, 0:512],
                    lhsT=hqT[:, ic * 128:(ic + 1) * 128],
                    rhs=wdq[:, ic * D: ic * D + 512],
                    start=(ic == 0), stop=(ic == 21)), deps=mdeps)
                mdeps = []
                td = p.op("tensor", lambda e, ic=ic: e.matmul(
                    out=ps_dn[:, 512:1024],
                    lhsT=hqT[:, ic * 128:(ic + 1) * 128],
                    rhs=wdq[:, ic * D + 512: ic * D + 1024],
                    start=(ic == 0), stop=(ic == 21)))
            st["td"] = td

        def p3_stage_b_epi(tt2):
            nonlocal dn_evac
            st = stA.pop(tt2)
            slot = st["slot"]
            td = st["td"]
            xsem = "dXa" if slot == 0 else "dXb"
            ysem = "dYa" if slot == 0 else "dYb"
            vv = 0
            gtt = 0
            for half in range(2):
                c0 = half * 512
                vv = p.op("vector", lambda e, c0=c0, slot=slot: e.scalar_tensor_tensor(
                    out=xt[:, slot * 1024 + c0: slot * 1024 + c0 + 512],
                    in0=ps_dn[:, c0: c0 + 512], scalar=1.0,
                    in1=xt[:, slot * 1024 + c0: slot * 1024 + c0 + 512],
                    op0=AL.mult, op1=AL.add),
                    deps=[("tch", td), (xsem, st["dxt"]), (ysem, ot_dma[slot])])
                vv = p.op("vector", lambda e, c0=c0, slot=slot: e.tensor_tensor(
                    out=xt[:, slot * 1024 + c0: slot * 1024 + c0 + 512],
                    in0=xt[:, slot * 1024 + c0: slot * 1024 + c0 + 512],
                    in1=at[:, slot * 1024 + c0: slot * 1024 + c0 + 512], op=AL.add),
                    deps=[("vch", vv), (st["atsem"], st["dat"])])
            dn_evac = vv
            at_cons[slot] = vv
            xq_cons[tt2] = td
            ot_dma[slot] = p.cnt[ysem] + 16
            x_cons[tt2] = p.cnt[ysem] + 16
            p.op("scalar", lambda e, tt2=tt2, slot=slot: e.dma_start(
                out=y_h[tt2 * 128:(tt2 + 1) * 128, :],
                in_=xt[:, slot * 1024:(slot + 1) * 1024]),
                deps=[("vch", vv)], inc=ysem, inc_by=16)

        for tt2 in range(16):
            p3_stage_a_load(tt2)
            p3_stage_a_mm(tt2, (0, 1, 2))
            p3_stage_a_q(tt2, 0)
            p3_stage_a_mm(tt2, (3, 4, 5))
            if tt2 >= 1:
                p3_stage_b_mm(tt2 - 1)
            p3_stage_a_q(tt2, 1)
            if tt2 >= 1:
                p3_stage_b_epi(tt2 - 1)
        p3_stage_b_mm(15)
        p3_stage_b_epi(15)
        es_p3.close()

        p.wait("sync", "dYa", p.cnt["dYa"])
        p.wait("sync", "dYb", p.cnt["dYb"])

        with nc.Block() as block:
            streams = {e: [fn for (eng, fn) in p.ops if eng == e]
                       for e in Plan.ENGS}

            @block.sync
            def _(e):
                for fn in streams["sync"]:
                    fn(e)

            @block.scalar
            def _(e):
                for fn in streams["scalar"]:
                    fn(e)

            @block.vector
            def _(e):
                for fn in streams["vector"]:
                    fn(e)

            @block.tensor
            def _(e):
                for fn in streams["tensor"]:
                    fn(e)

            @block.gpsimd
            def _(e):
                for fn in streams["gpsimd"]:
                    fn(e)

    return nc


_NC_CACHE = None


def kernel(L0_attn_x, L0_attn_Wq, L0_attn_Wk, L0_attn_Wv,
           L0_ffn_w_gate, L0_ffn_w_up, L0_ffn_w_down):
    global _NC_CACHE
    x = np.ascontiguousarray(np.asarray(L0_attn_x, dtype=np.float32))
    wqT = np.ascontiguousarray(np.asarray(L0_attn_Wq, np.float32).T)
    wkT = np.ascontiguousarray(np.asarray(L0_attn_Wk, np.float32).T)
    wvT = np.ascontiguousarray(np.asarray(L0_attn_Wv, np.float32).T)
    wg = np.ascontiguousarray(np.asarray(L0_ffn_w_gate, np.float32))
    wu = np.ascontiguousarray(np.asarray(L0_ffn_w_up, np.float32))
    wd = np.ascontiguousarray(np.asarray(L0_ffn_w_down, np.float32))
    pw_global = []
    for kind in ("g", "u"):
        for r in range(22):
            for cch in range(2):
                pw_global.append((kind, r, cch * 512, 512))
    for r in range(8):
        for cch in range(6):
            c0 = cch * 512
            pw_global.append(("d", r, c0, min(512, I - c0)))
    pwc_all = np.zeros((136, 128, 512), dtype=np.float32)
    for g, (kind, r, c0, width) in enumerate(pw_global):
        src = {"g": wg, "u": wu, "d": wd}[kind]
        pwc_all[g, :, :width] = src[r * 128:(r + 1) * 128, c0:c0 + width]
    idf = np.eye(128, dtype=np.float32)
    idb = np.eye(128, dtype=np.float32).astype(ml_dtypes.bfloat16)
    if _NC_CACHE is None:
        _NC_CACHE = build_nc()
    nc = _NC_CACHE
    in_maps = []
    for c in range(8):
        b, h = c // 2, c % 2
        own = x[b, h * TOK:(h + 1) * TOK]
        oth = x[b, (1 - h) * TOK:(2 - h) * TOK]
        xbc = np.ascontiguousarray(np.concatenate([own, oth], axis=0))
        in_maps.append({"xb": xbc, "wqT": wqT, "wkT": wkT, "wvT": wvT,
                        "pwc": np.ascontiguousarray(pwc_all[17 * c: 17 * (c + 1)]),
                        "idf": idf, "idb": idb})
    res = run_bass_kernel_spmd(nc, in_maps, core_ids=list(range(8)))
    y = np.empty((4, S, D), dtype=np.float32)
    for c in range(8):
        b, h = c // 2, c % 2
        y[b, h * TOK:(h + 1) * TOK] = res.results[c]["y"]
    return y


# version 26
# speedup vs baseline: 1.0356x; 1.0110x over previous
import contextlib
import numpy as np
import ml_dtypes
import concourse.bass as bass
import concourse.mybir as mybir
from concourse.bass_utils import run_bass_kernel_spmd

F32 = mybir.dt.float32
BF16 = mybir.dt.bfloat16
AL = mybir.AluOpType
AF = mybir.ActivationFunctionType
AX = mybir.AxisListType

S = 4096
D = 1024
I = 2816
TOK = 2048
NHS = 16
NF = 1.0 / 32.0
EXPB = -40.0


class Plan:

    ENGS = ("sync", "scalar", "vector", "tensor", "gpsimd")

    def __init__(self, nc):
        self.nc = nc
        self.ops = []
        self.cnt = {}
        self.sem = {}
        self.lastw = {e: {} for e in self.ENGS}

    def mksem(self, *names):
        for n in names:
            self.sem[n] = self._es.enter_context(self.nc.semaphore(n))
            self.cnt[n] = 0

    def emit(self, eng, fn):
        self.ops.append((eng, fn))

    def wait(self, eng, sname, val):
        if val <= 0:
            return
        if self.lastw[eng].get(sname, 0) >= val:
            return
        self.lastw[eng][sname] = val
        h = self.sem[sname]
        self.emit(eng, lambda e, h=h, v=val: e.wait_ge(h, v))

    AUTOCHAIN = {"vector": "vch", "scalar": "sch", "gpsimd": "gch"}

    def op(self, eng, build, deps=(), inc=None, inc_by=1):
        deps = list(deps)
        ac = self.AUTOCHAIN.get(eng)
        if ac is not None and self.cnt[ac] > 0:
            deps.append((ac, self.cnt[ac]))
        for (sn, v) in deps:
            self.wait(eng, sn, v)
        sname = inc if inc is not None else {"sync": None, "scalar": "sch",
                                             "vector": "vch", "tensor": "tch",
                                             "gpsimd": "gch"}[eng]
        assert sname is not None
        self.cnt[sname] += inc_by
        h = self.sem[sname]
        k = inc_by

        def fn(e, build=build, h=h, k=k):
            build(e).then_inc(h, k)
        self.emit(eng, fn)
        return self.cnt[sname]

    def dma(self, eng, sname, out, in_, deps=()):
        return self.op(eng, lambda e, o=out, i=in_: e.dma_start(out=o, in_=i),
                       deps=deps, inc=sname, inc_by=16)


def fp4_quant(p, xn_ap, ax_ap, out_ap, tmp, nblk, dep_src, gkey=0, gps=False,
              sa=None):
    F = 16 * nblk
    qt, qu1 = tmp["qt"], tmp["qu1"]
    qbm, qsc, qr = tmp["qbm"], tmp["qsc"], tmp["qr"]
    axg = tmp.setdefault("_axg", {})
    xv = xn_ap.rearrange("p (b w) -> p b w", w=16)
    if sa is None:
        sa = p.op("scalar", lambda e: e.activation(
            out=ax_ap, in_=xn_ap, func=AF.Abs),
            deps=list(dep_src) + [("vch", axg.get(gkey, 0))])
    c = p.op("vector", lambda e: e.tensor_reduce(
        out=qbm[:, 0:nblk], in_=xv, axis=AX.X, op=AL.max,
        apply_absolute_value=True), deps=dep_src)
    c = p.op("vector", lambda e: e.tensor_scalar(
        out=qsc[:, 0:nblk], in0=qbm[:, 0:nblk], scalar1=1.0 / 12.0,
        scalar2=0.5e-12, op0=AL.mult, op1=AL.max), deps=[("vch", c)])
    c = p.op("vector", lambda e: e.reciprocal(out=qr[:, 0:nblk], in_=qsc[:, 0:nblk]),
             deps=[("vch", c)])
    rb = qr[:, 0:nblk].unsqueeze(-1).broadcast_to([128, nblk, 16])
    scb = qsc[:, 0:nblk].unsqueeze(-1).broadcast_to([128, nblk, 16])
    ct = p.op("vector", lambda e: e.tensor_tensor(
        out=qt[:, 0:F].rearrange("p (b w) -> p b w", w=16),
        in0=ax_ap.rearrange("p (b w) -> p b w", w=16),
        in1=rb, op=AL.mult), deps=[("vch", c), ("sch", sa)])
    if gps:
        g1 = p.op("gpsimd", lambda e: e.tensor_scalar(
            out=ax_ap, in0=xn_ap, scalar1=0.0, scalar2=2.0,
            op0=AL.is_gt, op1=AL.mult), deps=list(dep_src) + [("vch", ct)])
        g1 = p.op("gpsimd", lambda e: e.tensor_scalar(
            out=ax_ap, in0=ax_ap, scalar1=-1.0, scalar2=None,
            op0=AL.add, op1=AL.bypass))
        gss = p.op("gpsimd", lambda e: e.tensor_tensor(
            out=ax_ap.rearrange("p (b w) -> p b w", w=16),
            in0=ax_ap.rearrange("p (b w) -> p b w", w=16),
            in1=scb, op=AL.mult))
    c = p.op("vector", lambda e: e.tensor_scalar(
        out=qu1[:, 0:F], in0=qt[:, 0:F], scalar1=10.0, scalar2=2.0,
        op0=AL.is_gt, op1=AL.mult), deps=[("vch", ct)])
    for thr in (5.0, 7.0):
        c = p.op("vector", lambda e, t=thr: e.scalar_tensor_tensor(
            out=qu1[:, 0:F], in0=qt[:, 0:F], scalar=t, in1=qu1[:, 0:F],
            op0=AL.is_gt, op1=AL.add), deps=[("vch", c)])
    c = p.op("vector", lambda e: e.tensor_scalar(
        out=qu1[:, 0:F], in0=qu1[:, 0:F], scalar1=2.0, scalar2=None,
        op0=AL.mult, op1=AL.bypass), deps=[("vch", c)])
    for thr in (0.5, 1.5, 2.5, 3.5):
        c = p.op("vector", lambda e, t=thr: e.scalar_tensor_tensor(
            out=qu1[:, 0:F], in0=qt[:, 0:F], scalar=t, in1=qu1[:, 0:F],
            op0=AL.is_gt, op1=AL.add), deps=[("vch", c)])
    if gps:
        c = p.op("vector", lambda e: e.tensor_tensor(
            out=out_ap, in0=qu1[:, 0:F], in1=ax_ap, op=AL.mult),
            deps=[("vch", c), ("gch", gss)])
        axg[gkey] = c
    else:
        c2 = p.op("vector", lambda e: e.tensor_scalar(
            out=qt[:, 0:F], in0=xn_ap, scalar1=0.0, scalar2=2.0,
            op0=AL.is_gt, op1=AL.mult), deps=[("vch", c)])
        c2 = p.op("vector", lambda e: e.scalar_tensor_tensor(
            out=qt[:, 0:F].rearrange("p (b w) -> p b w", w=16),
            in0=qt[:, 0:F].rearrange("p (b w) -> p b w", w=16),
            scalar=-1.0, in1=scb, op0=AL.add, op1=AL.mult), deps=[("vch", c2)])
        c = p.op("vector", lambda e: e.tensor_tensor(
            out=out_ap, in0=qu1[:, 0:F], in1=qt[:, 0:F], op=AL.mult),
            deps=[("vch", c2)])
        axg[gkey] = ct
    return c


def build_nc():
    nc = bass.Bass("TRN2", target_bir_lowering=False)
    xb_h = nc.dram_tensor("xb", [S, D], F32, kind="ExternalInput").ap()
    wqT_h = nc.dram_tensor("wqT", [D, D], F32, kind="ExternalInput").ap()
    wkT_h = nc.dram_tensor("wkT", [D, D], F32, kind="ExternalInput").ap()
    wvT_h = nc.dram_tensor("wvT", [D, D], F32, kind="ExternalInput").ap()
    pwc_h = nc.dram_tensor("pwc", [17, 128, 512], F32, kind="ExternalInput").ap()
    idf_h = nc.dram_tensor("idf", [128, 128], F32, kind="ExternalInput").ap()
    idb_h = nc.dram_tensor("idb", [128, 128], BF16, kind="ExternalInput").ap()
    y_h = nc.dram_tensor("y", [TOK, D], F32, kind="ExternalOutput").ap()
    sxnT_h = nc.dram_tensor("s_xnT", [128, 8 * S], BF16).ap()
    sxqT_h = nc.dram_tensor("s_xqT", [128, 16 * 1024], BF16).ap()
    satt_h = nc.dram_tensor("s_att", [TOK, D], F32).ap()
    ccin_h = nc.dram_tensor("cc_in", [128, 17 * 512], BF16).ap()
    ccout_h = nc.dram_tensor("cc_out", [8, 128, 17 * 512], BF16).ap()

    es_top = contextlib.ExitStack()
    p = Plan(nc)
    p._es = es_top
    sb = lambda name, shape, dt: es_top.enter_context(nc.sbuf_tensor(name, shape, dt))

    idf = sb("idf_s", [128, 128], F32)
    idb = sb("idb_s", [128, 128], BF16)
    nb40 = sb("nb40", [128, 1], F32)

    p.mksem("tch", "vch", "sch", "gch", "dID",
            "dWa", "dWb", "dWSa", "dWSb",
            "dW3g0", "dW3g1", "dW3g2", "dW3g3", "dW3g4", "dW3g5", "dW3d",
            "dPWa", "dPWb", "dPWc", "dPWSa", "dPWSb",
            "dXa", "dXb", "dXNa", "dXNb", "dXQSa", "dXQSb",
            "dSIa", "dSIb", "dAVa", "dAVb", "dATa", "dATb", "dYa", "dYb")

    def phase_fence(engines=("tensor", "vector", "scalar", "sync")):
        snap = {"tch": p.cnt["tch"], "vch": p.cnt["vch"], "sch": p.cnt["sch"]}
        dsems = ("dXNa", "dXNb", "dXQSa", "dXQSb", "dAVa", "dAVb",
                 "dPWSa", "dPWSb")
        dsnap = {sn: p.cnt[sn] for sn in dsems}
        for e in engines:
            for o, sn in (("tensor", "tch"), ("vector", "vch"), ("scalar", "sch")):
                if o != e:
                    p.wait(e, sn, snap[sn])
            for sn in dsems:
                p.wait(e, sn, dsnap[sn])

    with es_top:
        p.dma("sync", "dID", idf[:], idf_h[:, :])
        p.dma("sync", "dID", idb[:], idb_h[:, :])
        p.op("gpsimd", lambda e: e.memset(nb40[:], EXPB))

        es_pw = contextlib.ExitStack()

        def pwsb(n, sh, dt):
            return es_pw.enter_context(nc.sbuf_tensor(n, sh, dt, side="right"))
        pwb = {
            "win": pwsb("pw_in", [128, 2 * 512], F32),
            "ax": pwsb("pw_ax", [128, 512], F32),
            "qt": pwsb("pw_t", [128, 512], F32),
            "qu1": pwsb("pw_u1", [128, 512], F32),
            
            "qbm": pwsb("pw_bm", [128, 32], F32),
            "qsc": pwsb("pw_sc", [128, 32], F32),
            "qr": pwsb("pw_r", [128, 32], F32),
            "wq": pwsb("pw_q", [128, 2 * 512], BF16),
            "wqT": pwsb("pw_qT", [128, 2 * 512], BF16),
        }
        pwb["qss"] = pwb["ax"]
        ps_pw = es_pw.enter_context(nc.psum_tensor("ps_pw", [128, 2048], BF16))

        pw_global = []
        for kind in ("g", "u"):
            for r in range(22):
                for cch in range(2):
                    pw_global.append((kind, r, cch * 512, 512))
        for r in range(8):
            for cch in range(6):
                c0 = cch * 512
                pw_global.append(("d", r, c0, min(512, I - c0)))
        assert len(pw_global) == 136
        pw_sched = list(range(17))
        pw_state = {"a": 0, "b": 0, "c": 0,
                    "slotdep": [0, 0],
                    "chain": {}, "trans": {}, "loadsem": {},
                    "st_dep": [0, 0], "bank_evac": [0, 0]}

        def pw_stage_a():
            i = pw_state["a"]
            if i >= len(pw_sched):
                return False
            pw_state["a"] = i + 1
            width = 512
            slot = i % 2
            win = pwb["win"][:, slot * 512: slot * 512 + width]
            dep = [("vch", pw_state["slotdep"][slot])]
            dsem = ("dPWa", "dPWb")[slot]
            dcount = p.dma("sync", dsem, win, pwc_h[i], deps=dep)
            nblk = width // 16
            dsrc = [(dsem, dcount)]
            if i >= 2:
                assert i - 2 in pw_state["trans"], (i, pw_state)
                dsrc.append(("tch", pw_state["trans"][i - 2]))
            vc = fp4_quant(p, win, pwb["ax"][:, 0:width],
                           pwb["wq"][:, (i % 2) * 512:(i % 2) * 512 + width],
                           pwb, nblk, dep_src=dsrc)
            pw_state["slotdep"][slot] = vc
            pw_state["chain"][i] = vc
            return True

        def pw_stage_b():
            i = pw_state["b"]
            if i >= pw_state["a"] or i >= len(pw_sched):
                return False
            pw_state["b"] = i + 1
            width = 512
            nsub = 4
            half = i % 2
            bank = (i // 2) % 2
            base = bank * 1024 + (i % 2) * 512
            tc = None
            tdeps = [("vch", pw_state["chain"][i]),
                     ("sch", pw_state["bank_evac"][bank]), ("dID", 32)]
            for k in range(nsub):
                tc = p.op("tensor", lambda e, k=k, base=base, half=half: e.transpose(
                    out=ps_pw[:, base + k * 128: base + (k + 1) * 128],
                    in_=pwb["wq"][:, half * 512 + k * 128: half * 512 + k * 128 + 128],
                    identity=idb[:]), deps=tdeps)
                tdeps = []
            pw_state["trans"][i] = tc
            return True

        def pw_stage_c():
            i = pw_state["c"]
            if i >= pw_state["b"] or i >= len(pw_sched):
                return False
            pw_state["c"] = i + 1
            width = 512
            half = i % 2
            bank = (i // 2) % 2
            base = bank * 1024 + (i % 2) * 512
            pwssem = "dPWSa" if half == 0 else "dPWSb"
            ec = p.op("scalar", lambda e, w=width, base=base, half=half: e.copy(
                out=pwb["wqT"][:, half * 512: half * 512 + w],
                in_=ps_pw[:, base: base + w]),
                deps=[("tch", pw_state["trans"][i]),
                      (pwssem, pw_state["st_dep"][half])])
            pw_state["bank_evac"][bank] = ec
            pw_state["st_dep"][half] = p.cnt[pwssem] + 16
            p.op("scalar", lambda e, i=i, half=half: e.dma_start(
                out=ccin_h[:, i * 512:(i + 1) * 512],
                in_=pwb["wqT"][:, half * 512: half * 512 + 512]),
                deps=[("sch", ec)], inc=pwssem, inc_by=16)
            return True

        PW_LAG = 2

        def pw_step(n=1):
            for _ in range(n):
                if pw_state["b"] - pw_state["c"] >= PW_LAG or \
                        pw_state["b"] >= len(pw_sched):
                    pw_stage_c()
                if pw_state["a"] - pw_state["b"] >= PW_LAG or \
                        pw_state["a"] >= len(pw_sched):
                    pw_stage_b()
                pw_stage_a()

        def pw_drain():
            while pw_state["c"] < len(pw_sched):
                pw_stage_c()
                pw_stage_b()
                pw_stage_a()

        es_attn1 = contextlib.ExitStack()
        K_fm = es_attn1.enter_context(nc.sbuf_tensor("K_fm", [128, 8 * S], BF16))
        Q_fm = es_attn1.enter_context(nc.sbuf_tensor("Q_fm", [128, 8 * TOK], BF16))

        es_p0 = contextlib.ExitStack()
        p0 = lambda n, sh, dt: es_p0.enter_context(nc.sbuf_tensor(n, sh, dt))
        wkbf = p0("wk_bf", [128, 8 * 1024], BF16)
        wqbf = p0("wq_bf", [128, 8 * 1024], BF16)
        xin = p0("xin", [128, 2 * 2048], F32)
        xn1 = p0("xn1", [128, 2 * 2048], F32)
        xnT2 = p0("xnT2", [128, 2 * 2048], BF16)
        stt = p0("stt", [128, 2 * 24], F32)
        mvt = p0("mvt", [128, 2 * 4], F32)
        lns = p0("lns", [128, 2 * 12], F32)
        q_ax = p0("q_ax", [128, 2 * 1024], F32)
        q_t = p0("q_t", [128, 1024], F32)
        q_u1 = p0("q_u1", [128, 1024], F32)
        q_bm = p0("q_bm", [128, 64], F32)
        q_sc = p0("q_sc", [128, 64], F32)
        q_r = p0("q_r", [128, 64], F32)
        xq1 = p0("xq1", [128, 2048], BF16)
        qtmp = {"qt": q_t, "qu1": q_u1,
                "qbm": q_bm, "qsc": q_sc, "qr": q_r}
        ps_tp = [es_p0.enter_context(nc.psum_tensor(f"ps_tp{i}", [128, 512], F32))
                 for i in range(2)]
        ps_k = [es_p0.enter_context(nc.psum_tensor(f"ps_k{i}", [128, 512], F32))
                for i in range(2)]
        ps_xq = es_p0.enter_context(nc.psum_tensor("ps_xq", [128, 1024], BF16))

        wstg = xn1[:, 2048:3072]
        castdep = [0] * 16
        for i6 in range(16):
            w_src, w_dst = (wkT_h, wkbf) if i6 < 8 else (wqT_h, wqbf)
            ec = i6 % 8
            dep = [("vch", castdep[i6 - 1] if i6 >= 1 else 0)]
            dc = p.dma("sync", "dWa", wstg,
                       w_src[ec * 128:(ec + 1) * 128, :], deps=dep)
            castdep[i6] = p.op("vector", lambda e, ec=ec, w_dst=w_dst:
                               e.tensor_copy(
                out=w_dst[:, ec * 1024:(ec + 1) * 1024],
                in_=wstg), deps=[("dWa", dc)])
        wk_ready = castdep[7]
        wqproj_ready = castdep[15]

        xb_r = xb_h.rearrange("(H j p) e -> H p j e", j=2, p=128)
        xin_cons = [0] * NHS
        xin_cons_s = [0] * NHS
        xnT_cons_t = [0] * NHS
        xnT_cons_d = [0] * NHS
        kevac = {}
        xnT_spill_val = [None] * NHS
        norm_by_hs = [0] * NHS
        xn1_cons_t = [0] * NHS
        xn1_cons_v = [0] * NHS
        xn1_cons_s2 = [0] * NHS
        tp_evac = [("sch", 0), ("sch", 0)]
        k_evac_bank = [("sch", 0), ("sch", 0)]
        xq_evac = 0
        xqst_dep = [0, 0]

        def ln_stage(hs):
            slot = hs % 2
            xsem = "dXa" if slot == 0 else "dXb"
            xsem_n = "dXNa" if slot == 0 else "dXNb"
            st0[hs] = {"slot": slot, "xsem": xsem, "xsem_n": xsem_n}
            deps = []
            if hs >= 2:
                deps = [("vch", xin_cons[hs - 2]), ("sch", xin_cons_s[hs - 2])]
            dxc = p.dma("sync", xsem, xin[:, slot * 2048:(slot + 1) * 2048]
                        .rearrange("p (j e) -> p j e", j=2), xb_r[hs], deps=deps)
            L = slot * 12
            ST = slot * 24
            MT = slot * 4
            lndeps = [(xsem, dxc)]
            if hs >= 2:
                lndeps.append(("sch", norm_by_hs[hs - 2]))
            vv = 0
            for j in range(2):
                for half in range(2):
                    vv = p.op("vector", lambda e, j=j, h2=half, slot=slot, ST=ST:
                              e.bn_stats(
                        out=stt[:, ST + j * 12 + h2 * 6: ST + j * 12 + h2 * 6 + 6],
                        in_=xin[:, slot * 2048 + j * 1024 + h2 * 512:
                                slot * 2048 + j * 1024 + h2 * 512 + 512]),
                        deps=lndeps)
                    lndeps = []
            st0[hs]["xin_v"] = vv
            for j in range(2):
                vv = p.op("vector", lambda e, j=j, ST=ST, MT=MT: e.bn_aggr(
                    out=mvt[:, MT + j * 2: MT + j * 2 + 2],
                    in_=stt[:, ST + j * 12: ST + j * 12 + 12]),
                    deps=[("vch", vv)])
            mvv = mvt[:, MT: MT + 4].rearrange("p (t two) -> p t two", two=2)
            var2 = mvv[:, :, 1:2].rearrange("p t one -> p (t one)")
            mean2 = mvv[:, :, 0:1].rearrange("p t one -> p (t one)")
            vv = p.op("vector", lambda e, L=L, var2=var2: e.tensor_scalar(
                out=lns[:, L + 0: L + 2], in0=var2, scalar1=1e-5, scalar2=None,
                op0=AL.add, op1=AL.bypass), deps=[("vch", vv)])
            sv = p.op("scalar", lambda e, L=L: e.activation(
                out=lns[:, L + 2: L + 4], in_=lns[:, L + 0: L + 2], func=AF.Sqrt),
                deps=[("vch", vv)])
            vv = p.op("vector", lambda e, L=L: e.reciprocal(
                out=lns[:, L + 4: L + 6], in_=lns[:, L + 2: L + 4]),
                deps=[("sch", sv)])
            vv = p.op("vector", lambda e, L=L: e.scalar_tensor_tensor(
                out=lns[:, L + 6: L + 8], in0=lns[:, L + 0: L + 2], scalar=1.0,
                in1=lns[:, L + 4: L + 6],
                op0=AL.mult, op1=AL.mult), deps=[("vch", vv)])
            vv = p.op("vector", lambda e, L=L: e.tensor_tensor(
                out=lns[:, L + 6: L + 8], in0=lns[:, L + 6: L + 8],
                in1=lns[:, L + 2: L + 4], op=AL.add),
                deps=[("vch", vv)])
            vv = p.op("vector", lambda e, L=L: e.tensor_scalar(
                out=lns[:, L + 6: L + 8], in0=lns[:, L + 6: L + 8], scalar1=0.5,
                scalar2=None, op0=AL.mult, op1=AL.bypass), deps=[("vch", vv)])
            vv = p.op("vector", lambda e, L=L: e.reciprocal(
                out=lns[:, L + 8: L + 10], in_=lns[:, L + 6: L + 8]),
                deps=[("vch", vv)])
            vv = p.op("vector", lambda e, L=L, mean2=mean2: e.scalar_tensor_tensor(
                out=lns[:, L + 10: L + 12], in0=mean2, scalar=-1.0,
                in1=lns[:, L + 8: L + 10],
                op0=AL.mult, op1=AL.mult), deps=[("vch", vv)])
            xndeps = [("vch", vv), (xsem, dxc)]
            if hs == 1:
                xndeps.append(("vch", wqproj_ready))
            if hs >= 2:
                xndeps += [("tch", xn1_cons_t[hs - 2]), ("vch", xn1_cons_v[hs - 2]),
                           ("sch", xn1_cons_s2[hs - 2])]
            sn = 0
            for j in range(2):
                sn = p.op("scalar", lambda e, j=j, slot=slot, L=L: e.activation(
                    out=xn1[:, slot * 2048 + j * 1024: slot * 2048 + (j + 1) * 1024],
                    in_=xin[:, slot * 2048 + j * 1024: slot * 2048 + (j + 1) * 1024],
                    func=AF.Identity, scale=lns[:, L + 8 + j: L + 9 + j],
                    bias=lns[:, L + 10 + j: L + 11 + j]), deps=xndeps)
            xin_cons_s[hs] = sn
            norm_done = sn
            norm_by_hs[hs] = sn
            st0[hs]["dxc"] = dxc
            st0[hs]["norm"] = sn

        def work_stage(hs):
            nonlocal xq_evac
            slot = st0[hs]["slot"]
            xsem = st0[hs]["xsem"]
            xsem_n = st0[hs]["xsem_n"]
            norm_pre = st0[hs]["norm"]
            sa_pre = [None, None]
            if hs < 8:
                axg0 = qtmp.setdefault("_axg", {})
                for j in range(2):
                    sa_pre[j] = p.op(
                        "scalar", lambda e, j=j, slot=slot: e.activation(
                            out=q_ax[:, j * 1024:(j + 1) * 1024],
                            in_=xn1[:, slot * 2048 + j * 1024:
                                    slot * 2048 + (j + 1) * 1024],
                            func=AF.Abs),
                        deps=[("sch", norm_pre), ("vch", axg0.get(j, 0))])
            dxc = st0[hs]["dxc"]
            norm_done = st0[hs]["norm"]

            tt = 0
            for ecpair in range(4):
                bank = ecpair % 2
                bdeps = [("sch", norm_done), tp_evac[bank],
                         ("dID", 32)]
                if hs >= 2:
                    bdeps.append((xsem_n, xnT_cons_d[hs - 2]))
                    bdeps.append(("tch", xnT_cons_t[hs - 2]))
                for q4 in range(4):
                    ec = ecpair * 2 + q4 // 2
                    j = q4 % 2
                    tt = p.op("tensor", lambda e, b=bank, q4=q4, ec=ec, j=j,
                              slot=slot: e.transpose(
                        out=ps_tp[b][:, q4 * 128:(q4 + 1) * 128],
                        in_=xn1[:, slot * 2048 + j * 1024 + ec * 128:
                                 slot * 2048 + j * 1024 + ec * 128 + 128],
                        identity=idf[:]), deps=bdeps)
                    bdeps = []
                ec0 = ecpair * 2
                tp_evac[bank] = ("sch", p.op(
                    "scalar", lambda e, b=bank, ec0=ec0, slot=slot: e.copy(
                        out=xnT2[:, slot * 2048 + ec0 * 256:
                                 slot * 2048 + ec0 * 256 + 512],
                        in_=ps_tp[b][:]), deps=[("tch", tt)]))
            xnT_ready = tp_evac[0], tp_evac[1]
            xn1_cons_t[hs] = tt
            sxnT_r = sxnT_h.rearrange("p (g f) -> p g f", f=S)
            xnT_cons_d[hs] = p.op(
                "scalar", lambda e, slot=slot, hs=hs: e.dma_start(
                    out=sxnT_r[:, :, hs * 256:(hs + 1) * 256],
                    in_=xnT2[:, slot * 2048:(slot + 1) * 2048]
                    .rearrange("p (g f) -> p g f", f=256)),
                deps=list(xnT_ready), inc=xsem_n, inc_by=16)
            xnT_spill_val[hs] = (xsem_n, xnT_cons_d[hs])
            for dc in range(8):
                bank = dc % 2
                mdeps = [("vch", wk_ready), xnT_ready[0], xnT_ready[1],
                         k_evac_bank[bank]]
                for ec in range(8):
                    tt = p.op("tensor", lambda e, b=bank, dc=dc, ec=ec, slot=slot: e.matmul(
                        out=ps_k[b][:, 0:256],
                        lhsT=wkbf[:, ec * 1024 + dc * 128: ec * 1024 + dc * 128 + 128],
                        rhs=xnT2[:, slot * 2048 + ec * 256: slot * 2048 + ec * 256 + 256],
                        start=(ec == 0), stop=(ec == 7)), deps=mdeps)
                    mdeps = []
                k_evac_bank[bank] = ("sch", p.op(
                    "scalar", lambda e, b=bank, dc=dc, hs=hs: e.copy(
                        out=K_fm[:, dc * S + hs * 256: dc * S + hs * 256 + 256],
                        in_=ps_k[b][:, 0:256]), deps=[("tch", tt)]))
            if hs < 8:
                for dc in range(8):
                    bank = dc % 2
                    mdeps = [("vch", wqproj_ready), k_evac_bank[bank]]
                    for ec in range(8):
                        tt = p.op("tensor", lambda e, b=bank, dc=dc, ec=ec,
                                  slot=slot: e.matmul(
                            out=ps_k[b][:, 0:256],
                            lhsT=wqbf[:, ec * 1024 + dc * 128: ec * 1024 + dc * 128 + 128],
                            rhs=xnT2[:, slot * 2048 + ec * 256: slot * 2048 + ec * 256 + 256],
                            start=(ec == 0), stop=(ec == 7)), deps=mdeps)
                        mdeps = []
                    k_evac_bank[bank] = ("sch", p.op(
                        "scalar", lambda e, b=bank, dc=dc, hs=hs: e.copy(
                            out=Q_fm[:, dc * TOK + hs * 256: dc * TOK + hs * 256 + 256],
                            in_=ps_k[b][:, 0:256]), deps=[("tch", tt)]))
            xnT_cons_t[hs] = tt
            if hs < 8:
                vls = [0, 0]
                for j in range(2):
                    xqssem0 = "dXQSa" if j == 0 else "dXQSb"
                    o0 = j * 1024
                    xn_ap = xn1[:, slot * 2048 + o0: slot * 2048 + o0 + 1024]
                    out_ap = xq1[:, o0: o0 + 1024]
                    vls[j] = fp4_quant(p, xn_ap,
                                       q_ax[:, j * 1024:(j + 1) * 1024],
                                       out_ap, qtmp, 64,
                                       dep_src=[("sch", norm_done),
                                                (xqssem0, xqst_dep[j])],
                                       gkey=j, sa=sa_pre[j])
                vlast = vls[1]
                for j in range(2):
                    tq = 0
                    tdeps = [("vch", vls[j]), ("vch", xq_evac), ("dID", 32)]
                    for ec in range(8):
                        tq = p.op("tensor", lambda e, ec=ec, j=j: e.transpose(
                            out=ps_xq[:, ec * 128:(ec + 1) * 128],
                            in_=xq1[:, j * 1024 + ec * 128: j * 1024 + ec * 128 + 128],
                            identity=idb[:]), deps=tdeps)
                        tdeps = []
                    xqssem = "dXQSa" if j == 0 else "dXQSb"
                    xq_evac = p.op("vector", lambda e, j=j: e.tensor_copy(
                        out=xq1[:, j * 1024:(j + 1) * 1024], in_=ps_xq[:]),
                        deps=[("tch", tq), (xqssem, xqst_dep[j])])
                    tt2 = 2 * hs + j
                    xqst_dep[j] = p.cnt[xqssem] + 16
                    p.op("gpsimd", lambda e, tt2=tt2, j=j: e.dma_start(
                        out=sxqT_h[:, tt2 * 1024:(tt2 + 1) * 1024],
                        in_=xq1[:, j * 1024:(j + 1) * 1024]),
                        deps=[("vch", xq_evac)], inc=xqssem, inc_by=16)
                xn1_cons_v[hs] = vlast
                xn1_cons_s2[hs] = p.cnt["sch"]
                xin_cons[hs] = st0[hs]["xin_v"]
            else:
                xn1_cons_v[hs] = p.cnt["vch"]
                xn1_cons_s2[hs] = p.cnt["sch"]
                xin_cons[hs] = st0[hs]["xin_v"]

        st0 = {}
        ln_stage(0)
        for hs in range(NHS):
            if hs + 1 < NHS:
                ln_stage(hs + 1)
            work_stage(hs)

        es_p0.close()

        phase_fence()
        es_attn2 = contextlib.ExitStack()
        V_tm = es_attn2.enter_context(nc.sbuf_tensor("V_tm", [128, 32 * 1032], BF16))
        es_pvq = contextlib.ExitStack()
        wproj = es_pvq.enter_context(nc.sbuf_tensor("wproj", [128, 8 * 1024], BF16))
        wstg2 = es_pvq.enter_context(nc.sbuf_tensor("wstg2", [128, 2 * 1024], F32))
        es_pv = contextlib.ExitStack()
        xstr = es_pv.enter_context(nc.sbuf_tensor("xstr", [128, 2 * 2048], BF16))
        ps_v = [es_pv.enter_context(nc.psum_tensor(f"ps_v{i}", [128, 512], F32))
                for i in range(2)]
        ones_c = p.op("gpsimd", lambda e: e.memset(
            V_tm[:].rearrange("p (a b) -> p a b", b=1032)[:, :, 1024:1032], 1.0),
            deps=[("tch", p.cnt["tch"]), ("vch", p.cnt["vch"]),
                  ("sch", p.cnt["sch"]), ("dXNa", p.cnt["dXNa"]),
                  ("dXNb", p.cnt["dXNb"]), ("dXQSa", p.cnt["dXQSa"]),
                  ("dXQSb", p.cnt["dXQSb"])])
        castdep = [0] * 8
        for ec in range(8):
            sl = ec % 2
            wsem = "dWSa" if sl == 0 else "dWSb"
            dep = [("sch", castdep[ec - 2] if ec >= 2 else 0)]
            dc = p.dma("sync", wsem, wstg2[:, sl * 1024:(sl + 1) * 1024],
                       wvT_h[ec * 128:(ec + 1) * 128, :], deps=dep)
            castdep[ec] = p.op("scalar", lambda e, ec=ec, sl=sl: e.activation(
                out=wproj[:, ec * 1024:(ec + 1) * 1024],
                in_=wstg2[:, sl * 1024:(sl + 1) * 1024], func=AF.Identity),
                deps=[(wsem, dc)])
        sxnT_r = sxnT_h.rearrange("p (g f) -> p g f", f=S)
        v_bank_evac = [0, 0]
        xstr_cons = [0] * NHS
        for hs in range(NHS):
            slot = hs % 2
            ssem = "dSIa" if slot == 0 else "dSIb"
            deps = [xnT_spill_val[hs]]
            if hs >= 2:
                deps.append(("tch", xstr_cons[hs - 2]))
            dcs = p.dma("sync", ssem, xstr[:, slot * 2048:(slot + 1) * 2048]
                        .rearrange("p (g f) -> p g f", f=256),
                        sxnT_r[:, :, hs * 256:(hs + 1) * 256], deps=deps)
            pw_step(1)
            for tc_ in range(2):
                for db in range(2):
                    bank = (tc_ * 2 + db) % 2
                    mdeps = [(ssem, dcs),
                             ("sch", v_bank_evac[bank])]
                    for ec in range(8):
                        mdeps.append(("sch", castdep[ec]))
                        tt = p.op("tensor", lambda e, b=bank, tc_=tc_, db=db, ec=ec,
                                  slot=slot: e.matmul(
                            out=ps_v[b][:],
                            lhsT=xstr[:, slot * 2048 + ec * 256 + tc_ * 128:
                                      slot * 2048 + ec * 256 + tc_ * 128 + 128],
                            rhs=wproj[:, ec * 1024 + db * 512: ec * 1024 + db * 512 + 512],
                            start=(ec == 0), stop=(ec == 7)), deps=mdeps)
                        mdeps = []
                    tglob = 2 * hs + tc_
                    v_bank_evac[bank] = p.op(
                        "scalar", lambda e, b=bank, tglob=tglob, db=db: e.copy(
                            out=V_tm[:, tglob * 1032 + db * 512:
                                     tglob * 1032 + db * 512 + 512],
                            in_=ps_v[b][:]), deps=[("tch", tt)])
            xstr_cons[hs] = tt
        pw_drain()
        cc_done = p.op("gpsimd", lambda e: e.collective_compute(
            "AllGather", AL.bypass,
            replica_groups=[[0, 1, 2, 3, 4, 5, 6, 7]],
            ins=[ccin_h[:, :]], outs=[ccout_h[:, :, :]]),
            deps=[("dPWSa", p.cnt["dPWSa"]), ("dPWSb", p.cnt["dPWSb"])])
        es_pv.close()

        es_pvq.close()

        phase_fence()
        es_p2 = contextlib.ExitStack()
        pT = es_p2.enter_context(nc.sbuf_tensor("pT", [128, 32 * 256], BF16))
        attev = es_p2.enter_context(nc.sbuf_tensor("attev", [128, 2 * 1024], F32))
        arec = es_p2.enter_context(nc.sbuf_tensor("arec", [128, 2], F32))
        ps_l = [es_p2.enter_context(nc.psum_tensor(f"ps_l{i}", [128, 512], F32))
                for i in range(2)]
        ps_a = es_p2.enter_context(nc.psum_tensor("ps_a", [128, 1024], F32))
        ps_d = es_p2.enter_context(nc.psum_tensor("ps_d", [128, 512], F32))
        exp_bank = [0, 0]
        att_store_val = [None] * 16
        pv_done_prev = 0
        att_evac_v = 0
        att_evac_s = 0
        attev_dma = [0, 0]
        kq_ready_t = p.cnt["tch"]
        kq_ready_s = p.cnt["sch"]
        for qb in range(8):
            for kc in range(32):
                bank = kc % 2
                mdeps = [("sch", exp_bank[bank]), ("sch", kq_ready_s),
                         ("gch", 2)]
                if kc == 0:
                    mdeps.append(("tch", pv_done_prev))
                for dc in range(8):
                    tt = p.op("tensor", lambda e, b=bank, kc=kc, dc=dc, qb=qb: e.matmul(
                        out=ps_l[b][:, 0:256],
                        lhsT=K_fm[:, dc * S + kc * 128: dc * S + kc * 128 + 128],
                        rhs=Q_fm[:, dc * TOK + qb * 256: dc * TOK + qb * 256 + 256],
                        start=(dc == 0), stop=(dc == 7)), deps=mdeps)
                    mdeps = []
                exp_bank[bank] = p.op("scalar", lambda e, b=bank, kc=kc: e.activation(
                    out=pT[:, kc * 256:(kc + 1) * 256], in_=ps_l[b][:, 0:256],
                    func=AF.Exp, bias=nb40[:], scale=1.0), deps=[("tch", tt)])
            exps_done = max(exp_bank)
            for qc in range(2):
                mdeps = [("sch", exps_done), ("vch", att_evac_v),
                         ("sch", att_evac_s)]
                for kc in range(32):
                    tt = p.op("tensor", lambda e, kc=kc, qc=qc: e.matmul(
                        out=ps_a[:, 0:512],
                        lhsT=pT[:, kc * 256 + qc * 128: kc * 256 + qc * 128 + 128],
                        rhs=V_tm[:, kc * 1032: kc * 1032 + 512],
                        start=(kc == 0), stop=(kc == 31)), deps=mdeps)
                    mdeps = []
                    tt = p.op("tensor", lambda e, kc=kc, qc=qc: e.matmul(
                        out=ps_a[:, 512:1024],
                        lhsT=pT[:, kc * 256 + qc * 128: kc * 256 + qc * 128 + 128],
                        rhs=V_tm[:, kc * 1032 + 512: kc * 1032 + 1024],
                        start=(kc == 0), stop=(kc == 31)))
                    tt = p.op("tensor", lambda e, kc=kc, qc=qc: e.matmul(
                        out=ps_d[:, 0:8],
                        lhsT=pT[:, kc * 256 + qc * 128: kc * 256 + qc * 128 + 128],
                        rhs=V_tm[:, kc * 1032 + 1024: kc * 1032 + 1032],
                        start=(kc == 0), stop=(kc == 31)))
                slot = qc
                vv = p.op("vector", lambda e, slot=slot: e.tensor_scalar(
                    out=arec[:, slot: slot + 1], in0=ps_d[:, 0:1], scalar1=1e-30,
                    scalar2=None, op0=AL.max, op1=AL.bypass),
                    deps=[("tch", tt), ("sch", att_evac_s)])
                vv = p.op("vector", lambda e, slot=slot: e.reciprocal(
                    out=arec[:, slot: slot + 1], in_=arec[:, slot: slot + 1]),
                    deps=[("vch", vv)])
                vv = p.op("vector", lambda e, slot=slot: e.tensor_scalar(
                    out=arec[:, slot: slot + 1], in0=arec[:, slot: slot + 1],
                    scalar1=NF, scalar2=None, op0=AL.mult, op1=AL.bypass),
                    deps=[("vch", vv)])
                att_evac_v = vv
                avsem = "dAVa" if slot == 0 else "dAVb"
                sdeps = [("vch", vv), (avsem, attev_dma[slot])]
                ss = p.op("scalar", lambda e, slot=slot: e.activation(
                    out=attev[:, slot * 1024:(slot + 1) * 1024], in_=ps_a[:, 0:1024],
                    func=AF.Identity, scale=arec[:, slot: slot + 1]), deps=sdeps)
                att_evac_s = ss
                row = (qb * 2 + qc) * 128
                attev_dma[slot] = p.cnt[avsem] + 16
                p.op("scalar", lambda e, row=row, slot=slot: e.dma_start(
                    out=satt_h[row: row + 128, :],
                    in_=attev[:, slot * 1024:(slot + 1) * 1024]),
                    deps=[("sch", ss)], inc=avsem, inc_by=16)
                att_store_val[qb * 2 + qc] = (avsem, p.cnt[avsem])
            pv_done_prev = p.cnt["tch"]
        es_p2.close()
        es_attn2.close()
        es_attn1.close()
        es_pw.close()

        phase_fence(engines=("tensor", "vector", "scalar", "sync", "gpsimd"))
        es_p3 = contextlib.ExitStack()
        p3 = lambda n, sh, dt: es_p3.enter_context(nc.sbuf_tensor(n, sh, dt))
        wgq = p3("wgq", [128, 8 * I], BF16)
        wuq = p3("wuq", [128, 8 * I], BF16)
        wdq = p3("wdq", [128, 22 * D], BF16)
        xqT_t = p3("xqT_t", [128, 2 * 1024], BF16)
        h_f = p3("h_f", [128, I], F32)
        hq = p3("hq", [128, 2 * I], BF16)
        hqT = p3("hqT", [128, 22 * 128], BF16)
        sgb = p3("sgb", [128, 2 * 512], F32)
        xt = p3("xt", [128, 2 * 1024], F32)
        at = p3("at", [128, 2 * 1024], F32)
        f_ax = p3("f_ax", [128, 2 * 1408], F32)
        f_t = p3("f_t", [128, 1408], F32)
        f_u1 = p3("f_u1", [128, 1408], F32)
        f_bm = p3("f_bm", [128, 88], F32)
        f_sc = p3("f_sc", [128, 88], F32)
        f_r = p3("f_r", [128, 88], F32)
        ftmp = {"qt": f_t, "qu1": f_u1,
                "qbm": f_bm, "qsc": f_sc, "qr": f_r}
        ps_g = [es_p3.enter_context(nc.psum_tensor(f"ps_g{i}", [128, 512], F32))
                for i in range(2)]
        ps_u = [es_p3.enter_context(nc.psum_tensor(f"ps_u{i}", [128, 512], F32))
                for i in range(2)]
        ps_dn = es_p3.enter_context(nc.psum_tensor("ps_dn", [128, 1024], F32))
        ps_ht = [es_p3.enter_context(nc.psum_tensor(f"ps_ht{i}", [128, 1024], BF16))
                 for i in range(2)]

        def grp_of(g):
            kind, r, c0, width = pw_global[g]
            return "dW3g%d" % (r // 4) if kind in ("g", "u") else "dW3d"
        order = sorted(range(136), key=lambda g: (grp_of(g) == "dW3d", grp_of(g)))
        ldeps = [("gch", cc_done)]
        for g in order:
            kind, r, c0, width = pw_global[g]
            rank, k = divmod(g, 17)
            src = ccout_h[rank, :, k * 512: k * 512 + width] \
                .rearrange("p (j f) -> p j f", f=128)
            nsub = (width + 127) // 128
            if kind == "g" or kind == "u":
                wdst = wgq if kind == "g" else wuq
                dst = wdst[:].rearrange("p (q f) -> p q f", f=I)[
                    :, c0 // 128:(c0 + width) // 128, r * 128:(r + 1) * 128]
            else:
                dst = wdq[:].rearrange("p (q f) -> p q f", f=D)[
                    :, c0 // 128: c0 // 128 + nsub, r * 128:(r + 1) * 128]
            p.dma("gpsimd", grp_of(g), dst, src, deps=ldeps)
            ldeps = []
        w3g_total = {sn: p.cnt[sn] for sn in
                     ("dW3g0", "dW3g1", "dW3g2", "dW3g3", "dW3g4", "dW3g5",
                      "dW3d")}

        xq_cons = [0] * 16
        x_cons = [0] * 16
        at_cons = [0, 0]
        g_evac_bank = [0, 0]
        hmul_bank = [0, 0]
        ht_evac = [0, 0]
        dn_evac = 0
        ot_dma = [0, 0]
        hq_guard = [0, 0]
        stA = {}
        iblocks = [(0, 512), (512, 512), (1024, 512), (1536, 512), (2048, 512),
                   (2560, 256)]

        def p3_stage_a_load(tt2):
            slot = tt2 % 2
            ssem = "dSIa" if slot == 0 else "dSIb"
            xsem = "dXa" if slot == 0 else "dXb"
            deps = [("dXQSa", p.cnt["dXQSa"]), ("dXQSb", p.cnt["dXQSb"])] \
                if tt2 < 2 else [("tch", xq_cons[tt2 - 2])]
            dxq = p.dma("sync", ssem, xqT_t[:, slot * 1024:(slot + 1) * 1024],
                        sxqT_h[:, tt2 * 1024:(tt2 + 1) * 1024], deps=deps)
            ysem = "dYa" if slot == 0 else "dYb"
            deps = [] if tt2 < 2 else [(ysem, x_cons[tt2 - 2])]
            dxt = p.dma("sync", xsem, xt[:, slot * 1024:(slot + 1) * 1024],
                        xb_h[tt2 * 128:(tt2 + 1) * 128, :], deps=deps)
            atsem = "dATa" if slot == 0 else "dATb"
            dat = p.dma("sync", atsem, at[:, slot * 1024:(slot + 1) * 1024],
                        satt_h[tt2 * 128:(tt2 + 1) * 128, :],
                        deps=[("vch", at_cons[slot]), att_store_val[tt2]])
            stA[tt2] = dict(dxt=dxt, dat=dat, atsem=atsem, slot=slot,
                            dxq=dxq, ssem=ssem, hmul_ib=[])

        def p3_stage_a_mm(tt2, bsel):
            st = stA[tt2]
            slot = st["slot"]
            ssem, dxq = st["ssem"], st["dxq"]
            for bi, (i0, iw) in enumerate(iblocks):
                if bi not in bsel:
                    continue
                bank = bi % 2
                gsn = "dW3g%d" % (i0 // 512)
                mdeps = [(gsn, w3g_total[gsn]), (ssem, dxq),
                         ("sch", g_evac_bank[bank])]
                for ec in range(8):
                    tg = p.op("tensor", lambda e, b=bank, ec=ec, i0=i0, iw=iw,
                              slot=slot: e.matmul(
                        out=ps_g[b][:, 0:iw],
                        lhsT=xqT_t[:, slot * 1024 + ec * 128: slot * 1024 + ec * 128 + 128],
                        rhs=wgq[:, ec * I + i0: ec * I + i0 + iw],
                        start=(ec == 0), stop=(ec == 7)), deps=mdeps)
                    mdeps = []
                mdeps = [("vch", hmul_bank[bank])]
                for ec in range(8):
                    tu = p.op("tensor", lambda e, b=bank, ec=ec, i0=i0, iw=iw,
                              slot=slot: e.matmul(
                        out=ps_u[b][:, 0:iw],
                        lhsT=xqT_t[:, slot * 1024 + ec * 128: slot * 1024 + ec * 128 + 128],
                        rhs=wuq[:, ec * I + i0: ec * I + i0 + iw],
                        start=(ec == 0), stop=(ec == 7)), deps=mdeps)
                    mdeps = []
                g_evac_bank[bank] = p.op(
                    "scalar", lambda e, b=bank, iw=iw, sg0=bank * 512: e.activation(
                        out=sgb[:, sg0: sg0 + iw], in_=ps_g[b][:, 0:iw],
                        func=AF.Silu),
                    deps=[("tch", tg), ("vch", hmul_bank[bank])])
                hmul_bank[bank] = p.op(
                    "vector", lambda e, b=bank, i0=i0, iw=iw, sg0=bank * 512:
                    e.tensor_tensor(
                        out=h_f[:, i0: i0 + iw], in0=sgb[:, sg0: sg0 + iw],
                        in1=ps_u[b][:, 0:iw],
                        op=AL.mult), deps=[("sch", g_evac_bank[bank]), ("tch", tu)])
                st["hmul_ib"].append(hmul_bank[bank])

        def p3_stage_a_q(tt2, ch):
            st = stA[tt2]
            slot = st["slot"]
            hmul_dep = st["hmul_ib"][2] if ch == 0 else st["hmul_ib"][5]
            hap = h_f[:, ch * 1408: (ch + 1) * 1408]
            dsrc = [("vch", hmul_dep), ("tch", hq_guard[slot])]
            vq = fp4_quant(p, hap, f_ax[:, ch * 1408:(ch + 1) * 1408],
                           hq[:, slot * I + ch * 1408: slot * I + (ch + 1) * 1408],
                           ftmp, 88, dep_src=dsrc, gkey=ch)
            st["vq"] = vq

        def p3_stage_b_mm(tt2):
            st = stA[tt2]
            slot = st["slot"]
            tq = 0
            groups = [(0, 8), (8, 8), (16, 6)]
            for gi, (ic0, n) in enumerate(groups):
                bank = gi % 2
                tdeps = [("vch", st["vq"]), ("sch", ht_evac[bank])]
                for k in range(n):
                    ic = ic0 + k
                    tq = p.op("tensor", lambda e, b=bank, k=k, ic=ic, slot=slot:
                              e.transpose(
                        out=ps_ht[b][:, k * 128:(k + 1) * 128],
                        in_=hq[:, slot * I + ic * 128: slot * I + (ic + 1) * 128],
                        identity=idb[:]), deps=tdeps)
                    tdeps = []
                ht_evac[bank] = p.op("scalar", lambda e, b=bank, ic0=ic0, n=n: e.copy(
                    out=hqT[:, ic0 * 128: (ic0 + n) * 128],
                    in_=ps_ht[b][:, 0: n * 128]), deps=[("tch", tq)])
            hq_guard[slot] = tq
            hT_ready = max(ht_evac)
            mdeps = [("sch", hT_ready), ("dW3d", w3g_total["dW3d"]),
                     ("vch", dn_evac)]
            for ic in range(22):
                td = p.op("tensor", lambda e, ic=ic: e.matmul(
                    out=ps_dn[:, 0:512],
                    lhsT=hqT[:, ic * 128:(ic + 1) * 128],
                    rhs=wdq[:, ic * D: ic * D + 512],
                    start=(ic == 0), stop=(ic == 21)), deps=mdeps)
                mdeps = []
                td = p.op("tensor", lambda e, ic=ic: e.matmul(
                    out=ps_dn[:, 512:1024],
                    lhsT=hqT[:, ic * 128:(ic + 1) * 128],
                    rhs=wdq[:, ic * D + 512: ic * D + 1024],
                    start=(ic == 0), stop=(ic == 21)))
            st["td"] = td

        def p3_stage_b_epi(tt2):
            nonlocal dn_evac
            st = stA.pop(tt2)
            slot = st["slot"]
            td = st["td"]
            xsem = "dXa" if slot == 0 else "dXb"
            ysem = "dYa" if slot == 0 else "dYb"
            vv = 0
            gtt = 0
            for half in range(2):
                c0 = half * 512
                vv = p.op("vector", lambda e, c0=c0, slot=slot: e.scalar_tensor_tensor(
                    out=xt[:, slot * 1024 + c0: slot * 1024 + c0 + 512],
                    in0=ps_dn[:, c0: c0 + 512], scalar=1.0,
                    in1=xt[:, slot * 1024 + c0: slot * 1024 + c0 + 512],
                    op0=AL.mult, op1=AL.add),
                    deps=[("tch", td), (xsem, st["dxt"]), (ysem, ot_dma[slot])])
                vv = p.op("vector", lambda e, c0=c0, slot=slot: e.tensor_tensor(
                    out=xt[:, slot * 1024 + c0: slot * 1024 + c0 + 512],
                    in0=xt[:, slot * 1024 + c0: slot * 1024 + c0 + 512],
                    in1=at[:, slot * 1024 + c0: slot * 1024 + c0 + 512], op=AL.add),
                    deps=[("vch", vv), (st["atsem"], st["dat"])])
            dn_evac = vv
            at_cons[slot] = vv
            xq_cons[tt2] = td
            ot_dma[slot] = p.cnt[ysem] + 16
            x_cons[tt2] = p.cnt[ysem] + 16
            p.op("scalar", lambda e, tt2=tt2, slot=slot: e.dma_start(
                out=y_h[tt2 * 128:(tt2 + 1) * 128, :],
                in_=xt[:, slot * 1024:(slot + 1) * 1024]),
                deps=[("vch", vv)], inc=ysem, inc_by=16)

        for tt2 in range(16):
            p3_stage_a_load(tt2)
            p3_stage_a_mm(tt2, (0, 1, 2))
            p3_stage_a_q(tt2, 0)
            p3_stage_a_mm(tt2, (3, 4, 5))
            if tt2 >= 1:
                p3_stage_b_mm(tt2 - 1)
            p3_stage_a_q(tt2, 1)
            if tt2 >= 1:
                p3_stage_b_epi(tt2 - 1)
        p3_stage_b_mm(15)
        p3_stage_b_epi(15)
        es_p3.close()

        p.wait("sync", "dYa", p.cnt["dYa"])
        p.wait("sync", "dYb", p.cnt["dYb"])

        with nc.Block() as block:
            streams = {e: [fn for (eng, fn) in p.ops if eng == e]
                       for e in Plan.ENGS}

            @block.sync
            def _(e):
                for fn in streams["sync"]:
                    fn(e)

            @block.scalar
            def _(e):
                for fn in streams["scalar"]:
                    fn(e)

            @block.vector
            def _(e):
                for fn in streams["vector"]:
                    fn(e)

            @block.tensor
            def _(e):
                for fn in streams["tensor"]:
                    fn(e)

            @block.gpsimd
            def _(e):
                for fn in streams["gpsimd"]:
                    fn(e)

    return nc


_NC_CACHE = None


def kernel(L0_attn_x, L0_attn_Wq, L0_attn_Wk, L0_attn_Wv,
           L0_ffn_w_gate, L0_ffn_w_up, L0_ffn_w_down):
    global _NC_CACHE
    x = np.ascontiguousarray(np.asarray(L0_attn_x, dtype=np.float32))
    wqT = np.ascontiguousarray(np.asarray(L0_attn_Wq, np.float32).T)
    wkT = np.ascontiguousarray(np.asarray(L0_attn_Wk, np.float32).T)
    wvT = np.ascontiguousarray(np.asarray(L0_attn_Wv, np.float32).T)
    wg = np.ascontiguousarray(np.asarray(L0_ffn_w_gate, np.float32))
    wu = np.ascontiguousarray(np.asarray(L0_ffn_w_up, np.float32))
    wd = np.ascontiguousarray(np.asarray(L0_ffn_w_down, np.float32))
    pw_global = []
    for kind in ("g", "u"):
        for r in range(22):
            for cch in range(2):
                pw_global.append((kind, r, cch * 512, 512))
    for r in range(8):
        for cch in range(6):
            c0 = cch * 512
            pw_global.append(("d", r, c0, min(512, I - c0)))
    pwc_all = np.zeros((136, 128, 512), dtype=np.float32)
    for g, (kind, r, c0, width) in enumerate(pw_global):
        src = {"g": wg, "u": wu, "d": wd}[kind]
        pwc_all[g, :, :width] = src[r * 128:(r + 1) * 128, c0:c0 + width]
    idf = np.eye(128, dtype=np.float32)
    idb = np.eye(128, dtype=np.float32).astype(ml_dtypes.bfloat16)
    if _NC_CACHE is None:
        _NC_CACHE = build_nc()
    nc = _NC_CACHE
    in_maps = []
    for c in range(8):
        b, h = c // 2, c % 2
        own = x[b, h * TOK:(h + 1) * TOK]
        oth = x[b, (1 - h) * TOK:(2 - h) * TOK]
        xbc = np.ascontiguousarray(np.concatenate([own, oth], axis=0))
        in_maps.append({"xb": xbc, "wqT": wqT, "wkT": wkT, "wvT": wvT,
                        "pwc": np.ascontiguousarray(pwc_all[17 * c: 17 * (c + 1)]),
                        "idf": idf, "idb": idb})
    res = run_bass_kernel_spmd(nc, in_maps, core_ids=list(range(8)))
    y = np.empty((4, S, D), dtype=np.float32)
    for c in range(8):
        b, h = c // 2, c % 2
        y[b, h * TOK:(h + 1) * TOK] = res.results[c]["y"]
    return y


# version 27
# speedup vs baseline: 1.0450x; 1.0090x over previous
import contextlib
import numpy as np
import ml_dtypes
import concourse.bass as bass
import concourse.mybir as mybir
from concourse.bass_utils import run_bass_kernel_spmd

F32 = mybir.dt.float32
BF16 = mybir.dt.bfloat16
AL = mybir.AluOpType
AF = mybir.ActivationFunctionType
AX = mybir.AxisListType

S = 4096
D = 1024
I = 2816
TOK = 2048
NHS = 16
NF = 1.0 / 32.0
EXPB = -40.0


class Plan:

    ENGS = ("sync", "scalar", "vector", "tensor", "gpsimd")

    def __init__(self, nc):
        self.nc = nc
        self.ops = []
        self.cnt = {}
        self.sem = {}
        self.lastw = {e: {} for e in self.ENGS}

    def mksem(self, *names):
        for n in names:
            self.sem[n] = self._es.enter_context(self.nc.semaphore(n))
            self.cnt[n] = 0

    def emit(self, eng, fn):
        self.ops.append((eng, fn))

    def wait(self, eng, sname, val):
        if val <= 0:
            return
        if self.lastw[eng].get(sname, 0) >= val:
            return
        self.lastw[eng][sname] = val
        h = self.sem[sname]
        self.emit(eng, lambda e, h=h, v=val: e.wait_ge(h, v))

    AUTOCHAIN = {"vector": "vch", "scalar": "sch", "gpsimd": "gch"}

    def op(self, eng, build, deps=(), inc=None, inc_by=1):
        deps = list(deps)
        ac = self.AUTOCHAIN.get(eng)
        if ac is not None and self.cnt[ac] > 0:
            deps.append((ac, self.cnt[ac]))
        for (sn, v) in deps:
            self.wait(eng, sn, v)
        sname = inc if inc is not None else {"sync": None, "scalar": "sch",
                                             "vector": "vch", "tensor": "tch",
                                             "gpsimd": "gch"}[eng]
        assert sname is not None
        self.cnt[sname] += inc_by
        h = self.sem[sname]
        k = inc_by

        def fn(e, build=build, h=h, k=k):
            build(e).then_inc(h, k)
        self.emit(eng, fn)
        return self.cnt[sname]

    def dma(self, eng, sname, out, in_, deps=()):
        return self.op(eng, lambda e, o=out, i=in_: e.dma_start(out=o, in_=i),
                       deps=deps, inc=sname, inc_by=16)


def fp4_quant(p, xn_ap, ax_ap, out_ap, tmp, nblk, dep_src, gkey=0, gps=False,
              sa=None):
    F = 16 * nblk
    qt, qu1 = tmp["qt"], tmp["qu1"]
    qbm, qsc, qr = tmp["qbm"], tmp["qsc"], tmp["qr"]
    axg = tmp.setdefault("_axg", {})
    xv = xn_ap.rearrange("p (b w) -> p b w", w=16)
    if sa is None:
        sa = p.op("scalar", lambda e: e.activation(
            out=ax_ap, in_=xn_ap, func=AF.Abs),
            deps=list(dep_src) + [("vch", axg.get(gkey, 0))])
    c = p.op("vector", lambda e: e.tensor_reduce(
        out=qbm[:, 0:nblk], in_=xv, axis=AX.X, op=AL.max,
        apply_absolute_value=True), deps=dep_src)
    c = p.op("vector", lambda e: e.tensor_scalar(
        out=qsc[:, 0:nblk], in0=qbm[:, 0:nblk], scalar1=1.0 / 12.0,
        scalar2=0.5e-12, op0=AL.mult, op1=AL.max), deps=[("vch", c)])
    c = p.op("vector", lambda e: e.reciprocal(out=qr[:, 0:nblk], in_=qsc[:, 0:nblk]),
             deps=[("vch", c)])
    rb = qr[:, 0:nblk].unsqueeze(-1).broadcast_to([128, nblk, 16])
    scb = qsc[:, 0:nblk].unsqueeze(-1).broadcast_to([128, nblk, 16])
    ct = p.op("vector", lambda e: e.tensor_tensor(
        out=qt[:, 0:F].rearrange("p (b w) -> p b w", w=16),
        in0=ax_ap.rearrange("p (b w) -> p b w", w=16),
        in1=rb, op=AL.mult), deps=[("vch", c), ("sch", sa)])
    if gps:
        g1 = p.op("gpsimd", lambda e: e.tensor_scalar(
            out=ax_ap, in0=xn_ap, scalar1=0.0, scalar2=2.0,
            op0=AL.is_gt, op1=AL.mult), deps=list(dep_src) + [("vch", ct)])
        g1 = p.op("gpsimd", lambda e: e.tensor_scalar(
            out=ax_ap, in0=ax_ap, scalar1=-1.0, scalar2=None,
            op0=AL.add, op1=AL.bypass))
        gss = p.op("gpsimd", lambda e: e.tensor_tensor(
            out=ax_ap.rearrange("p (b w) -> p b w", w=16),
            in0=ax_ap.rearrange("p (b w) -> p b w", w=16),
            in1=scb, op=AL.mult))
    c = p.op("vector", lambda e: e.tensor_scalar(
        out=qu1[:, 0:F], in0=qt[:, 0:F], scalar1=10.0, scalar2=2.0,
        op0=AL.is_gt, op1=AL.mult), deps=[("vch", ct)])
    for thr in (5.0, 7.0):
        c = p.op("vector", lambda e, t=thr: e.scalar_tensor_tensor(
            out=qu1[:, 0:F], in0=qt[:, 0:F], scalar=t, in1=qu1[:, 0:F],
            op0=AL.is_gt, op1=AL.add), deps=[("vch", c)])
    c = p.op("vector", lambda e: e.tensor_scalar(
        out=qu1[:, 0:F], in0=qu1[:, 0:F], scalar1=2.0, scalar2=None,
        op0=AL.mult, op1=AL.bypass), deps=[("vch", c)])
    for thr in (0.5, 1.5, 2.5, 3.5):
        c = p.op("vector", lambda e, t=thr: e.scalar_tensor_tensor(
            out=qu1[:, 0:F], in0=qt[:, 0:F], scalar=t, in1=qu1[:, 0:F],
            op0=AL.is_gt, op1=AL.add), deps=[("vch", c)])
    if gps:
        c = p.op("vector", lambda e: e.tensor_tensor(
            out=out_ap, in0=qu1[:, 0:F], in1=ax_ap, op=AL.mult),
            deps=[("vch", c), ("gch", gss)])
        axg[gkey] = c
    else:
        c2 = p.op("vector", lambda e: e.tensor_scalar(
            out=qt[:, 0:F], in0=xn_ap, scalar1=0.0, scalar2=2.0,
            op0=AL.is_gt, op1=AL.mult), deps=[("vch", c)])
        c2 = p.op("vector", lambda e: e.scalar_tensor_tensor(
            out=qt[:, 0:F].rearrange("p (b w) -> p b w", w=16),
            in0=qt[:, 0:F].rearrange("p (b w) -> p b w", w=16),
            scalar=-1.0, in1=scb, op0=AL.add, op1=AL.mult), deps=[("vch", c2)])
        c = p.op("vector", lambda e: e.tensor_tensor(
            out=out_ap, in0=qu1[:, 0:F], in1=qt[:, 0:F], op=AL.mult),
            deps=[("vch", c2)])
        axg[gkey] = ct
    return c


def build_nc():
    nc = bass.Bass("TRN2", target_bir_lowering=False)
    xb_h = nc.dram_tensor("xb", [S, D], F32, kind="ExternalInput").ap()
    wqT_h = nc.dram_tensor("wqT", [D, D], F32, kind="ExternalInput").ap()
    wkT_h = nc.dram_tensor("wkT", [D, D], F32, kind="ExternalInput").ap()
    wvT_h = nc.dram_tensor("wvT", [D, D], F32, kind="ExternalInput").ap()
    pwc_h = nc.dram_tensor("pwc", [17, 128, 512], F32, kind="ExternalInput").ap()
    idf_h = nc.dram_tensor("idf", [128, 128], F32, kind="ExternalInput").ap()
    idb_h = nc.dram_tensor("idb", [128, 128], BF16, kind="ExternalInput").ap()
    y_h = nc.dram_tensor("y", [TOK, D], F32, kind="ExternalOutput").ap()
    sxnT_h = nc.dram_tensor("s_xnT", [128, 8 * S], BF16).ap()
    sxqT_h = nc.dram_tensor("s_xqT", [128, 16 * 1024], BF16).ap()
    satt_h = nc.dram_tensor("s_att", [TOK, D], F32).ap()
    ccin_h = nc.dram_tensor("cc_in", [128, 17 * 512], BF16).ap()
    ccout_h = nc.dram_tensor("cc_out", [8, 128, 17 * 512], BF16).ap()

    es_top = contextlib.ExitStack()
    p = Plan(nc)
    p._es = es_top
    sb = lambda name, shape, dt: es_top.enter_context(nc.sbuf_tensor(name, shape, dt))

    idf = sb("idf_s", [128, 128], F32)
    idb = sb("idb_s", [128, 128], BF16)
    nb40 = sb("nb40", [128, 1], F32)

    p.mksem("tch", "vch", "sch", "gch", "dID",
            "dWa", "dWb", "dWSa", "dWSb",
            "dW3g0", "dW3g1", "dW3g2", "dW3g3", "dW3g4", "dW3g5", "dW3d",
            "dPWa", "dPWb", "dPWc", "dPWSa", "dPWSb",
            "dXa", "dXb", "dXNa", "dXNb", "dXQSa", "dXQSb",
            "dSIa", "dSIb", "dAVa", "dAVb", "dATa", "dATb", "dYa", "dYb")

    def phase_fence(engines=("tensor", "vector", "scalar", "sync")):
        snap = {"tch": p.cnt["tch"], "vch": p.cnt["vch"], "sch": p.cnt["sch"]}
        dsems = ("dXNa", "dXNb", "dXQSa", "dXQSb", "dAVa", "dAVb",
                 "dPWSa", "dPWSb")
        dsnap = {sn: p.cnt[sn] for sn in dsems}
        for e in engines:
            for o, sn in (("tensor", "tch"), ("vector", "vch"), ("scalar", "sch")):
                if o != e:
                    p.wait(e, sn, snap[sn])
            for sn in dsems:
                p.wait(e, sn, dsnap[sn])

    with es_top:
        p.dma("sync", "dID", idf[:], idf_h[:, :])
        p.dma("sync", "dID", idb[:], idb_h[:, :])
        p.op("gpsimd", lambda e: e.memset(nb40[:], EXPB))

        es_pw = contextlib.ExitStack()

        def pwsb(n, sh, dt):
            return es_pw.enter_context(nc.sbuf_tensor(n, sh, dt, side="right"))
        pwb = {
            "win": pwsb("pw_in", [128, 2 * 512], F32),
            "ax": pwsb("pw_ax", [128, 512], F32),
            "qt": pwsb("pw_t", [128, 512], F32),
            "qu1": pwsb("pw_u1", [128, 512], F32),
            
            "qbm": pwsb("pw_bm", [128, 32], F32),
            "qsc": pwsb("pw_sc", [128, 32], F32),
            "qr": pwsb("pw_r", [128, 32], F32),
            "wq": pwsb("pw_q", [128, 2 * 512], BF16),
            "wqT": pwsb("pw_qT", [128, 2 * 512], BF16),
        }
        pwb["qss"] = pwb["ax"]
        ps_pw = es_pw.enter_context(nc.psum_tensor("ps_pw", [128, 2048], BF16))

        pw_global = []
        for kind in ("g", "u"):
            for r in range(22):
                for cch in range(2):
                    pw_global.append((kind, r, cch * 512, 512))
        for r in range(8):
            for cch in range(6):
                c0 = cch * 512
                pw_global.append(("d", r, c0, min(512, I - c0)))
        assert len(pw_global) == 136
        pw_sched = list(range(17))
        pw_state = {"a": 0, "b": 0, "c": 0,
                    "slotdep": [0, 0],
                    "chain": {}, "trans": {}, "loadsem": {},
                    "st_dep": [0, 0], "bank_evac": [0, 0]}

        def pw_stage_a():
            i = pw_state["a"]
            if i >= len(pw_sched):
                return False
            pw_state["a"] = i + 1
            width = 512
            slot = i % 2
            win = pwb["win"][:, slot * 512: slot * 512 + width]
            dep = [("vch", pw_state["slotdep"][slot])]
            dsem = ("dPWa", "dPWb")[slot]
            dcount = p.dma("sync", dsem, win, pwc_h[i], deps=dep)
            nblk = width // 16
            dsrc = [(dsem, dcount)]
            if i >= 2:
                assert i - 2 in pw_state["trans"], (i, pw_state)
                dsrc.append(("tch", pw_state["trans"][i - 2]))
            vc = fp4_quant(p, win, pwb["ax"][:, 0:width],
                           pwb["wq"][:, (i % 2) * 512:(i % 2) * 512 + width],
                           pwb, nblk, dep_src=dsrc)
            pw_state["slotdep"][slot] = vc
            pw_state["chain"][i] = vc
            return True

        def pw_stage_b():
            i = pw_state["b"]
            if i >= pw_state["a"] or i >= len(pw_sched):
                return False
            pw_state["b"] = i + 1
            width = 512
            nsub = 4
            half = i % 2
            bank = (i // 2) % 2
            base = bank * 1024 + (i % 2) * 512
            tc = None
            tdeps = [("vch", pw_state["chain"][i]),
                     ("sch", pw_state["bank_evac"][bank]), ("dID", 32)]
            for k in range(nsub):
                tc = p.op("tensor", lambda e, k=k, base=base, half=half: e.transpose(
                    out=ps_pw[:, base + k * 128: base + (k + 1) * 128],
                    in_=pwb["wq"][:, half * 512 + k * 128: half * 512 + k * 128 + 128],
                    identity=idb[:]), deps=tdeps)
                tdeps = []
            pw_state["trans"][i] = tc
            return True

        def pw_stage_c():
            i = pw_state["c"]
            if i >= pw_state["b"] or i >= len(pw_sched):
                return False
            pw_state["c"] = i + 1
            width = 512
            half = i % 2
            bank = (i // 2) % 2
            base = bank * 1024 + (i % 2) * 512
            pwssem = "dPWSa" if half == 0 else "dPWSb"
            ec = p.op("scalar", lambda e, w=width, base=base, half=half: e.copy(
                out=pwb["wqT"][:, half * 512: half * 512 + w],
                in_=ps_pw[:, base: base + w]),
                deps=[("tch", pw_state["trans"][i]),
                      (pwssem, pw_state["st_dep"][half])])
            pw_state["bank_evac"][bank] = ec
            pw_state["st_dep"][half] = p.cnt[pwssem] + 16
            p.op("scalar", lambda e, i=i, half=half: e.dma_start(
                out=ccin_h[:, i * 512:(i + 1) * 512],
                in_=pwb["wqT"][:, half * 512: half * 512 + 512]),
                deps=[("sch", ec)], inc=pwssem, inc_by=16)
            return True

        PW_LAG = 2

        def pw_step(n=1):
            for _ in range(n):
                if pw_state["b"] - pw_state["c"] >= PW_LAG or \
                        pw_state["b"] >= len(pw_sched):
                    pw_stage_c()
                if pw_state["a"] - pw_state["b"] >= PW_LAG or \
                        pw_state["a"] >= len(pw_sched):
                    pw_stage_b()
                pw_stage_a()

        def pw_drain():
            while pw_state["c"] < len(pw_sched):
                pw_stage_c()
                pw_stage_b()
                pw_stage_a()

        es_attn1 = contextlib.ExitStack()
        K_fm = es_attn1.enter_context(nc.sbuf_tensor("K_fm", [128, 8 * S], BF16))
        Q_fm = es_attn1.enter_context(nc.sbuf_tensor("Q_fm", [128, 8 * TOK], BF16))

        es_p0 = contextlib.ExitStack()
        p0 = lambda n, sh, dt: es_p0.enter_context(nc.sbuf_tensor(n, sh, dt))
        wkbf = p0("wk_bf", [128, 8 * 1024], BF16)
        wqbf = p0("wq_bf", [128, 8 * 1024], BF16)
        xin = p0("xin", [128, 2 * 2048], F32)
        xn1 = p0("xn1", [128, 2 * 2048], F32)
        xnT2 = p0("xnT2", [128, 2 * 2048], BF16)
        stt = p0("stt", [128, 2 * 24], F32)
        mvt = p0("mvt", [128, 2 * 4], F32)
        lns = p0("lns", [128, 2 * 12], F32)
        q_ax = p0("q_ax", [128, 2 * 1024], F32)
        q_t = p0("q_t", [128, 1024], F32)
        q_u1 = p0("q_u1", [128, 1024], F32)
        q_bm = p0("q_bm", [128, 64], F32)
        q_sc = p0("q_sc", [128, 64], F32)
        q_r = p0("q_r", [128, 64], F32)
        xq1 = p0("xq1", [128, 2048], BF16)
        qtmp = {"qt": q_t, "qu1": q_u1,
                "qbm": q_bm, "qsc": q_sc, "qr": q_r}
        ps_tp = [es_p0.enter_context(nc.psum_tensor(f"ps_tp{i}", [128, 512], F32))
                 for i in range(2)]
        ps_k = [es_p0.enter_context(nc.psum_tensor(f"ps_k{i}", [128, 512], F32))
                for i in range(2)]
        ps_xq = es_p0.enter_context(nc.psum_tensor("ps_xq", [128, 1024], BF16))

        wstg = xn1[:, 2048:3072]
        castdep = [0] * 16
        for i6 in range(16):
            w_src, w_dst = (wkT_h, wkbf) if i6 < 8 else (wqT_h, wqbf)
            ec = i6 % 8
            dep = [("vch", castdep[i6 - 1] if i6 >= 1 else 0)]
            dc = p.dma("sync", "dWa", wstg,
                       w_src[ec * 128:(ec + 1) * 128, :], deps=dep)
            castdep[i6] = p.op("vector", lambda e, ec=ec, w_dst=w_dst:
                               e.tensor_copy(
                out=w_dst[:, ec * 1024:(ec + 1) * 1024],
                in_=wstg), deps=[("dWa", dc)])
        wk_ready = castdep[7]
        wqproj_ready = castdep[15]

        xb_r = xb_h.rearrange("(H j p) e -> H p j e", j=2, p=128)
        xin_cons = [0] * NHS
        xin_cons_s = [0] * NHS
        xnT_cons_t = [0] * NHS
        xnT_cons_d = [0] * NHS
        kevac = {}
        xnT_spill_val = [None] * NHS
        norm_by_hs = [0] * NHS
        xn1_cons_t = [0] * NHS
        xn1_cons_v = [0] * NHS
        xn1_cons_s2 = [0] * NHS
        tp_evac = [("sch", 0), ("sch", 0)]
        k_evac_bank = [("sch", 0), ("sch", 0)]
        xq_evac = 0
        xqst_dep = [0, 0]

        def ln_stage(hs):
            slot = hs % 2
            xsem = "dXa" if slot == 0 else "dXb"
            xsem_n = "dXNa" if slot == 0 else "dXNb"
            st0[hs] = {"slot": slot, "xsem": xsem, "xsem_n": xsem_n}
            deps = []
            if hs >= 2:
                deps = [("vch", xin_cons[hs - 2]), ("sch", xin_cons_s[hs - 2])]
            dxc = p.dma("sync", xsem, xin[:, slot * 2048:(slot + 1) * 2048]
                        .rearrange("p (j e) -> p j e", j=2), xb_r[hs], deps=deps)
            L = slot * 12
            ST = slot * 24
            MT = slot * 4
            lndeps = [(xsem, dxc)]
            if hs >= 2:
                lndeps.append(("sch", norm_by_hs[hs - 2]))
            vv = 0
            for j in range(2):
                for half in range(2):
                    vv = p.op("vector", lambda e, j=j, h2=half, slot=slot, ST=ST:
                              e.bn_stats(
                        out=stt[:, ST + j * 12 + h2 * 6: ST + j * 12 + h2 * 6 + 6],
                        in_=xin[:, slot * 2048 + j * 1024 + h2 * 512:
                                slot * 2048 + j * 1024 + h2 * 512 + 512]),
                        deps=lndeps)
                    lndeps = []
            st0[hs]["xin_v"] = vv
            for j in range(2):
                vv = p.op("vector", lambda e, j=j, ST=ST, MT=MT: e.bn_aggr(
                    out=mvt[:, MT + j * 2: MT + j * 2 + 2],
                    in_=stt[:, ST + j * 12: ST + j * 12 + 12]),
                    deps=[("vch", vv)])
            mvv = mvt[:, MT: MT + 4].rearrange("p (t two) -> p t two", two=2)
            var2 = mvv[:, :, 1:2].rearrange("p t one -> p (t one)")
            mean2 = mvv[:, :, 0:1].rearrange("p t one -> p (t one)")
            vv = p.op("vector", lambda e, L=L, var2=var2: e.tensor_scalar(
                out=lns[:, L + 0: L + 2], in0=var2, scalar1=1e-5, scalar2=None,
                op0=AL.add, op1=AL.bypass), deps=[("vch", vv)])
            sv = p.op("scalar", lambda e, L=L: e.activation(
                out=lns[:, L + 2: L + 4], in_=lns[:, L + 0: L + 2], func=AF.Sqrt),
                deps=[("vch", vv)])
            vv = p.op("vector", lambda e, L=L: e.reciprocal(
                out=lns[:, L + 4: L + 6], in_=lns[:, L + 2: L + 4]),
                deps=[("sch", sv)])
            vv = p.op("vector", lambda e, L=L: e.scalar_tensor_tensor(
                out=lns[:, L + 6: L + 8], in0=lns[:, L + 0: L + 2], scalar=1.0,
                in1=lns[:, L + 4: L + 6],
                op0=AL.mult, op1=AL.mult), deps=[("vch", vv)])
            vv = p.op("vector", lambda e, L=L: e.tensor_tensor(
                out=lns[:, L + 6: L + 8], in0=lns[:, L + 6: L + 8],
                in1=lns[:, L + 2: L + 4], op=AL.add),
                deps=[("vch", vv)])
            vv = p.op("vector", lambda e, L=L: e.tensor_scalar(
                out=lns[:, L + 6: L + 8], in0=lns[:, L + 6: L + 8], scalar1=0.5,
                scalar2=None, op0=AL.mult, op1=AL.bypass), deps=[("vch", vv)])
            vv = p.op("vector", lambda e, L=L: e.reciprocal(
                out=lns[:, L + 8: L + 10], in_=lns[:, L + 6: L + 8]),
                deps=[("vch", vv)])
            vv = p.op("vector", lambda e, L=L, mean2=mean2: e.scalar_tensor_tensor(
                out=lns[:, L + 10: L + 12], in0=mean2, scalar=-1.0,
                in1=lns[:, L + 8: L + 10],
                op0=AL.mult, op1=AL.mult), deps=[("vch", vv)])
            xndeps = [("vch", vv), (xsem, dxc)]
            if hs == 1:
                xndeps.append(("vch", wqproj_ready))
            if hs >= 2:
                xndeps += [("tch", xn1_cons_t[hs - 2]), ("vch", xn1_cons_v[hs - 2]),
                           ("sch", xn1_cons_s2[hs - 2])]
            sn = 0
            for j in range(2):
                sn = p.op("scalar", lambda e, j=j, slot=slot, L=L: e.activation(
                    out=xn1[:, slot * 2048 + j * 1024: slot * 2048 + (j + 1) * 1024],
                    in_=xin[:, slot * 2048 + j * 1024: slot * 2048 + (j + 1) * 1024],
                    func=AF.Identity, scale=lns[:, L + 8 + j: L + 9 + j],
                    bias=lns[:, L + 10 + j: L + 11 + j]), deps=xndeps)
            xin_cons_s[hs] = sn
            norm_done = sn
            norm_by_hs[hs] = sn
            st0[hs]["dxc"] = dxc
            st0[hs]["norm"] = sn

        def work_stage(hs):
            nonlocal xq_evac
            slot = st0[hs]["slot"]
            xsem = st0[hs]["xsem"]
            xsem_n = st0[hs]["xsem_n"]
            norm_pre = st0[hs]["norm"]
            sa_pre = [None, None]
            if hs < 8:
                axg0 = qtmp.setdefault("_axg", {})
                for j in range(2):
                    sa_pre[j] = p.op(
                        "scalar", lambda e, j=j, slot=slot: e.activation(
                            out=q_ax[:, j * 1024:(j + 1) * 1024],
                            in_=xn1[:, slot * 2048 + j * 1024:
                                    slot * 2048 + (j + 1) * 1024],
                            func=AF.Abs),
                        deps=[("sch", norm_pre), ("vch", axg0.get(j, 0))])
            dxc = st0[hs]["dxc"]
            norm_done = st0[hs]["norm"]

            tt = 0
            for ecpair in range(4):
                bank = ecpair % 2
                bdeps = [("sch", norm_done), tp_evac[bank],
                         ("dID", 32)]
                if hs >= 2:
                    bdeps.append((xsem_n, xnT_cons_d[hs - 2]))
                    bdeps.append(("tch", xnT_cons_t[hs - 2]))
                for q4 in range(4):
                    ec = ecpair * 2 + q4 // 2
                    j = q4 % 2
                    tt = p.op("tensor", lambda e, b=bank, q4=q4, ec=ec, j=j,
                              slot=slot: e.transpose(
                        out=ps_tp[b][:, q4 * 128:(q4 + 1) * 128],
                        in_=xn1[:, slot * 2048 + j * 1024 + ec * 128:
                                 slot * 2048 + j * 1024 + ec * 128 + 128],
                        identity=idf[:]), deps=bdeps)
                    bdeps = []
                ec0 = ecpair * 2
                tp_evac[bank] = ("sch", p.op(
                    "scalar", lambda e, b=bank, ec0=ec0, slot=slot: e.copy(
                        out=xnT2[:, slot * 2048 + ec0 * 256:
                                 slot * 2048 + ec0 * 256 + 512],
                        in_=ps_tp[b][:]), deps=[("tch", tt)]))
            xnT_ready = tp_evac[0], tp_evac[1]
            xn1_cons_t[hs] = tt
            sxnT_r = sxnT_h.rearrange("p (g f) -> p g f", f=S)
            xnT_cons_d[hs] = p.op(
                "scalar", lambda e, slot=slot, hs=hs: e.dma_start(
                    out=sxnT_r[:, :, hs * 256:(hs + 1) * 256],
                    in_=xnT2[:, slot * 2048:(slot + 1) * 2048]
                    .rearrange("p (g f) -> p g f", f=256)),
                deps=list(xnT_ready), inc=xsem_n, inc_by=16)
            xnT_spill_val[hs] = (xsem_n, xnT_cons_d[hs])
            K_fr = K_fm[:].rearrange("p (d s) -> p d s", s=S)
            for dc in range(8):
                bank = (dc // 2) % 2
                col = (dc % 2) * 256
                mdeps = [("vch", wk_ready), xnT_ready[0], xnT_ready[1],
                         k_evac_bank[bank]]
                for ec in range(8):
                    tt = p.op("tensor", lambda e, b=bank, c=col, dc=dc, ec=ec,
                              slot=slot: e.matmul(
                        out=ps_k[b][:, c:c + 256],
                        lhsT=wkbf[:, ec * 1024 + dc * 128: ec * 1024 + dc * 128 + 128],
                        rhs=xnT2[:, slot * 2048 + ec * 256: slot * 2048 + ec * 256 + 256],
                        start=(ec == 0), stop=(ec == 7)), deps=mdeps)
                    mdeps = []
                if dc % 2 == 1:
                    k_evac_bank[bank] = ("sch", p.op(
                        "scalar", lambda e, b=bank, dc=dc, hs=hs: e.copy(
                            out=K_fr[:, dc - 1:dc + 1, hs * 256:(hs + 1) * 256],
                            in_=ps_k[b][:].rearrange("p (d c) -> p d c", c=256)),
                        deps=[("tch", tt)]))
            if hs < 8:
                Q_fr = Q_fm[:].rearrange("p (d t) -> p d t", t=TOK)
                for dc in range(8):
                    bank = (dc // 2) % 2
                    col = (dc % 2) * 256
                    mdeps = [("vch", wqproj_ready), k_evac_bank[bank]]
                    for ec in range(8):
                        tt = p.op("tensor", lambda e, b=bank, c=col, dc=dc, ec=ec,
                                  slot=slot: e.matmul(
                            out=ps_k[b][:, c:c + 256],
                            lhsT=wqbf[:, ec * 1024 + dc * 128: ec * 1024 + dc * 128 + 128],
                            rhs=xnT2[:, slot * 2048 + ec * 256: slot * 2048 + ec * 256 + 256],
                            start=(ec == 0), stop=(ec == 7)), deps=mdeps)
                        mdeps = []
                    if dc % 2 == 1:
                        k_evac_bank[bank] = ("sch", p.op(
                            "scalar", lambda e, b=bank, dc=dc, hs=hs: e.copy(
                                out=Q_fr[:, dc - 1:dc + 1, hs * 256:(hs + 1) * 256],
                                in_=ps_k[b][:].rearrange("p (d c) -> p d c", c=256)),
                            deps=[("tch", tt)]))
            xnT_cons_t[hs] = tt
            if hs < 8:
                vls = [0, 0]
                for j in range(2):
                    xqssem0 = "dXQSa" if j == 0 else "dXQSb"
                    o0 = j * 1024
                    xn_ap = xn1[:, slot * 2048 + o0: slot * 2048 + o0 + 1024]
                    out_ap = xq1[:, o0: o0 + 1024]
                    vls[j] = fp4_quant(p, xn_ap,
                                       q_ax[:, j * 1024:(j + 1) * 1024],
                                       out_ap, qtmp, 64,
                                       dep_src=[("sch", norm_done),
                                                (xqssem0, xqst_dep[j])],
                                       gkey=j, sa=sa_pre[j])
                vlast = vls[1]
                for j in range(2):
                    tq = 0
                    tdeps = [("vch", vls[j]), ("vch", xq_evac), ("dID", 32)]
                    for ec in range(8):
                        tq = p.op("tensor", lambda e, ec=ec, j=j: e.transpose(
                            out=ps_xq[:, ec * 128:(ec + 1) * 128],
                            in_=xq1[:, j * 1024 + ec * 128: j * 1024 + ec * 128 + 128],
                            identity=idb[:]), deps=tdeps)
                        tdeps = []
                    xqssem = "dXQSa" if j == 0 else "dXQSb"
                    xq_evac = p.op("vector", lambda e, j=j: e.tensor_copy(
                        out=xq1[:, j * 1024:(j + 1) * 1024], in_=ps_xq[:]),
                        deps=[("tch", tq), (xqssem, xqst_dep[j])])
                    tt2 = 2 * hs + j
                    xqst_dep[j] = p.cnt[xqssem] + 16
                    p.op("gpsimd", lambda e, tt2=tt2, j=j: e.dma_start(
                        out=sxqT_h[:, tt2 * 1024:(tt2 + 1) * 1024],
                        in_=xq1[:, j * 1024:(j + 1) * 1024]),
                        deps=[("vch", xq_evac)], inc=xqssem, inc_by=16)
                xn1_cons_v[hs] = vlast
                xn1_cons_s2[hs] = p.cnt["sch"]
                xin_cons[hs] = st0[hs]["xin_v"]
            else:
                xn1_cons_v[hs] = p.cnt["vch"]
                xn1_cons_s2[hs] = p.cnt["sch"]
                xin_cons[hs] = st0[hs]["xin_v"]

        st0 = {}
        ln_stage(0)
        for hs in range(NHS):
            if hs + 1 < NHS:
                ln_stage(hs + 1)
            work_stage(hs)

        es_p0.close()

        phase_fence()
        es_attn2 = contextlib.ExitStack()
        V_tm = es_attn2.enter_context(nc.sbuf_tensor("V_tm", [128, 32 * 1032], BF16))
        es_pvq = contextlib.ExitStack()
        wproj = es_pvq.enter_context(nc.sbuf_tensor("wproj", [128, 8 * 1024], BF16))
        wstg2 = es_pvq.enter_context(nc.sbuf_tensor("wstg2", [128, 2 * 1024], F32))
        es_pv = contextlib.ExitStack()
        xstr = es_pv.enter_context(nc.sbuf_tensor("xstr", [128, 2 * 2048], BF16))
        ps_v = [es_pv.enter_context(nc.psum_tensor(f"ps_v{i}", [128, 512], F32))
                for i in range(2)]
        ones_c = p.op("gpsimd", lambda e: e.memset(
            V_tm[:].rearrange("p (a b) -> p a b", b=1032)[:, :, 1024:1032], 1.0),
            deps=[("tch", p.cnt["tch"]), ("vch", p.cnt["vch"]),
                  ("sch", p.cnt["sch"]), ("dXNa", p.cnt["dXNa"]),
                  ("dXNb", p.cnt["dXNb"]), ("dXQSa", p.cnt["dXQSa"]),
                  ("dXQSb", p.cnt["dXQSb"])])
        castdep = [0] * 8
        for ec in range(8):
            sl = ec % 2
            wsem = "dWSa" if sl == 0 else "dWSb"
            dep = [("sch", castdep[ec - 2] if ec >= 2 else 0)]
            dc = p.dma("sync", wsem, wstg2[:, sl * 1024:(sl + 1) * 1024],
                       wvT_h[ec * 128:(ec + 1) * 128, :], deps=dep)
            castdep[ec] = p.op("scalar", lambda e, ec=ec, sl=sl: e.activation(
                out=wproj[:, ec * 1024:(ec + 1) * 1024],
                in_=wstg2[:, sl * 1024:(sl + 1) * 1024], func=AF.Identity),
                deps=[(wsem, dc)])
        sxnT_r = sxnT_h.rearrange("p (g f) -> p g f", f=S)
        v_bank_evac = [0, 0]
        xstr_cons = [0] * NHS
        for hs in range(NHS):
            slot = hs % 2
            ssem = "dSIa" if slot == 0 else "dSIb"
            deps = [xnT_spill_val[hs]]
            if hs >= 2:
                deps.append(("tch", xstr_cons[hs - 2]))
            dcs = p.dma("sync", ssem, xstr[:, slot * 2048:(slot + 1) * 2048]
                        .rearrange("p (g f) -> p g f", f=256),
                        sxnT_r[:, :, hs * 256:(hs + 1) * 256], deps=deps)
            pw_step(1)
            for tc_ in range(2):
                for db in range(2):
                    bank = (tc_ * 2 + db) % 2
                    mdeps = [(ssem, dcs),
                             ("sch", v_bank_evac[bank])]
                    for ec in range(8):
                        mdeps.append(("sch", castdep[ec]))
                        tt = p.op("tensor", lambda e, b=bank, tc_=tc_, db=db, ec=ec,
                                  slot=slot: e.matmul(
                            out=ps_v[b][:],
                            lhsT=xstr[:, slot * 2048 + ec * 256 + tc_ * 128:
                                      slot * 2048 + ec * 256 + tc_ * 128 + 128],
                            rhs=wproj[:, ec * 1024 + db * 512: ec * 1024 + db * 512 + 512],
                            start=(ec == 0), stop=(ec == 7)), deps=mdeps)
                        mdeps = []
                    tglob = 2 * hs + tc_
                    v_bank_evac[bank] = p.op(
                        "scalar", lambda e, b=bank, tglob=tglob, db=db: e.copy(
                            out=V_tm[:, tglob * 1032 + db * 512:
                                     tglob * 1032 + db * 512 + 512],
                            in_=ps_v[b][:]), deps=[("tch", tt)])
            xstr_cons[hs] = tt
        pw_drain()
        cc_done = p.op("gpsimd", lambda e: e.collective_compute(
            "AllGather", AL.bypass,
            replica_groups=[[0, 1, 2, 3, 4, 5, 6, 7]],
            ins=[ccin_h[:, :]], outs=[ccout_h[:, :, :]]),
            deps=[("dPWSa", p.cnt["dPWSa"]), ("dPWSb", p.cnt["dPWSb"])])
        es_pv.close()

        es_pvq.close()

        phase_fence()
        es_p2 = contextlib.ExitStack()
        pT = es_p2.enter_context(nc.sbuf_tensor("pT", [128, 32 * 256], BF16))
        attev = es_p2.enter_context(nc.sbuf_tensor("attev", [128, 2 * 1024], F32))
        arec = es_p2.enter_context(nc.sbuf_tensor("arec", [128, 2], F32))
        ps_l = [es_p2.enter_context(nc.psum_tensor(f"ps_l{i}", [128, 512], F32))
                for i in range(2)]
        ps_a = es_p2.enter_context(nc.psum_tensor("ps_a", [128, 1024], F32))
        ps_d = es_p2.enter_context(nc.psum_tensor("ps_d", [128, 512], F32))
        exp_bank = [0, 0]
        att_store_val = [None] * 16
        pv_done_prev = 0
        att_evac_v = 0
        att_evac_s = 0
        attev_dma = [0, 0]
        kq_ready_t = p.cnt["tch"]
        kq_ready_s = p.cnt["sch"]
        for qb in range(8):
            for kc in range(32):
                bank = kc % 2
                mdeps = [("sch", exp_bank[bank]), ("sch", kq_ready_s),
                         ("gch", 2)]
                if kc == 0:
                    mdeps.append(("tch", pv_done_prev))
                for dc in range(8):
                    tt = p.op("tensor", lambda e, b=bank, kc=kc, dc=dc, qb=qb: e.matmul(
                        out=ps_l[b][:, 0:256],
                        lhsT=K_fm[:, dc * S + kc * 128: dc * S + kc * 128 + 128],
                        rhs=Q_fm[:, dc * TOK + qb * 256: dc * TOK + qb * 256 + 256],
                        start=(dc == 0), stop=(dc == 7)), deps=mdeps)
                    mdeps = []
                exp_bank[bank] = p.op("scalar", lambda e, b=bank, kc=kc: e.activation(
                    out=pT[:, kc * 256:(kc + 1) * 256], in_=ps_l[b][:, 0:256],
                    func=AF.Exp, bias=nb40[:], scale=1.0), deps=[("tch", tt)])
            exps_done = max(exp_bank)
            for qc in range(2):
                mdeps = [("sch", exps_done), ("vch", att_evac_v),
                         ("sch", att_evac_s)]
                for kc in range(32):
                    tt = p.op("tensor", lambda e, kc=kc, qc=qc: e.matmul(
                        out=ps_a[:, 0:512],
                        lhsT=pT[:, kc * 256 + qc * 128: kc * 256 + qc * 128 + 128],
                        rhs=V_tm[:, kc * 1032: kc * 1032 + 512],
                        start=(kc == 0), stop=(kc == 31)), deps=mdeps)
                    mdeps = []
                    tt = p.op("tensor", lambda e, kc=kc, qc=qc: e.matmul(
                        out=ps_a[:, 512:1024],
                        lhsT=pT[:, kc * 256 + qc * 128: kc * 256 + qc * 128 + 128],
                        rhs=V_tm[:, kc * 1032 + 512: kc * 1032 + 1024],
                        start=(kc == 0), stop=(kc == 31)))
                    tt = p.op("tensor", lambda e, kc=kc, qc=qc: e.matmul(
                        out=ps_d[:, 0:8],
                        lhsT=pT[:, kc * 256 + qc * 128: kc * 256 + qc * 128 + 128],
                        rhs=V_tm[:, kc * 1032 + 1024: kc * 1032 + 1032],
                        start=(kc == 0), stop=(kc == 31)))
                slot = qc
                vv = p.op("vector", lambda e, slot=slot: e.tensor_scalar(
                    out=arec[:, slot: slot + 1], in0=ps_d[:, 0:1], scalar1=1e-30,
                    scalar2=None, op0=AL.max, op1=AL.bypass),
                    deps=[("tch", tt), ("sch", att_evac_s)])
                vv = p.op("vector", lambda e, slot=slot: e.reciprocal(
                    out=arec[:, slot: slot + 1], in_=arec[:, slot: slot + 1]),
                    deps=[("vch", vv)])
                vv = p.op("vector", lambda e, slot=slot: e.tensor_scalar(
                    out=arec[:, slot: slot + 1], in0=arec[:, slot: slot + 1],
                    scalar1=NF, scalar2=None, op0=AL.mult, op1=AL.bypass),
                    deps=[("vch", vv)])
                att_evac_v = vv
                avsem = "dAVa" if slot == 0 else "dAVb"
                sdeps = [("vch", vv), (avsem, attev_dma[slot])]
                ss = p.op("scalar", lambda e, slot=slot: e.activation(
                    out=attev[:, slot * 1024:(slot + 1) * 1024], in_=ps_a[:, 0:1024],
                    func=AF.Identity, scale=arec[:, slot: slot + 1]), deps=sdeps)
                att_evac_s = ss
                row = (qb * 2 + qc) * 128
                attev_dma[slot] = p.cnt[avsem] + 16
                p.op("scalar", lambda e, row=row, slot=slot: e.dma_start(
                    out=satt_h[row: row + 128, :],
                    in_=attev[:, slot * 1024:(slot + 1) * 1024]),
                    deps=[("sch", ss)], inc=avsem, inc_by=16)
                att_store_val[qb * 2 + qc] = (avsem, p.cnt[avsem])
            pv_done_prev = p.cnt["tch"]
        es_p2.close()
        es_attn2.close()
        es_attn1.close()
        es_pw.close()

        phase_fence(engines=("tensor", "vector", "scalar", "sync", "gpsimd"))
        es_p3 = contextlib.ExitStack()
        p3 = lambda n, sh, dt: es_p3.enter_context(nc.sbuf_tensor(n, sh, dt))
        wgq = p3("wgq", [128, 8 * I], BF16)
        wuq = p3("wuq", [128, 8 * I], BF16)
        wdq = p3("wdq", [128, 22 * D], BF16)
        xqT_t = p3("xqT_t", [128, 2 * 1024], BF16)
        h_f = p3("h_f", [128, I], F32)
        hq = p3("hq", [128, 2 * I], BF16)
        hqT = p3("hqT", [128, 22 * 128], BF16)
        sgb = p3("sgb", [128, 2 * 512], F32)
        xt = p3("xt", [128, 2 * 1024], F32)
        at = p3("at", [128, 2 * 1024], F32)
        f_ax = p3("f_ax", [128, 2 * 1408], F32)
        f_t = p3("f_t", [128, 1408], F32)
        f_u1 = p3("f_u1", [128, 1408], F32)
        f_bm = p3("f_bm", [128, 88], F32)
        f_sc = p3("f_sc", [128, 88], F32)
        f_r = p3("f_r", [128, 88], F32)
        ftmp = {"qt": f_t, "qu1": f_u1,
                "qbm": f_bm, "qsc": f_sc, "qr": f_r}
        ps_g = [es_p3.enter_context(nc.psum_tensor(f"ps_g{i}", [128, 512], F32))
                for i in range(2)]
        ps_u = [es_p3.enter_context(nc.psum_tensor(f"ps_u{i}", [128, 512], F32))
                for i in range(2)]
        ps_dn = es_p3.enter_context(nc.psum_tensor("ps_dn", [128, 1024], F32))
        ps_ht = [es_p3.enter_context(nc.psum_tensor(f"ps_ht{i}", [128, 1024], BF16))
                 for i in range(2)]

        def grp_of(g):
            kind, r, c0, width = pw_global[g]
            return "dW3g%d" % (r // 4) if kind in ("g", "u") else "dW3d"
        order = sorted(range(136), key=lambda g: (grp_of(g) == "dW3d", grp_of(g)))
        ldeps = [("gch", cc_done)]
        for g in order:
            kind, r, c0, width = pw_global[g]
            rank, k = divmod(g, 17)
            src = ccout_h[rank, :, k * 512: k * 512 + width] \
                .rearrange("p (j f) -> p j f", f=128)
            nsub = (width + 127) // 128
            if kind == "g" or kind == "u":
                wdst = wgq if kind == "g" else wuq
                dst = wdst[:].rearrange("p (q f) -> p q f", f=I)[
                    :, c0 // 128:(c0 + width) // 128, r * 128:(r + 1) * 128]
            else:
                dst = wdq[:].rearrange("p (q f) -> p q f", f=D)[
                    :, c0 // 128: c0 // 128 + nsub, r * 128:(r + 1) * 128]
            p.dma("gpsimd", grp_of(g), dst, src, deps=ldeps)
            ldeps = []
        w3g_total = {sn: p.cnt[sn] for sn in
                     ("dW3g0", "dW3g1", "dW3g2", "dW3g3", "dW3g4", "dW3g5",
                      "dW3d")}

        xq_cons = [0] * 16
        x_cons = [0] * 16
        at_cons = [0, 0]
        g_evac_bank = [0, 0]
        hmul_bank = [0, 0]
        ht_evac = [0, 0]
        dn_evac = 0
        ot_dma = [0, 0]
        hq_guard = [0, 0]
        stA = {}
        iblocks = [(0, 512), (512, 512), (1024, 512), (1536, 512), (2048, 512),
                   (2560, 256)]

        def p3_stage_a_load(tt2):
            slot = tt2 % 2
            ssem = "dSIa" if slot == 0 else "dSIb"
            xsem = "dXa" if slot == 0 else "dXb"
            deps = [("dXQSa", p.cnt["dXQSa"]), ("dXQSb", p.cnt["dXQSb"])] \
                if tt2 < 2 else [("tch", xq_cons[tt2 - 2])]
            dxq = p.dma("sync", ssem, xqT_t[:, slot * 1024:(slot + 1) * 1024],
                        sxqT_h[:, tt2 * 1024:(tt2 + 1) * 1024], deps=deps)
            ysem = "dYa" if slot == 0 else "dYb"
            deps = [] if tt2 < 2 else [(ysem, x_cons[tt2 - 2])]
            dxt = p.dma("sync", xsem, xt[:, slot * 1024:(slot + 1) * 1024],
                        xb_h[tt2 * 128:(tt2 + 1) * 128, :], deps=deps)
            atsem = "dATa" if slot == 0 else "dATb"
            dat = p.dma("sync", atsem, at[:, slot * 1024:(slot + 1) * 1024],
                        satt_h[tt2 * 128:(tt2 + 1) * 128, :],
                        deps=[("vch", at_cons[slot]), att_store_val[tt2]])
            stA[tt2] = dict(dxt=dxt, dat=dat, atsem=atsem, slot=slot,
                            dxq=dxq, ssem=ssem, hmul_ib=[])

        def p3_stage_a_mm(tt2, bsel):
            st = stA[tt2]
            slot = st["slot"]
            ssem, dxq = st["ssem"], st["dxq"]
            for bi, (i0, iw) in enumerate(iblocks):
                if bi not in bsel:
                    continue
                bank = bi % 2
                gsn = "dW3g%d" % (i0 // 512)
                mdeps = [(gsn, w3g_total[gsn]), (ssem, dxq),
                         ("sch", g_evac_bank[bank])]
                for ec in range(8):
                    tg = p.op("tensor", lambda e, b=bank, ec=ec, i0=i0, iw=iw,
                              slot=slot: e.matmul(
                        out=ps_g[b][:, 0:iw],
                        lhsT=xqT_t[:, slot * 1024 + ec * 128: slot * 1024 + ec * 128 + 128],
                        rhs=wgq[:, ec * I + i0: ec * I + i0 + iw],
                        start=(ec == 0), stop=(ec == 7)), deps=mdeps)
                    mdeps = []
                mdeps = [("vch", hmul_bank[bank])]
                for ec in range(8):
                    tu = p.op("tensor", lambda e, b=bank, ec=ec, i0=i0, iw=iw,
                              slot=slot: e.matmul(
                        out=ps_u[b][:, 0:iw],
                        lhsT=xqT_t[:, slot * 1024 + ec * 128: slot * 1024 + ec * 128 + 128],
                        rhs=wuq[:, ec * I + i0: ec * I + i0 + iw],
                        start=(ec == 0), stop=(ec == 7)), deps=mdeps)
                    mdeps = []
                g_evac_bank[bank] = p.op(
                    "scalar", lambda e, b=bank, iw=iw, sg0=bank * 512: e.activation(
                        out=sgb[:, sg0: sg0 + iw], in_=ps_g[b][:, 0:iw],
                        func=AF.Silu),
                    deps=[("tch", tg), ("vch", hmul_bank[bank])])
                hmul_bank[bank] = p.op(
                    "vector", lambda e, b=bank, i0=i0, iw=iw, sg0=bank * 512:
                    e.tensor_tensor(
                        out=h_f[:, i0: i0 + iw], in0=sgb[:, sg0: sg0 + iw],
                        in1=ps_u[b][:, 0:iw],
                        op=AL.mult), deps=[("sch", g_evac_bank[bank]), ("tch", tu)])
                st["hmul_ib"].append(hmul_bank[bank])

        def p3_stage_a_q(tt2, ch):
            st = stA[tt2]
            slot = st["slot"]
            hmul_dep = st["hmul_ib"][2] if ch == 0 else st["hmul_ib"][5]
            hap = h_f[:, ch * 1408: (ch + 1) * 1408]
            dsrc = [("vch", hmul_dep), ("tch", hq_guard[slot])]
            vq = fp4_quant(p, hap, f_ax[:, ch * 1408:(ch + 1) * 1408],
                           hq[:, slot * I + ch * 1408: slot * I + (ch + 1) * 1408],
                           ftmp, 88, dep_src=dsrc, gkey=ch)
            st["vq"] = vq

        def p3_stage_b_mm(tt2):
            st = stA[tt2]
            slot = st["slot"]
            tq = 0
            groups = [(0, 8), (8, 8), (16, 6)]
            for gi, (ic0, n) in enumerate(groups):
                bank = gi % 2
                tdeps = [("vch", st["vq"]), ("sch", ht_evac[bank])]
                for k in range(n):
                    ic = ic0 + k
                    tq = p.op("tensor", lambda e, b=bank, k=k, ic=ic, slot=slot:
                              e.transpose(
                        out=ps_ht[b][:, k * 128:(k + 1) * 128],
                        in_=hq[:, slot * I + ic * 128: slot * I + (ic + 1) * 128],
                        identity=idb[:]), deps=tdeps)
                    tdeps = []
                ht_evac[bank] = p.op("scalar", lambda e, b=bank, ic0=ic0, n=n: e.copy(
                    out=hqT[:, ic0 * 128: (ic0 + n) * 128],
                    in_=ps_ht[b][:, 0: n * 128]), deps=[("tch", tq)])
            hq_guard[slot] = tq
            hT_ready = max(ht_evac)
            mdeps = [("sch", hT_ready), ("dW3d", w3g_total["dW3d"]),
                     ("vch", dn_evac)]
            for ic in range(22):
                td = p.op("tensor", lambda e, ic=ic: e.matmul(
                    out=ps_dn[:, 0:512],
                    lhsT=hqT[:, ic * 128:(ic + 1) * 128],
                    rhs=wdq[:, ic * D: ic * D + 512],
                    start=(ic == 0), stop=(ic == 21)), deps=mdeps)
                mdeps = []
                td = p.op("tensor", lambda e, ic=ic: e.matmul(
                    out=ps_dn[:, 512:1024],
                    lhsT=hqT[:, ic * 128:(ic + 1) * 128],
                    rhs=wdq[:, ic * D + 512: ic * D + 1024],
                    start=(ic == 0), stop=(ic == 21)))
            st["td"] = td

        def p3_stage_b_epi(tt2):
            nonlocal dn_evac
            st = stA.pop(tt2)
            slot = st["slot"]
            td = st["td"]
            xsem = "dXa" if slot == 0 else "dXb"
            ysem = "dYa" if slot == 0 else "dYb"
            vv = 0
            gtt = 0
            for half in range(2):
                c0 = half * 512
                vv = p.op("vector", lambda e, c0=c0, slot=slot: e.scalar_tensor_tensor(
                    out=xt[:, slot * 1024 + c0: slot * 1024 + c0 + 512],
                    in0=ps_dn[:, c0: c0 + 512], scalar=1.0,
                    in1=xt[:, slot * 1024 + c0: slot * 1024 + c0 + 512],
                    op0=AL.mult, op1=AL.add),
                    deps=[("tch", td), (xsem, st["dxt"]), (ysem, ot_dma[slot])])
                vv = p.op("vector", lambda e, c0=c0, slot=slot: e.tensor_tensor(
                    out=xt[:, slot * 1024 + c0: slot * 1024 + c0 + 512],
                    in0=xt[:, slot * 1024 + c0: slot * 1024 + c0 + 512],
                    in1=at[:, slot * 1024 + c0: slot * 1024 + c0 + 512], op=AL.add),
                    deps=[("vch", vv), (st["atsem"], st["dat"])])
            dn_evac = vv
            at_cons[slot] = vv
            xq_cons[tt2] = td
            ot_dma[slot] = p.cnt[ysem] + 16
            x_cons[tt2] = p.cnt[ysem] + 16
            p.op("scalar", lambda e, tt2=tt2, slot=slot: e.dma_start(
                out=y_h[tt2 * 128:(tt2 + 1) * 128, :],
                in_=xt[:, slot * 1024:(slot + 1) * 1024]),
                deps=[("vch", vv)], inc=ysem, inc_by=16)

        for tt2 in range(16):
            p3_stage_a_load(tt2)
            p3_stage_a_mm(tt2, (0, 1, 2))
            p3_stage_a_q(tt2, 0)
            p3_stage_a_mm(tt2, (3, 4, 5))
            if tt2 >= 1:
                p3_stage_b_mm(tt2 - 1)
            p3_stage_a_q(tt2, 1)
            if tt2 >= 1:
                p3_stage_b_epi(tt2 - 1)
        p3_stage_b_mm(15)
        p3_stage_b_epi(15)
        es_p3.close()

        p.wait("sync", "dYa", p.cnt["dYa"])
        p.wait("sync", "dYb", p.cnt["dYb"])

        with nc.Block() as block:
            streams = {e: [fn for (eng, fn) in p.ops if eng == e]
                       for e in Plan.ENGS}

            @block.sync
            def _(e):
                for fn in streams["sync"]:
                    fn(e)

            @block.scalar
            def _(e):
                for fn in streams["scalar"]:
                    fn(e)

            @block.vector
            def _(e):
                for fn in streams["vector"]:
                    fn(e)

            @block.tensor
            def _(e):
                for fn in streams["tensor"]:
                    fn(e)

            @block.gpsimd
            def _(e):
                for fn in streams["gpsimd"]:
                    fn(e)

    return nc


_NC_CACHE = None


def kernel(L0_attn_x, L0_attn_Wq, L0_attn_Wk, L0_attn_Wv,
           L0_ffn_w_gate, L0_ffn_w_up, L0_ffn_w_down):
    global _NC_CACHE
    x = np.ascontiguousarray(np.asarray(L0_attn_x, dtype=np.float32))
    wqT = np.ascontiguousarray(np.asarray(L0_attn_Wq, np.float32).T)
    wkT = np.ascontiguousarray(np.asarray(L0_attn_Wk, np.float32).T)
    wvT = np.ascontiguousarray(np.asarray(L0_attn_Wv, np.float32).T)
    wg = np.ascontiguousarray(np.asarray(L0_ffn_w_gate, np.float32))
    wu = np.ascontiguousarray(np.asarray(L0_ffn_w_up, np.float32))
    wd = np.ascontiguousarray(np.asarray(L0_ffn_w_down, np.float32))
    pw_global = []
    for kind in ("g", "u"):
        for r in range(22):
            for cch in range(2):
                pw_global.append((kind, r, cch * 512, 512))
    for r in range(8):
        for cch in range(6):
            c0 = cch * 512
            pw_global.append(("d", r, c0, min(512, I - c0)))
    pwc_all = np.zeros((136, 128, 512), dtype=np.float32)
    for g, (kind, r, c0, width) in enumerate(pw_global):
        src = {"g": wg, "u": wu, "d": wd}[kind]
        pwc_all[g, :, :width] = src[r * 128:(r + 1) * 128, c0:c0 + width]
    idf = np.eye(128, dtype=np.float32)
    idb = np.eye(128, dtype=np.float32).astype(ml_dtypes.bfloat16)
    if _NC_CACHE is None:
        _NC_CACHE = build_nc()
    nc = _NC_CACHE
    in_maps = []
    for c in range(8):
        b, h = c // 2, c % 2
        own = x[b, h * TOK:(h + 1) * TOK]
        oth = x[b, (1 - h) * TOK:(2 - h) * TOK]
        xbc = np.ascontiguousarray(np.concatenate([own, oth], axis=0))
        in_maps.append({"xb": xbc, "wqT": wqT, "wkT": wkT, "wvT": wvT,
                        "pwc": np.ascontiguousarray(pwc_all[17 * c: 17 * (c + 1)]),
                        "idf": idf, "idb": idb})
    res = run_bass_kernel_spmd(nc, in_maps, core_ids=list(range(8)))
    y = np.empty((4, S, D), dtype=np.float32)
    for c in range(8):
        b, h = c // 2, c % 2
        y[b, h * TOK:(h + 1) * TOK] = res.results[c]["y"]
    return y
